# version 15
# baseline (speedup 1.0000x reference)
import numpy as np
import ml_dtypes
from contextlib import ExitStack

import concourse.bass as bass
import concourse.mybir as mybir
from concourse.bass_utils import run_bass_kernel_spmd

F32 = mybir.dt.float32
BF16 = mybir.dt.bfloat16
AF = mybir.ActivationFunctionType
ALU = mybir.AluOpType

T = 2048
D = 1024
NCH = 8
TG = 2050
NT = 16
NTB = 4
HID = 2730
A_OFFS = [128 * i for i in range(21)] + [2688]
A_WIDS = [128] * 21 + [42]
NPAIR = 22

ENGS = ("sync", "gpsimd", "scalar", "vector", "tensor")


class Sem:
    __slots__ = ("h", "n", "name")

    def __init__(self, h, name):
        self.h, self.n, self.name = h, 0, name


class Prog:
    def __init__(self):
        self.q = {e: [] for e in ENGS}

    def do(self, eng, emit, waits=(), inc=None, incn=1):
        ws = []
        for (s, n) in waits:
            if n <= 0:
                continue
            assert n <= s.n, f"wait {s.name}>={n} but only {s.n} emitted"
            ws.append((s, n))
        self.q[eng].append((tuple(ws), emit, inc, incn))
        if inc is not None:
            inc.n += incn
            return inc.n
        return None

    def realize(self, block):
        def mk(engname):
            def body(eng):
                for waits, emit, inc, incn in self.q[engname]:
                    for (s, n) in waits:
                        eng.wait_ge(s.h, n)
                    if emit is not None:
                        inst = emit(eng)
                        if inc is not None:
                            inst.then_inc(inc.h, incn)
            return body
        for e in ENGS:
            getattr(block, e)(mk(e))


def _host_rotary_tables():
    n, dh, win = T, 64, 512
    half = np.arange(0, dh, 2, dtype=np.float64)
    inv_freq = 1.0 / 10000.0 ** (half / dh)
    t = np.arange(n, dtype=np.float64)
    freqs = t[:, None] * inv_freq[None, :]
    freqs = np.concatenate([freqs, freqs], axis=-1)
    base = (half + 0.4 * dh) / (1.4 * dh)
    power = (t - n // 2) / float(win // 2)
    scale = base[None, :] ** power[:, None]
    scale = np.concatenate([scale, scale], axis=-1)
    cos, sin = np.cos(freqs), np.sin(freqs)
    sgn = np.concatenate([-np.ones(32), np.ones(32)])[None, :]
    return np.stack([cos * scale, sgn * sin * scale,
                     cos / scale, sgn * sin / scale]).astype(np.float32)


def _host_masks():
    m = np.zeros((8, 128, 512), dtype=np.float32)
    sl = np.arange(128)[:, None]
    tr = np.arange(512)[None, :]
    for r in range(4):
        m[r] = (tr <= 128 * r + sl).astype(np.float32)
    for r in range(4, 8):
        m[r] = (tr >= 128 * (r - 4) + sl).astype(np.float32)
    return m


def build(phases=(1, 2, 3, 4)):
    nc = bass.Bass("TRN2", target_bir_lowering=False)

    def din(name, shape):
        return nc.dram_tensor(name, list(shape), F32, kind="ExternalInput").ap()

    x_in = din("x", (T, D))
    aWqT = din("aWqT", (D, D))
    aWkT = din("aWkT", (D, 64))
    aWvT = din("aWvT", (D, 64))
    aWoT = din("aWoT", (D, D))
    abq_c = din("abq_c", (128, 8))
    abv_c = din("abv_c", (128, 1))
    abo_c = din("abo_c", (128, 8))
    bDQT = din("bDQT", (64, 512))
    bUQT = din("bUQT", (512, 64))
    bDKVT = din("bDKVT", (64, 1024))
    bUKVT = din("bUKVT", (1024, 128))
    bQRT = din("bQRT", (64, 64))
    bKRT = din("bKRT", (64, 64))
    bWOT = din("bWOT", (64, 64))
    bqnw_c = din("bqnw_c", (128, 1))
    bknw_c = din("bknw_c", (128, 1))
    fwqkv = din("fwqkv", (2, D, 1536))
    fqs = din("fqs", (2, 64))
    fks = din("fks", (2, 64))
    fwo = din("fwo", (2, 512, D))
    flng_c = din("flng_c", (2, 128, 8))
    flnb_c = din("flnb_c", (2, 128, 8))
    fw1 = din("fw1", (2, D, 5460))
    fb1_c = din("fb1_c", (2, 128, 44))
    fw2 = din("fw2", (2, HID, D))
    fb2_c = din("fb2_c", (2, 128, 8))
    ident_in = din("ident_in", (128, 128))
    rotc = din("rotc", (4, T, 64))
    masks_in = din("masks_in", (8, 128, 512))

    out = nc.dram_tensor("out", [T, D], F32, kind="ExternalOutput").ap()
    xpark = nc.dram_tensor("xpark", [NCH, 128, TG], F32, kind="Internal").ap()
    w1C = nc.dram_tensor("w1C", [2, NPAIR, 2, 128, NCH, 128], BF16,
                         kind="Internal").ap()
    wqkvC = nc.dram_tensor("wqkvC", [2, D, 1536], BF16, kind="Internal").ap()
    woC = nc.dram_tensor("woC", [2, 512, D], BF16, kind="Internal").ap()
    w2C = nc.dram_tensor("w2C", [2, HID, D], BF16, kind="Internal").ap()

    es = ExitStack()

    def sb(name, shape, dt=F32):
        return es.enter_context(nc.sbuf_tensor(name, list(shape), dt))

    X = sb("X", [128, NCH, TG], F32)
    B = sb("B", [128, NCH * TG], BF16)
    AR1 = sb("AR1", [128, 8192], BF16)
    AR2 = sb("AR2", [128, 4096], BF16)
    V65 = sb("V65", [128, 4160], BF16)
    WSLOT = sb("WSLOT", [128, NCH, 1536], BF16)
    WSLOT2 = sb("WSLOT2", [128, 4096], BF16)
    ROT = sb("ROT", [128, 4, NT, 64], BF16)
    MASKS = sb("MASKS", [128, 8, 512], BF16)
    KD = sb("KD", [128, 2, T], BF16)
    PT = sb("PT", [128, 4, 512], BF16)
    IDENT = sb("IDENT", [128, 128], F32)
    IDENTB = sb("IDENTB", [128, 128], BF16)
    BC1 = sb("BC1", [128, 512], F32)
    BC2 = sb("BC2", [128, 512], F32)
    TMP32 = sb("TMP32", [128, 512], F32)
    TMPA = sb("TMPA", [128, 520], BF16)
    TMPB = sb("TMPB", [128, 520], BF16)
    TMPC = sb("TMPC", [128, 520], BF16)
    TMPE = sb("TMPE", [128, 512], BF16)
    NRM3 = sb("NRM3", [128, 8], F32)
    NRM4 = sb("NRM4", [128, 8], F32)
    ROWS = sb("ROWS", [1, 1152], F32)
    STAT3 = sb("STAT3", [128, 3], BF16)
    HALF2 = sb("HALF2", [128, 2], BF16)
    ONESC = sb("ONESC", [1, 128], F32)
    ONESCB = sb("ONESCB", [1, 128], BF16)
    RECB = sb("RECB", [1, 1024], BF16)
    BIASC = sb("BIASC", [128, 20], F32)
    LGC = sb("LGC", [128, 8], F32)
    LBC = sb("LBC", [128, 8], F32)
    B2C = sb("B2C", [128, 8], F32)
    B1C = sb("B1C", [128, 44], F32)
    QSB = sb("QSB", [128, 2, 64], F32)
    NRM = sb("NRM", [128, 8], F32)
    NRM2 = sb("NRM2", [128, 8], F32)
    ZSAVE = sb("ZSAVE", [128, NCH, 3], F32)
    EPSC = sb("EPSC", [4, 2], F32)

    SP = es.enter_context(nc.psum_tensor("SP", [128, 2048], F32))
    PVa = es.enter_context(nc.psum_tensor("PVa", [128, 512], F32))
    PVb = es.enter_context(nc.psum_tensor("PVb", [128, 512], F32))
    PBF = es.enter_context(nc.psum_tensor("PBF", [128, 1024], BF16))
    PBX = es.enter_context(nc.psum_tensor("PBX", [128, 512], F32))

    class PView:
        __slots__ = ("th", "c0")

        def __init__(self, th, c0):
            self.th, self.c0 = th, c0

        def __getitem__(self, idx):
            ps, fs = idx
            a = fs.start or 0
            b = fs.stop if fs.stop is not None else 512
            return self.th[ps, self.c0 + a:self.c0 + b]

    PB = [PView(SP, 0), PView(SP, 512), PView(SP, 1024), PView(SP, 1536),
          PView(PVa, 0), PView(PVb, 0)]

    sems = {}
    for name in ("t", "v", "s", "g0", "g1", "d0", "d1", "d2", "d3", "sp",
                 "spb", "dx", "w0a", "w0b", "w1a", "w1b", "w2a", "w2b",
                 "gc", "dq1", "dk1"):
        sems[name] = Sem(es.enter_context(nc.semaphore(name)), name)
    S = sems
    p = Prog()

    def dma(queue, sem, outap, inap, waits=()):
        p.do(queue, lambda e, o=outap, i=inap: e.dma_start(out=o, in_=i),
             waits=waits, inc=sem, incn=16)
        return sem.n

    def vop(emit, waits=(), incn=1):
        return p.do("vector", emit, waits=waits, inc=S["v"], incn=incn)

    def sop(emit, waits=(), incn=1):
        return p.do("scalar", emit, waits=waits, inc=S["s"], incn=incn)

    def top(emit, waits=(), incn=1):
        return p.do("tensor", emit, waits=waits, inc=S["t"], incn=incn)

    def gop(emit, waits=(), incn=1):
        return p.do("gpsimd", emit, waits=waits, inc=S["gc"], incn=incn)

    def Xs(c, lo, n):
        return X[:, c, lo:lo + n]

    def Bs(c, lo, n, p0=0, pn=128):
        return B[p0:p0 + pn, c * TG + lo: c * TG + lo + n]

    bank_guard = [None] * 6
    pbf_guard = [None]
    pbx_guard = [None]

    def bw(i):
        g = bank_guard[i]
        return [g] if g else []

    def pbfw():
        g = pbf_guard[0]
        return [g] if g else []

    def pbxw():
        g = pbx_guard[0]
        return [g] if g else []

    def barrier():
        fence = [(sm, sm.n) for sm in S.values()]
        for e in ENGS:
            for (s, n) in fence:
                p.do(e, None, waits=[(s, n)])
        return fence

    dma("sync", S["d0"], IDENT[:, :], ident_in[:, :])
    dma("gpsimd", S["g0"], IDENTB[:, :], ident_in[:, :])
    for r in range(8):
        dma("gpsimd", S["g0"], MASKS[:, r, :], masks_in[r, :, :])
    g_const = S["g0"].n

    vop(lambda e: e.memset(ONESC[:, :], 1.0))
    vop(lambda e: e.memset(ONESCB[:, :], 1.0), waits=[(S["v"], S["v"].n)])
    vop(lambda e: e.memset(STAT3[:, 0:1], 1.0 / 1024.0),
        waits=[(S["v"], S["v"].n)])
    vop(lambda e: e.memset(STAT3[:, 1:2], 0.0),
        waits=[(S["v"], S["v"].n)])
    vop(lambda e: e.memset(STAT3[:, 2:3], 1.0 / 1024.0),
        waits=[(S["v"], S["v"].n)])
    vop(lambda e: e.memset(HALF2[:, :], 0.0),
        waits=[(S["v"], S["v"].n)])
    vop(lambda e: e.memset(HALF2[0:64, 0:1], 1.0),
        waits=[(S["v"], S["v"].n)])
    vop(lambda e: e.memset(HALF2[64:128, 1:2], 1.0),
        waits=[(S["v"], S["v"].n)])
    vop(lambda e: e.memset(EPSC[:, 0:1], 1e-5),
        waits=[(S["v"], S["v"].n)])
    vop(lambda e: e.memset(EPSC[:, 1:2], 1.1920929e-07),
        waits=[(S["v"], S["v"].n)])
    vop(lambda e: e.memset(X[:, :, 0:1], 0.0),
        waits=[(S["v"], S["v"].n)])
    v_guard = vop(lambda e: e.memset(X[:, :, 2049:2050], 0.0),
                  waits=[(S["v"], S["v"].n)])

    dma("sync", S["d0"], BIASC[:, 0:8], abq_c[:, :])
    dma("sync", S["d0"], BIASC[:, 8:16], abo_c[:, :])
    dma("sync", S["d0"], BIASC[:, 16:17], abv_c[:, :])
    dma("sync", S["d0"], BIASC[:, 17:18], bqnw_c[:, :])
    d_bias = dma("sync", S["d0"], BIASC[:, 18:19], bknw_c[:, :])

    t_x0 = S["t"].n
    v_x0 = S["v"].n
    xd = [S["dx"], S["d1"], S["d2"], S["d3"]]
    xslot = [BC1[:, 0:128], BC1[:, 128:256], BC2[:, 0:128], BC2[:, 128:256]]
    dcnt = [0, 0, 0, 0]
    for c in range(NCH):
        for tt in range(NT):
            i = c * NT + tt
            k = i % 4
            waits = [(S["t"], t_x0 + i - 3)] if i >= 4 else []
            dcnt[k] = dma("sync", xd[k], xslot[k],
                          x_in[128 * tt:128 * (tt + 1), 128 * c:128 * (c + 1)],
                          waits=waits)
            tn = top(lambda e, s=xslot[k], pb=PB[i % 2]: e.transpose(
                pb[:, 0:128], s, IDENT[:, :]),
                waits=[(xd[k], dcnt[k]), (S["v"], v_guard),
                       (S["v"], v_x0 + i - 1)] +
                ([(S["d0"], d_bias)] if i == 0 else []))
            vop(lambda e, pb=PB[i % 2], c=c, tt=tt: e.tensor_copy(
                Xs(c, 1 + 128 * tt, 128), pb[:, 0:128]),
                waits=[(S["t"], tn)])
    v_xload = S["v"].n
    bank_guard[0] = (S["v"], v_xload)
    bank_guard[1] = (S["v"], v_xload)

    dma("gpsimd", S["g0"], WSLOT[:, :, 1024:1088],
        aWkT[:, :].rearrange("(c p) o -> p c o", p=128))
    dma("gpsimd", S["g0"], WSLOT[:, :, 1088:1152],
        aWvT[:, :].rearrange("(c p) o -> p c o", p=128))
    g_awq = dma("gpsimd", S["g0"], WSLOT[:, :, 0:1024],
                aWqT[:, :].rearrange("(c p) o -> p c o", p=128))

    for l in range(2):
        for c in range(NCH):
            dma("gpsimd", S["g1"], wqkvC[l, 128 * c:128 * (c + 1), :],
                fwqkv[l, 128 * c:128 * (c + 1), :])
        for c in range(4):
            dma("gpsimd", S["g1"], woC[l, 128 * c:128 * (c + 1), :],
                fwo[l, 128 * c:128 * (c + 1), :])
        for pr in range(NPAIR):
            off, wid = A_OFFS[pr], A_WIDS[pr]
            for half in range(2):
                hoff = 2730 * half
                dma("gpsimd", S["g1"],
                    bass.AP(w1C.tensor,
                            (((l * NPAIR + pr) * 2 + half) * 128 * NCH * 128),
                            [[NCH * 128, 128], [128, NCH], [1, wid]]),
                    bass.AP(fw1.tensor, l * D * 5460 + hoff + off,
                            [[5460, 128], [128 * 5460, NCH], [1, wid]]))
        for c in range(22):
            w = min(128, HID - 128 * c)
            dma("gpsimd", S["g1"], w2C[l, 128 * c:128 * c + w, :],
                fw2[l, 128 * c:128 * c + w, :])
    g_wcache = S["g1"].n

    def emit_stats_block(get_src, eps, dep_waits):
        RM = ROWS[:, 0:512]
        RV = ROWS[:, 512:1024]
        RI = ROWS[:, 0:512]
        sq_t = [0, 0]
        for c in range(NCH):
            src = get_src(c)
            sq = TMPA[:, 0:512] if c % 2 == 0 else TMPB[:, 0:512]
            w_sq = list(dep_waits) + [(S["s"], S["s"].n)]
            if c > 1:
                w_sq.append((S["t"], sq_t[c % 2]))
            scnt = sop(lambda e, s=src, sq=sq: e.activation(sq, s,
                                                            AF.Square),
                       waits=w_sq)
            top(lambda e, s=src, st=(c == 0), sp=(c == NCH - 1): e.matmul(
                PB[4][0:1, 0:512], STAT3[:, 0:1], s, start=st, stop=sp),
                waits=(bw(4) + bw(5) + list(dep_waits)) if c == 0 else [])
            sq_t[c % 2] = top(lambda e, sq=sq, st=(c == 0),
                              sp=(c == NCH - 1):
                              e.matmul(PB[5][0:1, 0:512], STAT3[:, 0:1], sq,
                                       start=st, stop=sp),
                              waits=[(S["s"], scnt)])
        last_sq_t = sq_t[(NCH - 1) % 2]
        v1 = vop(lambda e: e.tensor_copy(RM, PB[4][0:1, 0:512]),
                 waits=[(S["t"], last_sq_t), (S["v"], S["v"].n)])
        v1a = vop(lambda e: e.tensor_copy(RV, PB[5][0:1, 0:512]),
                  waits=[(S["v"], v1)])
        bank_guard[4] = (S["v"], v1)
        bank_guard[5] = (S["v"], v1a)
        t1 = top(lambda e: e.matmul(PBX[:, :], ONESC[:, :], RM,
                                    start=True, stop=True),
                 waits=pbxw() + [(S["v"], v1)])
        v4 = vop(lambda e: e.tensor_copy(BC2[:, :], PBX[:, :]),
                 waits=[(S["t"], t1), (S["v"], S["v"].n)])
        v1b = vop(lambda e: e.tensor_mul(RI, RM, RM),
                  waits=[(S["v"], v1a), (S["t"], t1)])
        v2 = vop(lambda e: e.tensor_sub(RV, RV, RI),
                 waits=[(S["v"], v1b)])
        s2 = sop(lambda e: e.activation(RV, RV, AF.Ln,
                                        bias=EPSC[0:1, 0:1]),
                 waits=[(S["v"], v2), (S["s"], S["s"].n)])
        s3 = sop(lambda e: e.activation(RI, RV, AF.Exp, scale=-0.5),
                 waits=[(S["s"], s2)])
        t2 = top(lambda e: e.matmul(PBX[:, :], ONESC[:, :], RI,
                                    start=True, stop=True),
                 waits=[(S["v"], v4), (S["s"], s3)])
        v5 = vop(lambda e: e.tensor_copy(BC1[:, :], PBX[:, :]),
                 waits=[(S["t"], t2), (S["v"], S["v"].n)])
        pbx_guard[0] = (S["v"], v5)
        return v5

    def rowbc(row_ap, dst, width, extra_waits=()):
        t1 = top(lambda e, r=row_ap, w=width: e.matmul(
            PBX[:, 0:w], ONESC[:, :], r, start=True, stop=True),
            waits=pbxw() + list(extra_waits))
        v1 = vop(lambda e, d=dst, w=width: e.tensor_copy(d, PBX[:, 0:w]),
                 waits=[(S["t"], t1), (S["v"], S["v"].n)])
        pbx_guard[0] = (S["v"], v1)
        return v1

    def ln_materialize(dst_fn, v_bc, lo):
        for c in range(NCH):
            v1 = vop(lambda e, c=c, lo=lo: e.tensor_sub(
                TMP32[:, :], Xs(c, lo, 512), BC2[:, :]),
                waits=[(S["v"], v_bc if c == 0 else S["v"].n)])
            vop(lambda e, c=c: e.tensor_mul(dst_fn(c), TMP32[:, :],
                                            BC1[:, :]),
                waits=[(S["v"], v1)])
        return S["v"].n

    def entry_ln(mat_dst, extra_mat_waits=None):
        v_snap = [0] * NTB
        v_mat = [0] * NTB

        def snap(tb):
            lo = 1 + 512 * tb
            for c in range(NCH):
                v_snap[tb] = vop(lambda e, c=c, lo=lo: e.tensor_copy(
                    Bs(c, lo, 512), Xs(c, lo, 512)),
                    waits=[(S["v"], S["v"].n)])

        def stats(tb):
            lo = 1 + 512 * tb
            return emit_stats_block(lambda c, lo=lo: Bs(c, lo, 512), 1e-5,
                                    [(S["v"], v_snap[tb])])

        def mat(tb, v_bc):
            lo = 1 + 512 * tb
            w_extra = extra_mat_waits(tb) if extra_mat_waits else []
            for c in range(NCH):
                v1 = vop(lambda e, c=c, lo=lo: e.tensor_sub(
                    TMP32[:, :], Xs(c, lo, 512), BC2[:, :]),
                    waits=[(S["v"], v_bc if c == 0 else S["v"].n)] +
                    (w_extra if c == 0 else []))
                v_mat[tb] = vop(lambda e, c=c, lo=lo: e.tensor_mul(
                    mat_dst(c, lo), TMP32[:, :], BC1[:, :]),
                    waits=[(S["v"], v1)])

        snap(0)
        vbc = stats(0)
        for tb in range(NTB):
            if tb + 1 < NTB:
                snap(tb + 1)
            mat(tb, vbc)
            if tb + 1 < NTB:
                vbc = stats(tb + 1)
        return v_mat

    def ctx_norm(pv_bank, dst_ap, t_pv):
        s1 = sop(lambda e, b=pv_bank: e.activation(
            ROWS[:, 0:512], PB[b][64:65, :], AF.Ln),
            waits=[(S["t"], t_pv), (S["s"], S["s"].n)])
        s2 = sop(lambda e: e.activation(RECB[:, 0:512], ROWS[:, 0:512],
                                        AF.Exp, scale=-1.0),
                 waits=[(S["s"], s1)])
        tb2 = top(lambda e: e.matmul(PBX[0:64, :], ONESCB[0:1, 0:64],
                                     RECB[:, 0:512], start=True, stop=True),
                  waits=pbxw() + [(S["s"], s2)])
        s3 = sop(lambda e: e.copy(TMP32[0:64, :], PBX[0:64, :]),
                 waits=[(S["t"], tb2), (S["s"], S["s"].n),
                        (S["v"], S["v"].n)])
        pbx_guard[0] = (S["s"], s3)
        vc = vop(lambda e, d=dst_ap, b=pv_bank: e.tensor_mul(
            d, PB[b][0:64, :], TMP32[0:64, :]),
            waits=[(S["s"], s3), (S["v"], S["v"].n)])
        bank_guard[pv_bank] = (S["v"], vc)
        return vc

    def ctx_norm2(dstA, dstB, t_pv):
        s1 = sop(lambda e: e.activation(ROWS[:, 0:512], PVa[64:65, :],
                                        AF.Ln),
                 waits=[(S["t"], t_pv), (S["s"], S["s"].n)])
        s1 = sop(lambda e: e.activation(ROWS[:, 512:1024], PVb[64:65, :],
                                        AF.Ln),
                 waits=[(S["s"], s1)])
        s2 = sop(lambda e: e.activation(RECB[:, :], ROWS[:, 0:1024],
                                        AF.Exp, scale=-1.0),
                 waits=[(S["s"], s1)])
        tb2 = top(lambda e: e.matmul(PBX[0:64, :], ONESCB[0:1, 0:64],
                                     RECB[:, 0:512], start=True, stop=True),
                  waits=pbxw() + [(S["s"], s2)])
        tb2 = top(lambda e: e.matmul(PBX[64:128, :], ONESCB[0:1, 64:128],
                                     RECB[:, 512:1024], start=True,
                                     stop=True))
        s3 = sop(lambda e: e.copy(TMP32[:, :], PBX[:, :]),
                 waits=[(S["t"], tb2), (S["s"], S["s"].n),
                        (S["v"], S["v"].n)])
        pbx_guard[0] = (S["s"], s3)
        vcA = vop(lambda e, d=dstA: e.tensor_mul(d, PVa[0:64, :],
                                                 TMP32[0:64, :]),
                  waits=[(S["s"], s3), (S["v"], S["v"].n)])
        vcB = vop(lambda e, d=dstB: e.tensor_mul(d, PVb[0:64, :],
                                                 TMP32[64:128, :]),
                  waits=[(S["v"], vcA)])
        bank_guard[4] = (S["v"], vcB)
        bank_guard[5] = (S["v"], vcB)
        return vcB

    def phase_attn_a():
        barrier()
        v_mat = entry_ln(lambda c, lo: Bs(c, lo, 512))
        v_y1 = v_mat[NTB - 1]

        g_wq = g_awq

        for tb in range(NTB):
            lo = 1 + 512 * tb
            for c in range(NCH):
                top(lambda e, c=c, lo=lo: e.matmul(
                    PB[0][0:64, :], WSLOT[:, c, 1024:1088], Bs(c, lo, 512),
                    start=(c == 0), stop=(c == NCH - 1)),
                    waits=(bw(0) + [(S["g0"], g_wq), (S["v"], v_mat[tb])])
                    if c == 0 else [])
            tk = S["t"].n
            vk = vop(lambda e, tb=tb: e.tensor_copy(
                KD[0:64, 0, 512 * tb:512 * (tb + 1)], PB[0][0:64, :]),
                waits=[(S["t"], tk), (S["v"], S["v"].n)])
            bank_guard[0] = (S["v"], vk)
            for c in range(NCH):
                top(lambda e, c=c, lo=lo: e.matmul(
                    PB[1][0:64, :], WSLOT[:, c, 1088:1152], Bs(c, lo, 512),
                    start=(c == 0), stop=(c == NCH - 1)),
                    waits=(bw(1) + [(S["v"], v_mat[tb])]) if c == 0 else [])
            tv = S["t"].n
            vv = vop(lambda e: e.tensor_scalar_add(
                TMPB[0:64, 0:512], PB[1][0:64, :], BIASC[0:64, 16:17]),
                waits=[(S["t"], tv), (S["g0"], g_wq),
                       (S["v"], S["v"].n)])
            bank_guard[1] = (S["v"], vv)
            for q in range(4):
                tt = 4 * tb + q
                tq = top(lambda e, q=q: e.transpose(
                    PBF[:, 0:64], TMPB[0:64, 128 * q:128 * (q + 1)],
                    IDENTB[0:64, 0:64]),
                    waits=[(S["v"], vv)] + pbfw())
                vq = vop(lambda e, tt=tt: e.tensor_copy(
                    V65[:, 65 * tt:65 * tt + 64], PBF[:, 0:64]),
                    waits=[(S["t"], tq), (S["v"], S["v"].n)])
                pbf_guard[0] = (S["v"], vq)
        v_ones = vop(lambda e: e.memset(
            bass.AP(V65, 64, [[4160, 128], [65, NT], [1, 1]]), 1.0),
            waits=[(S["v"], S["v"].n)])

        for tb in range(NTB):
            lo = 1 + 512 * tb
            stage_v = []
            for ot in range(NCH):
                bank = 2 + ot % 2
                for c in range(NCH):
                    top(lambda e, c=c, ot=ot, lo=lo, bank=bank: e.matmul(
                        PB[bank][:, :], WSLOT[:, c, 128 * ot:128 * (ot + 1)],
                        Bs(c, lo, 512), start=(c == 0), stop=(c == NCH - 1)),
                        waits=(bw(bank) + [(S["g0"], g_wq),
                                           (S["d0"], d_bias)])
                        if c == 0 else [])
                tq = S["t"].n
                vq = vop(lambda e, ot=ot, bank=bank: e.tensor_scalar(
                    AR2[:, 512 * ot:512 * (ot + 1)], PB[bank][:, :],
                    BIASC[:, ot:ot + 1], 0.125, ALU.add, ALU.mult),
                    waits=[(S["t"], tq), (S["v"], S["v"].n)])
                bank_guard[bank] = (S["v"], vq)
                stage_v.append(vq)
            for ot in range(NCH):
                vop(lambda e, ot=ot, lo=lo: e.tensor_copy(
                    Bs(ot, lo, 512), AR2[:, 512 * ot:512 * (ot + 1)]),
                    waits=[(S["v"], stage_v[ot]), (S["v"], S["v"].n)])
        v_qdone = S["v"].n
        d_kdup = dma("sync", S["d0"], KD[64:128, 0, 0:T], KD[0:64, 0, 0:T],
                     waits=[(S["v"], v_qdone)])
        g_wo = dma("gpsimd", S["g0"], WSLOT[:, :, 0:1024],
                   aWoT[:, :].rearrange("(c p) o -> p c o", p=128),
                   waits=[(S["v"], v_qdone)])

        for tb in range(NTB):
            lo = 1 + 512 * tb
            prev_ctx = [None]
            for hp in range(NCH):
                sexp = [0, 0]
                tpv = [0, 0]
                tS = [0] * NT

                def emit_S(j, hp=hp, lo=lo):
                    sl = j % 2
                    w0 = []
                    if j < 2:
                        w0 += bw(2 * sl) + bw(2 * sl + 1)
                    else:
                        w0 += [(S["s"], sexp[sl])]
                    if j == 0 and hp == 0 and tb == 0:
                        w0 += [(S["d0"], d_kdup), (S["v"], v_ones)]
                    top(lambda e, j=j, hp=hp, lo=lo, sl=sl: e.matmul(
                        SP[:, 1024 * sl:1024 * sl + 512],
                        KD[0:64, 0, 128 * j:128 * (j + 1)],
                        Bs(hp, lo, 512, 0, 64), start=True, stop=True),
                        waits=w0)
                    tS[j] = top(lambda e, j=j, hp=hp, lo=lo, sl=sl: e.matmul(
                        SP[:, 1024 * sl + 512:1024 * sl + 1024],
                        KD[64:128, 0, 128 * j:128 * (j + 1)],
                        Bs(hp, lo, 512, 64, 64), start=True, stop=True))

                emit_S(0)
                emit_S(1)
                if prev_ctx[0] is not None:
                    prev_ctx[0]()
                for j in range(NT):
                    sl = j % 2
                    wE = [(S["t"], tS[j]), (S["s"], S["s"].n)]
                    if j >= 2:
                        wE.append((S["t"], tpv[sl]))
                    se = sop(lambda e, sl=sl: e.activation(
                        PT[:, 2 * sl:2 * sl + 2, :],
                        SP[:, 1024 * sl:1024 * sl + 1024], AF.Exp),
                        waits=wE)
                    sexp[sl] = se
                    bank_guard[2 * sl] = (S["s"], se)
                    bank_guard[2 * sl + 1] = (S["s"], se)
                    if j + 2 < NT:
                        emit_S(j + 2)
                    top(lambda e, j=j, sl=sl: e.matmul(
                        PVa[0:65, :], V65[:, 65 * j:65 * j + 65],
                        PT[:, 2 * sl, :], start=(j == 0),
                        stop=(j == NT - 1)),
                        waits=[(S["s"], se)] + (bw(4) if j == 0 else []))
                    tpv[sl] = top(lambda e, j=j, sl=sl: e.matmul(
                        PVb[0:65, :], V65[:, 65 * j:65 * j + 65],
                        PT[:, 2 * sl + 1, :], start=(j == 0),
                        stop=(j == NT - 1)),
                        waits=(bw(5) if j == 0 else []))
                t_pv = S["t"].n

                def mk_ctx(hp=hp, t_pv=t_pv):
                    def go():
                        ctx_norm2(AR2[0:64, 512 * hp:512 * hp + 512],
                                  AR2[64:128, 512 * hp:512 * hp + 512],
                                  t_pv)
                    return go
                prev_ctx[0] = mk_ctx()
            prev_ctx[0]()
            v_ctx = S["v"].n
            for ot in range(NCH):
                bank = ot % 2
                for c in range(NCH):
                    top(lambda e, c=c, ot=ot, bank=bank: e.matmul(
                        PB[bank][:, :], WSLOT[:, c, 128 * ot:128 * (ot + 1)],
                        AR2[:, 512 * c:512 * (c + 1)], start=(c == 0),
                        stop=(c == NCH - 1)),
                        waits=(bw(bank) + [(S["v"], v_ctx),
                                           (S["g0"], g_wo)])
                        if c == 0 else [])
                tn = S["t"].n
                vr = vop(lambda e, ot=ot, lo=lo, bank=bank:
                         e.scalar_tensor_tensor(
                             Xs(ot, lo, 512), PB[bank][:, :],
                             BIASC[:, 8 + ot:9 + ot], Xs(ot, lo, 512),
                             ALU.add, ALU.add),
                         waits=[(S["t"], tn), (S["v"], S["v"].n)])
                bank_guard[bank] = (S["v"], vr)

    def phase_ffn(app):
        barrier()
        pt_guard = [0]
        psem = [S["d0"], S["d1"], S["d2"], S["d3"]]
        d_park = [0] * NTB
        for tb in range(NTB):
            lo = 1 + 512 * tb
            for c in range(NCH):
                d_park[tb] = dma("scalar", psem[tb],
                                 xpark[c, :, lo:lo + 512],
                                 Xs(c, lo, 512),
                                 waits=[(S["v"], S["v"].n)] if (tb == 0 and
                                                               c == 0) else [])
        for q in range(NTB):
            p.do("gpsimd", None, waits=[(psem[q], d_park[q])])
        v_mat = entry_ln(lambda c, lo: Xs(c, lo, 512),
                         extra_mat_waits=lambda tb: [(psem[tb], d_park[tb])])
        v_z0 = v_mat[NTB - 1]

        def layer_loads(l):
            dma("sync", S["d1"], ROWS[:, 1024:1088], fqs[l, :][None, :],
                waits=[(S["s"], S["s"].n), (S["t"], S["t"].n)])
            d_qs = dma("sync", S["d1"], ROWS[:, 1088:1152],
                       fks[l, :][None, :])
            rowbc(ROWS[:, 1024:1088], QSB[:, 0, :], 64, [(S["d1"], d_qs)])
            v_qs = rowbc(ROWS[:, 1088:1152], QSB[:, 1, :], 64)
            g_rot = 0
            gate = [(S["v"], S["v"].n), (S["t"], S["t"].n)]
            for tbl in range(4):
                g_rot = dma("gpsimd", S["g0"], ROT[:, tbl, :, :],
                            rotc[tbl, :, :]
                            .rearrange("(t p) d -> p t d", p=128),
                            waits=gate)
                gate = ()
            g_wqkv = dma("sync", S["d2"], WSLOT[:, :, :],
                         wqkvC[l, :, :].rearrange("(c p) o -> p c o", p=128),
                         waits=[(S["g1"], g_wcache), (S["t"], S["t"].n)])
            g_wo = dma("sync", S["d2"],
                       WSLOT2[:, :].rearrange("p (c o) -> p c o", c=4),
                       woC[l, :, :].rearrange("(c p) o -> p c o", p=128))
            return g_wqkv, g_wo, g_rot, v_qs

        def layer_scales(g_rot, v_qs):
            v_tbl = 0
            for tbl in range(4):
                qk = 0 if tbl < 2 else 1
                for tt in range(NT):
                    if tbl % 2 == 0:
                        v_tbl = vop(lambda e, tbl=tbl, tt=tt, qk=qk:
                                    e.tensor_mul(ROT[:, tbl, tt, :],
                                                 ROT[:, tbl, tt, :],
                                                 QSB[:, qk, :]),
                                    waits=[(S["g0"], g_rot), (S["v"], v_qs),
                                           (S["v"], S["v"].n)])
                    else:
                        v_tbl = vop(lambda e, tbl=tbl, tt=tt, qk=qk:
                                    e.tensor_mul(ROT[:, tbl, tt, 0:32],
                                                 ROT[:, tbl, tt, 0:32],
                                                 QSB[:, qk, 32:64]),
                                    waits=[(S["g0"], g_rot), (S["v"], v_qs),
                                           (S["v"], S["v"].n)])
                        v_tbl = vop(lambda e, tbl=tbl, tt=tt, qk=qk:
                                    e.tensor_mul(ROT[:, tbl, tt, 32:64],
                                                 ROT[:, tbl, tt, 32:64],
                                                 QSB[:, qk, 0:32]),
                                    waits=[(S["v"], S["v"].n)])
            return v_tbl

        pending = layer_loads(0)
        for l in range(2):
            fence = barrier()
            g_wqkv, g_wo, g_rot_p, v_qs_p = pending
            dma("sync", S["d3"], LGC[:, :], flng_c[l, :, :],
                waits=[(S["v"], S["v"].n)])
            dma("sync", S["d3"], LBC[:, :], flnb_c[l, :, :])
            dma("sync", S["d3"], B2C[:, :], fb2_c[l, :, :])
            d_cols = dma("sync", S["d3"], B1C[:, :], fb1_c[l, :, :])

            ZBOFF = 0
            QTOFF = 8256
            VXOFF = 12352

            def ZBr(c, ring, lo2, n, p0=0, pn=128):
                off = ZBOFF + 1032 * c + 516 * ring + lo2
                return B[p0:p0 + pn, off:off + n]

            def KTr(ring, cb, lo2, n, p0=0, pn=128):
                off = 2048 * ring + 512 * cb + lo2
                return AR1[p0:p0 + pn, off:off + n]

            def QTc(ring, cb, lo2, n, p0=0, pn=128):
                off = QTOFF + 2048 * ring + 512 * cb + lo2
                return B[p0:p0 + pn, off:off + n]

            def Qtm(tt, lo2, n):
                off = 6144 + 512 * tt + lo2
                return AR1[:, off:off + n]

            def Ktm(tt, lo2, n):
                return AR2[:, 512 * tt + lo2:512 * tt + lo2 + n]

            def CTXc(cb, p0=0, pn=128):
                return AR2[p0:p0 + pn, 2048 + 512 * cb:2048 + 512 * (cb + 1)]

            def v65s(slot, h, n=65):
                if slot < 8:
                    return V65[:, 520 * slot + 65 * h:520 * slot + 65 * h + n]
                off = VXOFF + 520 * (slot - 8) + 65 * h
                return B[:, off:off + n]

            v_tbl = layer_scales(g_rot_p, v_qs_p)
            v65o = vop(lambda e: e.memset(
                bass.AP(V65, 64, [[4160, 128], [520, 8], [65, 8], [1, 1]]),
                1.0), waits=[(S["v"], S["v"].n)])
            v65o = vop(lambda e: e.memset(
                bass.AP(B, VXOFF + 64,
                        [[NCH * TG, 128], [520, 4], [65, 8], [1, 1]]),
                1.0), waits=[(S["v"], v65o)])

            zb_guard = [0, 0]
            qt_guard = [0, 0]
            kt_guard = [0, 0, 0]
            v65_guard = [0, 0, 0]
            prep_state = {}

            def prepA(w):
                wlo = 1 + 512 * w
                ring = w % 2
                wid = 514 if w == 3 else 516
                v_zb = 0
                for c in range(NCH):
                    v_zb = vop(lambda e, c=c, wlo=wlo, ring=ring, wid=wid:
                               e.tensor_copy(ZBr(c, ring, 0, wid),
                                             Xs(c, wlo - 1, wid)),
                               waits=[(S["v"], S["v"].n),
                                      (S["t"], zb_guard[ring])])
                v_rotq = [0] * 4
                g_rotk = [0] * 4
                for q in range(4):
                    for part in range(3):
                        bank = part
                        for c in range(NCH):
                            sh = 0 if c < 4 else -1
                            top(lambda e, c=c, q=q, part=part, bank=bank,
                                sh=sh, ring=ring: e.matmul(
                                    PB[bank][:, :],
                                    ZBr(c, ring, 1 + 128 * q + sh, 128),
                                    WSLOT[:, c, 512 * part:512 * (part + 1)],
                                    start=(c == 0), stop=(c == NCH - 1)),
                                waits=(bw(bank) +
                                       [(S["d2"], g_wo),
                                        (S["v"], v_zb)]) if c == 0 else [])
                    t_qkv = S["t"].n
                    slot = (4 * w + q) % 12
                    vq = vop(lambda e, q=q: e.tensor_copy(
                        Qtm(q, 0, 512), PB[0][:, :]),
                        waits=[(S["t"], t_qkv), (S["v"], S["v"].n)])
                    bank_guard[0] = (S["v"], vq)
                    sk = sop(lambda e, q=q: e.copy(
                        Ktm(q, 0, 512), PB[1][:, :]),
                        waits=[(S["t"], t_qkv), (S["s"], S["s"].n),
                               (S["gc"], S["gc"].n)])
                    bank_guard[1] = (S["s"], sk)
                    vv = vop(lambda e, slot=slot: e.tensor_copy(
                        bass.AP(V65.th if False else V65, 520 * slot,
                                [[4160, 128], [65, 8], [1, 64]])
                        if slot < 8 else
                        bass.AP(B, VXOFF + 520 * (slot - 8),
                                [[NCH * TG, 128], [65, 8], [1, 64]]),
                        PB[2][:, :].rearrange("p (h d) -> p h d", h=8)),
                        waits=[(S["v"], vq), (S["v"], v65o),
                               (S["t"], v65_guard[w % 3])])
                    bank_guard[2] = (S["v"], vv)
                    tt_ = 4 * w + q

                    vx = vop(lambda e, q=q: e.tensor_mul(
                        TMP32[:, :], Qtm(q, 0, 512), Qtm(q, 0, 512)),
                        waits=[(S["v"], vq), (S["v"], S["v"].n)])
                    vx = vop(lambda e: e.tensor_reduce(
                        NRM[:, :],
                        TMP32[:, :].rearrange("p (h d) -> p h d", h=8),
                        mybir.AxisListType.X, ALU.add),
                        waits=[(S["v"], vx)])
                    vx = vop(lambda e: e.tensor_scalar_max(
                        NRM[:, :], NRM[:, :], 1e-24),
                        waits=[(S["v"], vx)])
                    sx = sop(lambda e: e.activation(
                        NRM[:, :], NRM[:, :], AF.Ln),
                        waits=[(S["v"], vx), (S["s"], S["s"].n)])
                    sxq = sop(lambda e: e.activation(
                        NRM2[:, :], NRM[:, :], AF.Exp, scale=-0.5),
                        waits=[(S["s"], sx)])
                    vx = vop(lambda e, q=q: e.tensor_mul(
                        TMPA[:, 0:512].rearrange("p (h d) -> p h d", h=8),
                        Qtm(q, 0, 512).rearrange("p (h d) -> p h d", h=8),
                        bass.AP(NRM2, 0, [[8, 128], [1, 8], [0, 64]])),
                        waits=[(S["v"], S["v"].n), (S["s"], sxq)])
                    vx = vop(lambda e, tt_=tt_: e.tensor_mul(
                        TMPB[:, 0:512].rearrange("p (h d) -> p h d", h=8),
                        TMPA[:, 0:512].rearrange("p (h d) -> p h d", h=8),
                        bass.AP(ROT, (0 * NT + tt_) * 64,
                                [[4 * NT * 64, 128], [0, 8], [1, 64]])),
                        waits=[(S["v"], vx), (S["v"], v_tbl)])
                    vx1 = vop(lambda e, q=q, tt_=tt_: e.tensor_mul(
                        Qtm(q, 0, 512).rearrange(
                            "p (h d) -> p h d", h=8)[:, :, 0:32],
                        TMPA[:, 0:512].rearrange(
                            "p (h d) -> p h d", h=8)[:, :, 32:64],
                        bass.AP(ROT, (1 * NT + tt_) * 64,
                                [[4 * NT * 64, 128], [0, 8], [1, 32]])),
                        waits=[(S["v"], vx)])
                    vx2 = vop(lambda e, q=q, tt_=tt_: e.tensor_mul(
                        Qtm(q, 0, 512).rearrange(
                            "p (h d) -> p h d", h=8)[:, :, 32:64],
                        TMPA[:, 0:512].rearrange(
                            "p (h d) -> p h d", h=8)[:, :, 0:32],
                        bass.AP(ROT, (1 * NT + tt_) * 64 + 32,
                                [[4 * NT * 64, 128], [0, 8], [1, 32]])),
                        waits=[(S["v"], vx1)])
                    v_rotq[q] = vop(lambda e, q=q: e.tensor_add(
                        Qtm(q, 0, 512), Qtm(q, 0, 512), TMPB[:, 0:512]),
                        waits=[(S["v"], vx2)])

                    gx = gop(lambda e, q=q: e.tensor_mul(
                        TMPE[:, :], Ktm(q, 0, 512), Ktm(q, 0, 512)),
                        waits=[(S["s"], sk), (S["gc"], S["gc"].n)])
                    vkr = vop(lambda e: e.tensor_reduce(
                        NRM3[:, :],
                        TMPE[:, :].rearrange("p (h d) -> p h d", h=8),
                        mybir.AxisListType.X, ALU.add),
                        waits=[(S["gc"], gx), (S["v"], S["v"].n)])
                    gx = gop(lambda e: e.tensor_scalar_max(
                        NRM3[:, :], NRM3[:, :], 1e-24),
                        waits=[(S["v"], vkr), (S["gc"], S["gc"].n)])
                    sx = sop(lambda e: e.activation(
                        NRM3[:, :], NRM3[:, :], AF.Ln),
                        waits=[(S["gc"], gx), (S["s"], S["s"].n)])
                    sxk = sop(lambda e: e.activation(
                        NRM4[:, :], NRM3[:, :], AF.Exp, scale=-0.5),
                        waits=[(S["s"], sx)])
                    gx = gop(lambda e, q=q: e.tensor_mul(
                        TMPC[:, 0:512].rearrange("p (h d) -> p h d", h=8),
                        Ktm(q, 0, 512).rearrange("p (h d) -> p h d", h=8),
                        bass.AP(NRM4, 0, [[8, 128], [1, 8], [0, 64]])),
                        waits=[(S["gc"], S["gc"].n), (S["s"], sxk)])
                    gx = gop(lambda e, tt_=tt_: e.tensor_mul(
                        TMPE[:, :].rearrange("p (h d) -> p h d", h=8),
                        TMPC[:, 0:512].rearrange("p (h d) -> p h d", h=8),
                        bass.AP(ROT, (2 * NT + tt_) * 64,
                                [[4 * NT * 64, 128], [0, 8], [1, 64]])),
                        waits=[(S["gc"], gx), (S["v"], v_tbl)])
                    gx1 = gop(lambda e, q=q, tt_=tt_: e.tensor_mul(
                        Ktm(q, 0, 512).rearrange(
                            "p (h d) -> p h d", h=8)[:, :, 0:32],
                        TMPC[:, 0:512].rearrange(
                            "p (h d) -> p h d", h=8)[:, :, 32:64],
                        bass.AP(ROT, (3 * NT + tt_) * 64,
                                [[4 * NT * 64, 128], [0, 8], [1, 32]])),
                        waits=[(S["gc"], gx)])
                    gx2 = gop(lambda e, q=q, tt_=tt_: e.tensor_mul(
                        Ktm(q, 0, 512).rearrange(
                            "p (h d) -> p h d", h=8)[:, :, 32:64],
                        TMPC[:, 0:512].rearrange(
                            "p (h d) -> p h d", h=8)[:, :, 0:32],
                        bass.AP(ROT, (3 * NT + tt_) * 64 + 32,
                                [[4 * NT * 64, 128], [0, 8], [1, 32]])),
                        waits=[(S["gc"], gx1)])
                    g_rotk[q] = gop(lambda e, q=q: e.tensor_add(
                        Ktm(q, 0, 512), Ktm(q, 0, 512), TMPE[:, :]),
                        waits=[(S["gc"], gx2)])
                prep_state[w] = (v_rotq, g_rotk)
                zb_guard[ring] = S["t"].n

            def prepB(w):
                v_rotq, g_rotk = prep_state[w]
                for q in range(4):
                    for cb in range(4):
                        tq = top(lambda e, q=q, cb=cb: e.transpose(
                            PBF[:, 0:128], Qtm(q, 128 * cb, 128),
                            IDENTB[:, :]),
                            waits=[(S["v"], v_rotq[q]),
                                   (S["t"], qt_guard[w % 2])] + pbfw())
                        vq2 = vop(lambda e, q=q, cb=cb, w=w: e.tensor_copy(
                            QTc(w % 2, cb, 128 * q, 128), PBF[:, 0:128]),
                            waits=[(S["t"], tq), (S["v"], S["v"].n)])
                        pbf_guard[0] = (S["v"], vq2)
                        tk2 = top(lambda e, q=q, cb=cb: e.transpose(
                            PBF[:, 128:256], Ktm(q, 128 * cb, 128),
                            IDENTB[:, :]),
                            waits=[(S["gc"], g_rotk[q]),
                                   (S["t"], kt_guard[w % 3])] + pbfw())
                        vk2 = vop(lambda e, q=q, cb=cb, w=w: e.tensor_copy(
                            KTr(w % 3, cb, 128 * q, 128), PBF[:, 128:256]),
                            waits=[(S["t"], tk2), (S["v"], S["v"].n)])
                        pbf_guard[0] = (S["v"], vk2)
                prep_state[w] = S["v"].n

            def attn_win(w):
                kts = list(range(8)) if w > 0 else list(range(4, 8))
                v_qkT = prep_state[w]
                prev_ctx2 = [None]
                for cb in range(4):
                    sfree = {}
                    tpv = {}
                    nj = len(kts)
                    tS = [0] * nj

                    def emit_S(jj, cb=cb, kts=kts, w=w, v_qkT=v_qkT):
                        kt = kts[jj]
                        ring = (w - 1) % 3 if kt < 4 else w % 3
                        ktt = kt % 4
                        sl = jj % 2
                        w0 = []
                        if jj < 2:
                            w0 += bw(2 * sl) + bw(2 * sl + 1)
                        else:
                            w0 += [(S["s"], sfree[sl])]
                        if jj == 0:
                            w0 += [(S["v"], v_qkT)]
                        top(lambda e, ring=ring, cb=cb, ktt=ktt, sl=sl,
                            w=w: e.matmul(
                                SP[:, 1024 * sl:1024 * sl + 512],
                                KTr(ring, cb, 128 * ktt, 128, 0, 64),
                                QTc(w % 2, cb, 0, 512, 0, 64),
                                start=True, stop=True),
                            waits=w0)
                        tS[jj] = top(
                            lambda e, ring=ring, cb=cb, ktt=ktt, sl=sl, w=w:
                            e.matmul(SP[:, 1024 * sl + 512:1024 * sl + 1024],
                                     KTr(ring, cb, 128 * ktt, 128, 64, 64),
                                     QTc(w % 2, cb, 0, 512, 64, 64),
                                     start=True, stop=True))

                    emit_S(0)
                    if nj > 1:
                        emit_S(1)
                    if prev_ctx2[0] is not None:
                        prev_ctx2[0]()
                    for jj, kt in enumerate(kts):
                        sl = jj % 2
                        wE = [(S["t"], tS[jj]), (S["s"], S["s"].n)]
                        if jj >= 2:
                            wE.append((S["t"], tpv[sl]))
                        se = sop(lambda e, sl=sl: e.activation(
                            PT[:, 2 * sl:2 * sl + 2, :],
                            SP[:, 1024 * sl:1024 * sl + 1024], AF.Exp,
                            scale=8.0), waits=wE)
                        sfree[sl] = se
                        bank_guard[2 * sl] = (S["s"], se)
                        bank_guard[2 * sl + 1] = (S["s"], se)
                        if jj + 2 < nj:
                            emit_S(jj + 2)
                        vm = vop(lambda e, sl=sl, kt=kt: e.tensor_mul(
                            PT[:, 2 * sl:2 * sl + 2, :],
                            PT[:, 2 * sl:2 * sl + 2, :],
                            bass.AP(MASKS, kt * 512,
                                    [[8 * 512, 128], [0, 2], [1, 512]])),
                            waits=[(S["s"], se), (S["v"], S["v"].n)])
                        wv = w - 1 if kt < 4 else w
                        ktt = kt % 4
                        slot = (4 * wv + ktt) % 12
                        top(lambda e, cb=cb, slot=slot, jj=jj, nj=nj, sl=sl:
                            e.matmul(
                                PVa[0:65, :], v65s(slot, 2 * cb),
                                PT[:, 2 * sl, :], start=(jj == 0),
                                stop=(jj == nj - 1)),
                            waits=[(S["v"], vm)] + (bw(4) if jj == 0 else []))
                        tpv[sl] = top(
                            lambda e, cb=cb, slot=slot, jj=jj, nj=nj, sl=sl:
                            e.matmul(
                                PVb[0:65, :], v65s(slot, 2 * cb + 1),
                                PT[:, 2 * sl + 1, :], start=(jj == 0),
                                stop=(jj == nj - 1)),
                            waits=(bw(5) if jj == 0 else []))
                    t_pv = S["t"].n

                    def mk_ctx2(cb=cb, t_pv=t_pv):
                        def go():
                            ctx_norm2(CTXc(cb, 0, 64), CTXc(cb, 64, 64),
                                      t_pv)
                        return go
                    prev_ctx2[0] = mk_ctx2()
                prev_ctx2[0]()
                qt_guard[w % 2] = S["t"].n
                kt_guard[(w - 1) % 3] = S["t"].n
                if w > 0:
                    v65_guard[(w - 1) % 3] = S["t"].n

            def wo_win(w):
                wlo = 1 + 512 * w
                v_ctx = S["v"].n
                for ot in range(NCH):
                    bank = ot % 2
                    for cb in range(4):
                        top(lambda e, cb=cb, ot=ot, bank=bank: e.matmul(
                            PB[bank][:, :],
                            WSLOT2[:, 1024 * cb + 128 * ot:
                                   1024 * cb + 128 * (ot + 1)],
                            CTXc(cb), start=(cb == 0), stop=(cb == 3)),
                            waits=(bw(bank) + [(S["d2"], g_wo),
                                               (S["v"], v_ctx)])
                            if cb == 0 else [])
                    tn = S["t"].n
                    vr = vop(lambda e, ot=ot, wlo=wlo, bank=bank:
                             e.tensor_add(
                                 Xs(ot, wlo, 512), Xs(ot, wlo, 512),
                                 PB[bank][:, :]),
                             waits=[(S["t"], tn), (S["v"], S["v"].n)])
                    bank_guard[bank] = (S["v"], vr)

            prepA(0)
            prepB(0)
            prepA(1)
            for w in range(NTB):
                attn_win(w)
                if w + 1 < NTB:
                    prepB(w + 1)
                wo_win(w)
                if w + 2 < NTB:
                    prepA(w + 2)
            v_attn = S["v"].n
            if l == 0:
                pending = layer_loads(1)
                pending = (pending[0], pending[1], pending[2], pending[3])

            v_zs = vop(lambda e: e.tensor_copy(
                ZSAVE[:, :, :],
                bass.AP(X, 512, [[NCH * TG, 128], [TG, NCH], [512, 3]])),
                waits=[(S["v"], v_attn)])
            h2_guard = {0: 0, 1: 0}
            zw_guard = [0, 0]

            def H2(pp, c):
                if pp == 0:
                    return AR2[:, 512 * c:512 * (c + 1)]
                return B[:, 11264 + 512 * c:11264 + 512 * (c + 1)]

            def mlp_prep(tb):
                pp = tb % 2
                lo = 1 + 512 * tb
                v_zw = 0
                for c in range(NCH):
                    v_zw = vop(lambda e, c=c, lo=lo: e.tensor_copy(
                        AR1[:, 513 * c + 1:513 * c + 513], Xs(c, lo, 512)),
                        waits=[(S["v"], v_zs if (tb == 0 and c == 0)
                                else S["v"].n),
                               (S["t"], zw_guard[0]),
                               (S["s"], zw_guard[1])] if c == 0 else
                        [(S["v"], S["v"].n)])
                    if tb == 0:
                        v_zw = vop(lambda e, c=c: e.memset(
                            AR1[:, 513 * c:513 * c + 1], 0.0),
                            waits=[(S["v"], v_zw)])
                    else:
                        v_zw = vop(lambda e, c=c, tb=tb: e.tensor_copy(
                            AR1[:, 513 * c:513 * c + 1],
                            ZSAVE[:, c, tb - 1:tb]),
                            waits=[(S["v"], v_zw)])
                v_bc = emit_stats_block(
                    lambda c: AR1[:, 513 * c + (1 if c < 4 else 0):
                                  513 * c + (1 if c < 4 else 0) + 512],
                    1e-5, [(S["v"], v_zw)])
                zw_guard[0] = S["t"].n
                zw_guard[1] = S["s"].n
                hg = h2_guard[pp]
                v_h2c = [0] * NCH
                for c in range(NCH):
                    if c < 4:
                        v1 = vop(lambda e, c=c, lo=lo: e.tensor_sub(
                            TMP32[:, :], Xs(c, lo, 512), BC2[:, :]),
                            waits=[(S["v"], v_bc if c == 0 else S["v"].n)])
                    elif tb == 0:
                        v1 = vop(lambda e, c=c, lo=lo: e.tensor_sub(
                            TMP32[:, :], Xs(c, lo - 1, 512), BC2[:, :]),
                            waits=[(S["v"], v_bc if c == 0 else S["v"].n)])
                    else:
                        vop(lambda e, c=c, lo=lo: e.tensor_sub(
                            TMP32[:, 1:512], Xs(c, lo, 511), BC2[:, 1:512]),
                            waits=[(S["v"], S["v"].n)])
                        v1 = vop(lambda e, c=c, tb=tb: e.tensor_sub(
                            TMP32[:, 0:1], ZSAVE[:, c, tb - 1:tb],
                            BC2[:, 0:1]),
                            waits=[(S["v"], S["v"].n)])
                    v2 = vop(lambda e, c=c: e.scalar_tensor_tensor(
                        TMP32[:, :], TMP32[:, :], LGC[:, c:c + 1], BC1[:, :],
                        ALU.mult, ALU.mult),
                        waits=[(S["v"], v1), (S["d3"], d_cols)])
                    v_h2c[c] = vop(lambda e, c=c, pp=pp: e.tensor_scalar_add(
                        H2(pp, c), TMP32[:, :], LBC[:, c:c + 1]),
                        waits=[(S["v"], v2), (S["t"], hg)])
                return v_h2c

            vh2 = {0: mlp_prep(0)}
            for tb in range(NTB):
                lo = 1 + 512 * tb
                pp = tb % 2
                v_h2c = vh2[tb]
                v_h2 = v_h2c[NCH - 1]

                tg_last = [0, 0]
                last_va = 0
                for pr in range(NPAIR):
                    par = pr % 2
                    off, wid = A_OFFS[pr], A_WIDS[pr]
                    sA = S["w0a"] if par == 0 else S["w1a"]
                    sB = S["w0b"] if par == 0 else S["w1b"]
                    wld = [(S["t"], tg_last[par])] if pr >= 2 else \
                        [(S["g1"], g_wcache), (S["t"], h2_guard[pp])]
                    dna = dma("sync", sA,
                              bass.AP(KD, 2048 * par,
                                      [[2 * T, 128], [256, NCH], [1, wid]]),
                              bass.AP(w1C.tensor,
                                      (((l * NPAIR + pr) * 2 + 0) * 128 * NCH * 128),
                                      [[NCH * 128, 128], [128, NCH], [1, wid]]),
                              waits=wld)
                    dnb = dma("sync", sB,
                              bass.AP(KD, 2048 * par + 128,
                                      [[2 * T, 128], [256, NCH], [1, wid]]),
                              bass.AP(w1C.tensor,
                                      (((l * NPAIR + pr) * 2 + 1) * 128 * NCH * 128),
                                      [[NCH * 128, 128], [128, NCH], [1, wid]]))
                    ba, bg = 2 + 2 * par, 3 + 2 * par
                    tg_l = 0
                    for c in range(NCH):
                        top(lambda e, c=c, par=par, wid=wid, ba=ba,
                            pp=pp: e.matmul(
                            PB[ba][0:wid, :],
                            bass.AP(KD, 2048 * par + 256 * c,
                                    [[2 * T, 128], [1, wid]]),
                            H2(pp, c),
                            start=(c == 0), stop=(c == NCH - 1)),
                            waits=(bw(ba) + [(sA, dna), (sB, dnb)]
                                   if c == 0 else []) +
                            ([(S["v"], v_h2c[c])] if pr < 2 else []))
                        tg_l = top(lambda e, c=c, par=par, wid=wid, bg=bg,
                                   pp=pp: e.matmul(
                                       PB[bg][0:wid, :],
                                       bass.AP(KD, 2048 * par + 256 * c + 128,
                                               [[2 * T, 128], [1, wid]]),
                                       H2(pp, c),
                                       start=(c == 0), stop=(c == NCH - 1)),
                                   waits=(bw(bg)) if c == 0 else [])
                    tg_last[par] = tg_l
                    sg = sop(lambda e, pr=pr, wid=wid, bg=bg: e.activation(
                        TMPB[0:wid, 0:512], PB[bg][0:wid, :], AF.Gelu,
                        bias=B1C[0:wid, 22 + pr:23 + pr]),
                        waits=[(S["t"], tg_l), (S["s"], S["s"].n),
                               (S["v"], last_va)])
                    bank_guard[bg] = (S["s"], sg)
                    va = vop(lambda e, pr=pr, wid=wid, ba=ba:
                             e.scalar_tensor_tensor(
                                 B[0:wid, 512 * pr:512 * (pr + 1)],
                                 PB[ba][0:wid, :], B1C[0:wid, pr:pr + 1],
                                 TMPB[0:wid, 0:512], ALU.add, ALU.mult),
                             waits=[(S["s"], sg), (S["v"], S["v"].n)])
                    bank_guard[ba] = (S["v"], va)
                    last_va = va
                v_actn = S["v"].n
                h2_guard[pp] = S["t"].n
                if tb + 1 < NTB:
                    vh2[tb + 1] = mlp_prep(tb + 1)

                for half in range(2):
                    ots = list(range(4 * half, 4 * half + 4))
                    tw_last = [0, 0]
                    for pr in range(NPAIR):
                        par = pr % 2
                        off, wid = A_OFFS[pr], A_WIDS[pr]
                        dsem = S["w2a"] if par == 0 else S["w2b"]
                        wld = [(S["t"], tw_last[par])] if pr >= 2 else \
                            [(S["t"], pt_guard[0])]
                        d_w2 = dma("sync", dsem,
                                   bass.AP(PT, 1024 * par,
                                           [[4 * 512, wid], [1, 1024]]),
                                   w2C[l, off:off + wid, :], waits=wld)
                        tw = 0
                        for oi, ot in enumerate(ots):
                            w0 = []
                            if pr == 0:
                                w0 += bw(oi) + [(S["v"], v_actn)]
                            if oi == 0:
                                w0 += [(dsem, d_w2)]
                            tw = top(lambda e, pr=pr, par=par, wid=wid,
                                     ot=ot, oi=oi: e.matmul(
                                         PB[oi][:, :],
                                         bass.AP(PT, 1024 * par + 128 * ot,
                                                 [[4 * 512, wid], [1, 128]]),
                                         B[0:wid, 512 * pr:512 * (pr + 1)],
                                         start=(pr == 0),
                                         stop=(pr == NPAIR - 1)),
                                     waits=w0)
                        tw_last[par] = tw
                    t_done = S["t"].n
                    pt_guard[0] = t_done
                    for oi, ot in enumerate(ots):
                        vr = vop(lambda e, ot=ot, lo=lo, oi=oi:
                                 e.scalar_tensor_tensor(
                                     Xs(ot, lo, 512), PB[oi][:, :],
                                     B2C[:, ot:ot + 1], Xs(ot, lo, 512),
                                     ALU.add, ALU.add),
                                 waits=[(S["t"], t_done),
                                        (S["v"], S["v"].n)])
                        bank_guard[oi] = (S["v"], vr)
            v_z0 = S["v"].n

        v_fin = S["v"].n
        slots = [BC1[:, 0:512], BC2[:, 0:512], TMP32[:, 0:512]]
        ssem = [S["d0"], S["d1"], S["d2"]]
        vres = [0, 0, 0]
        for c in range(NCH):
            for tb in range(NTB):
                i = c * NTB + tb
                k = i % 3
                waits = [(S["v"], v_fin)]
                if i >= 3:
                    waits.append((S["v"], vres[k]))
                dx = dma("sync", ssem[k], slots[k],
                         xpark[c, :, 1 + 512 * tb:1 + 512 * (tb + 1)],
                         waits=waits)
                vres[k] = vop(lambda e, c=c, tb=tb, k=k: e.tensor_add(
                    Xs(c, 1 + 512 * tb, 512), Xs(c, 1 + 512 * tb, 512),
                    slots[k]),
                    waits=[(ssem[k], dx), (S["v"], S["v"].n)])

    def phase_attn_b():
        barrier()
        gw0 = dma("gpsimd", S["g0"], WSLOT2[0:64, 0:512], bDQT[:, :])
        dma("gpsimd", S["g0"], WSLOT2[64:128, 0:512], bDQT[:, :])
        dma("gpsimd", S["g0"], WSLOT2[0:64, 512:1536], bDKVT[:, :])
        dma("gpsimd", S["g0"], WSLOT2[64:128, 512:1536], bDKVT[:, :])
        dma("gpsimd", S["g0"],
            WSLOT2[:, 1536:1792].rearrange("p (c o) -> p c o", c=4),
            bUQT[:, :].rearrange("(c p) o -> p c o", p=128))
        dma("gpsimd", S["g0"],
            WSLOT2[:, 2048:3072].rearrange("p (c o) -> p c o", c=8),
            bUKVT[:, :].rearrange("(c p) o -> p c o", p=128))
        dma("gpsimd", S["g0"], WSLOT2[0:64, 3584:3648], bQRT[:, :])
        dma("gpsimd", S["g0"], WSLOT2[0:64, 3648:3712], bKRT[:, :])
        dma("gpsimd", S["g0"], WSLOT2[64:128, 3648:3712], bKRT[:, :])
        dma("gpsimd", S["g0"], WSLOT2[0:64, 3712:3776], bWOT[:, :])
        gw = dma("gpsimd", S["g0"], WSLOT2[64:128, 3712:3776], bWOT[:, :])

        v_mat = entry_ln(lambda c, lo: Bs(c, lo, 512))

        def Qh(h01, lo2, n, p0=0, pn=128):
            return AR1[p0:p0 + pn, 2048 * h01 + lo2:2048 * h01 + lo2 + n]

        def CQb(par, h01, p0=0, pn=128):
            off = 4096 + 1024 * par + 512 * h01
            return AR1[p0:p0 + pn, off:off + 512]

        def CKVb(par, h01, p0=0, pn=128):
            off = 6144 + 1024 * par + 512 * h01
            return AR1[p0:p0 + pn, off:off + 512]

        dqr_g = {0: 0, 1: 0}
        dkr_g = {0: 0, 1: 0}
        dv_g = [0]
        tre_g = {0: 0, 1: 0}

        def rms2_rows(t_ms_a, t_ms_b):
            v1 = vop(lambda e: e.tensor_copy(ROWS[:, 0:512], PBX[0:1, :]),
                     waits=[(S["t"], t_ms_a), (S["v"], S["v"].n)])
            v1a = vop(lambda e: e.tensor_copy(ROWS[:, 512:1024],
                                              PBX[64:65, :]),
                      waits=[(S["t"], t_ms_b), (S["v"], v1)])
            pbx_guard[0] = (S["v"], v1a)
            s1 = sop(lambda e: e.activation(
                ROWS[:, 0:1024], ROWS[:, 0:1024], AF.Ln,
                bias=EPSC[0:1, 1:2], scale=1.0 / 64.0),
                waits=[(S["v"], v1a), (S["s"], S["s"].n)])
            s1b = sop(lambda e: e.activation(
                RECB[:, :], ROWS[:, 0:1024], AF.Exp, scale=-0.5),
                waits=[(S["s"], s1)])
            return s1b

        def rms2_bcast(s1b):
            tb1 = top(lambda e: e.matmul(
                PBX[0:64, :], ONESCB[0:1, 0:64], RECB[:, 0:512],
                start=True, stop=True),
                waits=pbxw() + [(S["s"], s1b)])
            tb2 = top(lambda e: e.matmul(
                PBX[64:128, :], ONESCB[0:1, 64:128], RECB[:, 512:1024],
                start=True, stop=True))
            s3 = sop(lambda e: e.copy(TMP32[:, :], PBX[:, :]),
                     waits=[(S["t"], tb2), (S["s"], S["s"].n),
                            (S["v"], S["v"].n)])
            pbx_guard[0] = (S["s"], s3)
            return s3

        for hp in range(NCH):
            for tb in range(NTB):
                lo = 1 + 512 * tb

                def emit_cq(lt, hp=hp, lo=lo):
                    par = lt % 2
                    bA, bB = 2 * par, 2 * par + 1
                    w0 = bw(bA) + bw(bB)
                    if lt == 0:
                        w0 += [(S["v"], v_mat[tb])]
                    if hp == 0 and tb == 0 and lt == 0:
                        w0 += [(S["g0"], gw)]
                    top(lambda e, hp=hp, lo=lo, lt=lt, bA=bA: e.matmul(
                        PB[bA][:, :], WSLOT2[0:64, 128 * lt:128 * (lt + 1)],
                        Bs(hp, lo, 512, 0, 64), start=True, stop=True),
                        waits=w0)
                    return top(lambda e, hp=hp, lo=lo, lt=lt, bB=bB:
                               e.matmul(
                                   PB[bB][:, :],
                                   WSLOT2[64:128, 128 * lt:128 * (lt + 1)],
                                   Bs(hp, lo, 512, 64, 64),
                                   start=True, stop=True))

                tcq = [0] * 4
                qc_done = [0, 0]
                tcq[0] = emit_cq(0)
                for lt in range(4):
                    par = lt % 2
                    bA, bB = 2 * par, 2 * par + 1
                    if lt + 1 < 4:
                        tcq[lt + 1] = emit_cq(lt + 1)
                    vA = vop(lambda e, par=par, bA=bA: e.tensor_copy(
                        CQb(par, 0), PB[bA][:, :]),
                        waits=[(S["t"], tcq[lt]), (S["v"], S["v"].n),
                               (S["t"], qc_done[par])])
                    sB = sop(lambda e, par=par, bB=bB: e.copy(
                        CQb(par, 1), PB[bB][:, :]),
                        waits=[(S["t"], tcq[lt]), (S["s"], S["s"].n),
                               (S["t"], qc_done[par])])
                    bank_guard[bA] = (S["v"], vA)
                    bank_guard[bB] = (S["s"], sB)
                    top(lambda e, lt=lt: e.matmul(
                        PVa[0:64, :],
                        WSLOT2[:, 1536 + 64 * lt:1600 + 64 * lt],
                        CQb(lt % 2, 0), start=(lt == 0), stop=(lt == 3)),
                        waits=[(S["v"], vA)] + (bw(4) if lt == 0 else []))
                    tqc = top(lambda e, lt=lt: e.matmul(
                        PVb[0:64, :],
                        WSLOT2[:, 1536 + 64 * lt:1600 + 64 * lt],
                        CQb(lt % 2, 1), start=(lt == 0), stop=(lt == 3)),
                        waits=[(S["s"], sB)] + (bw(5) if lt == 0 else []))
                    qc_done[par] = tqc

                sqA = sop(lambda e: e.activation(TMPA[0:64, 0:512],
                                                 PVa[0:64, :], AF.Square),
                          waits=[(S["t"], tqc), (S["s"], S["s"].n)])
                sqB = sop(lambda e: e.activation(TMPA[64:128, 0:512],
                                                 PVb[0:64, :], AF.Square),
                          waits=[(S["s"], sqA)])
                tmsA = top(lambda e: e.matmul(
                    PBX[0:1, :], HALF2[0:64, 0:1], TMPA[0:64, 0:512],
                    start=True, stop=True),
                    waits=[(S["s"], sqA)] + pbxw())
                tmsB = top(lambda e: e.matmul(
                    PBX[64:65, :], HALF2[64:128, 1:2], TMPA[64:128, 0:512],
                    start=True, stop=True),
                    waits=[(S["s"], sqB)])
                s1b = rms2_rows(tmsA, tmsB)
                tkv_started = []

                def emit_ckv(lt, hp=hp, lo=lo):
                    par = lt % 2
                    bA, bB = 2 * par, 2 * par + 1
                    top(lambda e, hp=hp, lo=lo, lt=lt, bA=bA: e.matmul(
                        PB[bA][:, :],
                        WSLOT2[0:64, 512 + 128 * lt:512 + 128 * (lt + 1)],
                        Bs(hp, lo, 512, 0, 64), start=True, stop=True),
                        waits=bw(bA) + bw(bB))
                    return top(lambda e, hp=hp, lo=lo, lt=lt, bB=bB:
                               e.matmul(
                                   PB[bB][:, :],
                                   WSLOT2[64:128,
                                          512 + 128 * lt:512 + 128 * (lt + 1)],
                                   Bs(hp, lo, 512, 64, 64),
                                   start=True, stop=True))

                tckv = [0] * 8
                tckv[0] = emit_ckv(0)
                tckv[1] = emit_ckv(1)
                s3 = rms2_bcast(s1b)
                vqA = vop(lambda e, lo=lo: e.scalar_tensor_tensor(
                    Qh(0, lo - 1, 512, 0, 64), PVa[0:64, :],
                    BIASC[0:64, 17:18], TMP32[0:64, :],
                    ALU.mult, ALU.mult),
                    waits=[(S["s"], s3), (S["v"], S["v"].n)])
                vqB = vop(lambda e, lo=lo: e.scalar_tensor_tensor(
                    Qh(1, lo - 1, 512, 0, 64), PVb[0:64, :],
                    BIASC[64:128, 17:18], TMP32[64:128, :],
                    ALU.mult, ALU.mult),
                    waits=[(S["v"], vqA)])
                bank_guard[4] = (S["v"], vqB)
                bank_guard[5] = (S["v"], vqB)
                for h01 in range(2):
                    bank = 4 + h01
                    tr = top(lambda e, h01=h01, lo=lo, bank=bank: e.matmul(
                        PB[bank][0:64, :], WSLOT2[0:64, 3584:3648],
                        Qh(h01, lo - 1, 512, 0, 64), start=True, stop=True),
                        waits=bw(bank) + [(S["v"], vqB)])
                    qsem = S["d0"] if h01 == 0 else S["dq1"]
                    ksem = S["d2"] if h01 == 0 else S["dk1"]
                    vr = vop(lambda e, h01=h01, bank=bank: e.tensor_copy(
                        TMPC[64 * h01:64 * h01 + 64, 0:512],
                        PB[bank][0:64, :]),
                        waits=[(S["t"], tr), (S["v"], S["v"].n),
                               (qsem, dqr_g[h01]),
                               (ksem, dkr_g[h01])])
                    bank_guard[bank] = (S["v"], vr)
                    dqr_g[h01] = dma("sync", qsem,
                                     Qh(h01, lo - 1, 512, 64, 64),
                                     TMPC[64 * h01:64 * h01 + 64, 0:512],
                                     waits=[(S["v"], vr)])

                tkv = 0
                kv_done = [0, 0]
                for lt in range(NCH):
                    par = lt % 2
                    bA, bB = 2 * par, 2 * par + 1
                    vA = vop(lambda e, par=par, bA=bA: e.tensor_copy(
                        CKVb(par, 0), PB[bA][:, :]),
                        waits=[(S["t"], tckv[lt]), (S["v"], S["v"].n),
                               (S["t"], kv_done[par])])
                    sB = sop(lambda e, par=par, bB=bB: e.copy(
                        CKVb(par, 1), PB[bB][:, :]),
                        waits=[(S["t"], tckv[lt]), (S["s"], S["s"].n),
                               (S["t"], kv_done[par])])
                    bank_guard[bA] = (S["v"], vA)
                    bank_guard[bB] = (S["s"], sB)
                    if lt + 2 < NCH:
                        tckv[lt + 2] = emit_ckv(lt + 2)
                    top(lambda e, lt=lt: e.matmul(
                        PVa[:, :],
                        WSLOT2[:, 2048 + 128 * lt:2176 + 128 * lt],
                        CKVb(lt % 2, 0), start=(lt == 0), stop=(lt == 7)),
                        waits=[(S["v"], vA)] + (bw(4) if lt == 0 else []))
                    tkv = top(lambda e, lt=lt: e.matmul(
                        PVb[:, :],
                        WSLOT2[:, 2048 + 128 * lt:2176 + 128 * lt],
                        CKVb(lt % 2, 1), start=(lt == 0), stop=(lt == 7)),
                        waits=[(S["s"], sB)] + (bw(5) if lt == 0 else []))
                    kv_done[par] = tkv

                sqA = sop(lambda e: e.activation(TMPA[0:64, 0:512],
                                                 PVa[0:64, :], AF.Square),
                          waits=[(S["t"], tkv), (S["s"], S["s"].n)])
                sqB = sop(lambda e: e.activation(TMPA[64:128, 0:512],
                                                 PVb[0:64, :], AF.Square),
                          waits=[(S["s"], sqA)])
                tmsA = top(lambda e: e.matmul(
                    PBX[0:1, :], HALF2[0:64, 0:1], TMPA[0:64, 0:512],
                    start=True, stop=True),
                    waits=[(S["s"], sqA)] + pbxw())
                tmsB = top(lambda e: e.matmul(
                    PBX[64:65, :], HALF2[64:128, 1:2], TMPA[64:128, 0:512],
                    start=True, stop=True),
                    waits=[(S["s"], sqB)])
                s1b = rms2_rows(tmsA, tmsB)
                s3 = rms2_bcast(s1b)
                for h01 in range(2):
                    pv = PVa if h01 == 0 else PVb
                    vk = vop(lambda e, h01=h01, tb=tb, pv=pv:
                             e.scalar_tensor_tensor(
                                 KD[0:64, h01, 512 * tb:512 * (tb + 1)],
                                 pv[0:64, :],
                                 BIASC[64 * h01:64 * h01 + 64, 18:19],
                                 TMP32[64 * h01:64 * h01 + 64, :],
                                 ALU.mult, ALU.mult),
                             waits=[(S["s"], s3), (S["v"], S["v"].n)])
                    vv = vop(lambda e, pv=pv: e.tensor_copy(
                        TMPB[64:128, 0:512], pv[64:128, :]),
                        waits=[(S["v"], vk), (S["d1"], dv_g[0])])
                    bank_guard[4 + h01] = (S["v"], vv)
                    dv_g[0] = dma("sync", S["d1"],
                                  TMPE[64 * h01:64 * h01 + 64, 0:512],
                                  TMPB[64:128, 0:512],
                                  waits=[(S["v"], vv), (S["t"], tre_g[h01])])
                    for q in range(4):
                        tt = 4 * tb + q
                        tq = top(lambda e, q=q, h01=h01: e.transpose(
                            PBF[:, 0:64],
                            TMPE[64 * h01:64 * h01 + 64,
                                 128 * q:128 * (q + 1)],
                            IDENTB[0:64, 0:64] if h01 == 0 else
                            IDENTB[64:128, 64:128]),
                            waits=[(S["d1"], dv_g[0])] + pbfw())
                        vq2 = vop(lambda e, tt=tt, h01=h01: e.tensor_copy(
                            V65[:, 1040 * h01 + 65 * tt:
                                1040 * h01 + 65 * tt + 64], PBF[:, 0:64]),
                            waits=[(S["t"], tq), (S["v"], S["v"].n)])
                        pbf_guard[0] = (S["v"], vq2)
                    tre_g[h01] = S["t"].n
                for h01 in range(2):
                    bank = h01
                    tr = top(lambda e, hp=hp, lo=lo, h01=h01, bank=bank:
                             e.matmul(
                                 PB[bank][0:64, :],
                                 WSLOT2[64 * h01:64 * h01 + 64, 3648:3712],
                                 Bs(hp, lo, 512, 64 * h01, 64),
                                 start=True, stop=True),
                             waits=bw(bank))
                    qsem = S["d0"] if h01 == 0 else S["dq1"]
                    ksem = S["d2"] if h01 == 0 else S["dk1"]
                    vr = vop(lambda e, bank=bank, h01=h01: e.tensor_copy(
                        TMPC[64 * h01:64 * h01 + 64, 0:512],
                        PB[bank][0:64, :]),
                        waits=[(S["t"], tr), (S["v"], S["v"].n),
                               (ksem, dkr_g[h01]),
                               (qsem, dqr_g[h01])])
                    bank_guard[bank] = (S["v"], vr)
                    dkr_g[h01] = dma("sync", ksem,
                                     KD[64:128, h01, 512 * tb:512 * (tb + 1)],
                                     TMPC[64 * h01:64 * h01 + 64, 0:512],
                                     waits=[(S["v"], vr)])
            d_proj = [(S["d0"], S["d0"].n), (S["d1"], S["d1"].n),
                      (S["d2"], S["d2"].n), (S["dq1"], S["dq1"].n),
                      (S["dk1"], S["dk1"].n)]
            v65o = vop(lambda e: e.memset(
                bass.AP(V65, 64, [[4160, 128], [1040, 2], [65, NT], [1, 1]]),
                1.0), waits=[(S["v"], S["v"].n)])

            prev_ctxb = [None]
            for tb in range(NTB):
                for h01 in range(2):
                    njt = 4 * tb + 4
                    npair = njt // 2
                    sfree = {}
                    tpvp = {}
                    tS = [0] * npair

                    def emit_S(jp, h01=h01, tb=tb):
                        sl = jp % 2
                        j0, j1 = 2 * jp, 2 * jp + 1
                        w0 = []
                        if jp < 2:
                            w0 += bw(2 * sl) + bw(2 * sl + 1)
                        else:
                            w0 += [(S["s"], sfree[sl])]
                        if jp == 0:
                            w0 += [(S["v"], v65o)] + d_proj
                        top(lambda e, j=j0, h01=h01, tb=tb, sl=sl:
                            e.matmul(
                                SP[:, 1024 * sl:1024 * sl + 512],
                                KD[:, h01, 128 * j:128 * (j + 1)],
                                Qh(h01, 512 * tb, 512),
                                start=True, stop=True),
                            waits=w0)
                        tS[jp] = top(lambda e, j=j1, h01=h01, tb=tb, sl=sl:
                                     e.matmul(
                                         SP[:, 1024 * sl + 512:
                                            1024 * sl + 1024],
                                         KD[:, h01, 128 * j:128 * (j + 1)],
                                         Qh(h01, 512 * tb, 512),
                                         start=True, stop=True))

                    emit_S(0)
                    if npair > 1:
                        emit_S(1)
                    if prev_ctxb[0] is not None:
                        prev_ctxb[0]()
                    for jp in range(npair):
                        sl = jp % 2
                        j0, j1 = 2 * jp, 2 * jp + 1
                        wE = [(S["t"], tS[jp]), (S["s"], S["s"].n)]
                        if jp >= 2:
                            wE.append((S["t"], tpvp[sl]))
                        se = sop(lambda e, sl=sl: e.activation(
                            PT[:, 2 * sl:2 * sl + 2, :],
                            SP[:, 1024 * sl:1024 * sl + 1024], AF.Exp,
                            scale=0.125), waits=wE)
                        sfree[sl] = se
                        bank_guard[2 * sl] = (S["s"], se)
                        bank_guard[2 * sl + 1] = (S["s"], se)
                        if jp + 2 < npair:
                            emit_S(jp + 2)
                        if j0 >= 4 * tb:
                            moff = 4 + j0 - 4 * tb
                            vm = vop(lambda e, sl=sl, moff=moff: e.tensor_mul(
                                PT[:, 2 * sl:2 * sl + 2, :],
                                PT[:, 2 * sl:2 * sl + 2, :],
                                MASKS[:, moff:moff + 2, :]),
                                waits=[(S["s"], se), (S["v"], S["v"].n)])
                            wPV = [(S["v"], vm)]
                        else:
                            wPV = [(S["s"], se)]
                        top(lambda e, j=j0, h01=h01, sl=sl, jp=jp:
                            e.matmul(
                                PVa[0:65, :],
                                V65[:, 1040 * h01 + 65 * j:
                                    1040 * h01 + 65 * j + 65],
                                PT[:, 2 * sl, :], start=(jp == 0),
                                stop=False),
                            waits=wPV + (bw(4) if jp == 0 else []))
                        tpvp[sl] = top(lambda e, j=j1, h01=h01, sl=sl,
                                       jp=jp, npair=npair: e.matmul(
                                           PVa[0:65, :],
                                           V65[:, 1040 * h01 + 65 * j:
                                               1040 * h01 + 65 * j + 65],
                                           PT[:, 2 * sl + 1, :], start=False,
                                           stop=(jp == npair - 1)))
                    t_pv = S["t"].n

                    def mk_ctxb(h01=h01, t_pv=t_pv):
                        def go():
                            ctx_norm(4, AR2[64 * h01:64 * h01 + 64, 0:512],
                                     t_pv)
                        return go
                    prev_ctxb[0] = mk_ctxb()
                if prev_ctxb[0] is not None:
                    prev_ctxb[0]()
                    prev_ctxb[0] = None
                vboth = S["v"].n
                top(lambda e: e.matmul(
                    PB[5][0:64, :], WSLOT2[0:64, 3712:3776],
                    AR2[0:64, 0:512], start=True, stop=True),
                    waits=bw(5) + [(S["v"], vboth)])
                to2 = top(lambda e: e.matmul(
                    PB[5][64:128, :], WSLOT2[64:128, 3712:3776],
                    AR2[64:128, 0:512], start=True, stop=True))
                vr1 = vop(lambda e, hp=hp, tb=tb: e.tensor_add(
                    Xs(hp, 1 + 512 * tb, 512), Xs(hp, 1 + 512 * tb, 512),
                    PB[5][:, :]),
                    waits=[(S["t"], to2), (S["v"], S["v"].n)])
                bank_guard[5] = (S["v"], vr1)

    def phase_out():
        barrier()
        v_fin = S["v"].n
        dstore = {}
        ssems = [S["sp"], S["spb"], S["dq1"], S["dk1"]]
        slots = [BC1[:, 0:128], BC1[:, 128:256], BC2[:, 0:128],
                 BC2[:, 128:256]]
        for c in range(NCH):
            for tt in range(NT):
                i = c * NT + tt
                k = i % 4
                tw = [(S["v"], v_fin)] + (bw(i % 4) if i < 4 else [])
                if i >= 4:
                    tw.append((ssems[k], dstore[i - 4]))
                tn = top(lambda e, c=c, tt=tt, pb=PB[i % 4]: e.transpose(
                    pb[:, 0:128], Xs(c, 1 + 128 * tt, 128), IDENT[:, :]),
                    waits=tw)
                vv = vop(lambda e, pb=PB[i % 4], k=k: e.tensor_copy(
                    slots[k], pb[:, 0:128]),
                    waits=[(S["t"], tn), (S["v"], S["v"].n)])
                dstore[i] = dma("scalar", ssems[k],
                                out[128 * tt:128 * (tt + 1),
                                    128 * c:128 * (c + 1)],
                                slots[k],
                                waits=[(S["v"], vv)])
        p.do("scalar", None, waits=[(s, s.n) for s in ssems])

    if 1 in phases:
        phase_attn_a()
    if 2 in phases:
        phase_ffn(0)
    if 3 in phases:
        phase_attn_b()
    if 4 in phases:
        phase_ffn(1)
    phase_out()

    block = es.enter_context(nc.Block())
    with es:
        p.realize(block)
    return nc


def _prep_inputs(inputs):
    f32 = np.float32

    def c(a):
        return np.ascontiguousarray(np.asarray(a, dtype=f32))

    shared = {
        "aWqT": c(np.asarray(inputs["L0_attn_a_Wq"]).T),
        "aWkT": c(np.asarray(inputs["L0_attn_a_Wk"]).T),
        "aWvT": c(np.asarray(inputs["L0_attn_a_Wv"]).T),
        "aWoT": c(np.asarray(inputs["L0_attn_a_Wo"]).T),
        "abq_c": c(np.asarray(inputs["L0_attn_a_bq"]).reshape(8, 128).T),
        "abo_c": c(np.asarray(inputs["L0_attn_a_bo"]).reshape(8, 128).T),
        "bDQT": c(np.asarray(inputs["L0_attn_b_W_DQ"]).T),
        "bUQT": c(np.asarray(inputs["L0_attn_b_W_UQ"]).T),
        "bDKVT": c(np.asarray(inputs["L0_attn_b_W_DKV"]).T),
        "bUKVT": c(np.asarray(inputs["L0_attn_b_W_UKV"]).T),
        "bQRT": c(np.asarray(inputs["L0_attn_b_W_QR"]).T),
        "bKRT": c(np.asarray(inputs["L0_attn_b_W_KR"]).T),
        "bWOT": c(np.asarray(inputs["L0_attn_b_W_O"]).T),
        "fwqkv": c(inputs["L0_ffn_wqkv"]),
        "fqs": c(inputs["L0_ffn_q_scale"]),
        "fks": c(inputs["L0_ffn_k_scale"]),
        "fwo": c(inputs["L0_ffn_wo"]),
        "flng_c": c(np.asarray(inputs["L0_ffn_ln_g"]).reshape(2, 8, 128)
                    .transpose(0, 2, 1)),
        "flnb_c": c(np.asarray(inputs["L0_ffn_ln_b"]).reshape(2, 8, 128)
                    .transpose(0, 2, 1)),
        "fw1": c(inputs["L0_ffn_w1"]),
        "fw2": c(inputs["L0_ffn_w2"]),
        "fb2_c": c(np.asarray(inputs["L0_ffn_b2"]).reshape(2, 8, 128)
                   .transpose(0, 2, 1)),
        "ident_in": np.eye(128, dtype=f32),
        "rotc": _host_rotary_tables(),
        "masks_in": _host_masks(),
    }
    bv = np.zeros((128, 1), f32)
    bv[0:64, 0] = np.asarray(inputs["L0_attn_a_bv"], dtype=f32)
    shared["abv_c"] = bv
    qn = np.zeros((128, 1), f32)
    qn[0:64, 0] = np.asarray(inputs["L0_attn_b_q_norm_w"], dtype=f32)
    qn[64:128, 0] = qn[0:64, 0]
    shared["bqnw_c"] = qn
    kn = np.zeros((128, 1), f32)
    kn[0:64, 0] = np.asarray(inputs["L0_attn_b_k_norm_w"], dtype=f32)
    kn[64:128, 0] = kn[0:64, 0]
    shared["bknw_c"] = kn
    b1 = np.asarray(inputs["L0_ffn_b1"], dtype=f32)
    b1c = np.zeros((2, 128, 44), f32)
    for pr in range(NPAIR):
        off, wid = A_OFFS[pr], A_WIDS[pr]
        b1c[:, 0:wid, pr] = b1[:, off:off + wid]
        b1c[:, 0:wid, 22 + pr] = b1[:, 2730 + off:2730 + off + wid]
    shared["fb1_c"] = b1c
    return shared


def kernel(**inputs):
    shared = _prep_inputs(inputs)
    x = np.ascontiguousarray(np.asarray(inputs["L0_attn_a_hidden_states"],
                                        dtype=np.float32))
    nc = build()
    in_maps = [dict(shared, x=x[i]) for i in range(8)]
    res = run_bass_kernel_spmd(nc, in_maps, core_ids=list(range(8)))
    return np.stack([r["out"] for r in res.results], axis=0)


# version 16
# speedup vs baseline: 1.0060x; 1.0060x over previous
import numpy as np
import ml_dtypes
from contextlib import ExitStack

import concourse.bass as bass
import concourse.mybir as mybir
from concourse.bass_utils import run_bass_kernel_spmd

F32 = mybir.dt.float32
BF16 = mybir.dt.bfloat16
AF = mybir.ActivationFunctionType
ALU = mybir.AluOpType

T = 2048
D = 1024
NCH = 8
TG = 2050
NT = 16
NTB = 4
HID = 2730
A_OFFS = [128 * i for i in range(21)] + [2688]
A_WIDS = [128] * 21 + [42]
NPAIR = 22

ENGS = ("sync", "gpsimd", "scalar", "vector", "tensor")


class Sem:
    __slots__ = ("h", "n", "name")

    def __init__(self, h, name):
        self.h, self.n, self.name = h, 0, name


class Prog:
    def __init__(self):
        self.q = {e: [] for e in ENGS}

    def do(self, eng, emit, waits=(), inc=None, incn=1):
        ws = []
        for (s, n) in waits:
            if n <= 0:
                continue
            assert n <= s.n, f"wait {s.name}>={n} but only {s.n} emitted"
            ws.append((s, n))
        self.q[eng].append((tuple(ws), emit, inc, incn))
        if inc is not None:
            inc.n += incn
            return inc.n
        return None

    def realize(self, block):
        def mk(engname):
            def body(eng):
                for waits, emit, inc, incn in self.q[engname]:
                    for (s, n) in waits:
                        eng.wait_ge(s.h, n)
                    if emit is not None:
                        inst = emit(eng)
                        if inc is not None:
                            inst.then_inc(inc.h, incn)
            return body
        for e in ENGS:
            getattr(block, e)(mk(e))


def _host_rotary_tables():
    n, dh, win = T, 64, 512
    half = np.arange(0, dh, 2, dtype=np.float64)
    inv_freq = 1.0 / 10000.0 ** (half / dh)
    t = np.arange(n, dtype=np.float64)
    freqs = t[:, None] * inv_freq[None, :]
    freqs = np.concatenate([freqs, freqs], axis=-1)
    base = (half + 0.4 * dh) / (1.4 * dh)
    power = (t - n // 2) / float(win // 2)
    scale = base[None, :] ** power[:, None]
    scale = np.concatenate([scale, scale], axis=-1)
    cos, sin = np.cos(freqs), np.sin(freqs)
    sgn = np.concatenate([-np.ones(32), np.ones(32)])[None, :]
    return np.stack([cos * scale, sgn * sin * scale,
                     cos / scale, sgn * sin / scale]).astype(np.float32)


def _host_masks():
    m = np.zeros((8, 128, 512), dtype=np.float32)
    sl = np.arange(128)[:, None]
    tr = np.arange(512)[None, :]
    for r in range(4):
        m[r] = (tr <= 128 * r + sl).astype(np.float32)
    for r in range(4, 8):
        m[r] = (tr >= 128 * (r - 4) + sl).astype(np.float32)
    return m


def build(phases=(1, 2, 3, 4)):
    nc = bass.Bass("TRN2", target_bir_lowering=False)

    def din(name, shape):
        return nc.dram_tensor(name, list(shape), F32, kind="ExternalInput").ap()

    x_in = din("x", (T, D))
    aWqT = din("aWqT", (D, D))
    aWkT = din("aWkT", (D, 64))
    aWvT = din("aWvT", (D, 64))
    aWoT = din("aWoT", (D, D))
    abq_c = din("abq_c", (128, 8))
    abv_c = din("abv_c", (128, 1))
    abo_c = din("abo_c", (128, 8))
    bDQT = din("bDQT", (64, 512))
    bUQT = din("bUQT", (512, 64))
    bDKVT = din("bDKVT", (64, 1024))
    bUKVT = din("bUKVT", (1024, 128))
    bQRT = din("bQRT", (64, 64))
    bKRT = din("bKRT", (64, 64))
    bWOT = din("bWOT", (64, 64))
    bqnw_c = din("bqnw_c", (128, 1))
    bknw_c = din("bknw_c", (128, 1))
    fwqkv = din("fwqkv", (2, D, 1536))
    fqs = din("fqs", (2, 64))
    fks = din("fks", (2, 64))
    fwo = din("fwo", (2, 512, D))
    flng_c = din("flng_c", (2, 128, 8))
    flnb_c = din("flnb_c", (2, 128, 8))
    fw1 = din("fw1", (2, D, 5460))
    fb1_c = din("fb1_c", (2, 128, 44))
    fw2 = din("fw2", (2, HID, D))
    fb2_c = din("fb2_c", (2, 128, 8))
    ident_in = din("ident_in", (128, 128))
    rotc = din("rotc", (4, T, 64))
    masks_in = din("masks_in", (8, 128, 512))

    out = nc.dram_tensor("out", [T, D], F32, kind="ExternalOutput").ap()
    xpark = nc.dram_tensor("xpark", [NCH, 128, TG], F32, kind="Internal").ap()
    w1C = nc.dram_tensor("w1C", [2, NPAIR, 2, 128, NCH, 128], BF16,
                         kind="Internal").ap()
    wqkvC = nc.dram_tensor("wqkvC", [2, D, 1536], BF16, kind="Internal").ap()
    woC = nc.dram_tensor("woC", [2, 512, D], BF16, kind="Internal").ap()
    w2C = nc.dram_tensor("w2C", [2, HID, D], BF16, kind="Internal").ap()

    es = ExitStack()

    def sb(name, shape, dt=F32):
        return es.enter_context(nc.sbuf_tensor(name, list(shape), dt))

    X = sb("X", [128, NCH, TG], F32)
    B = sb("B", [128, NCH * TG], BF16)
    AR1 = sb("AR1", [128, 8192], BF16)
    AR2 = sb("AR2", [128, 4096], BF16)
    V65 = sb("V65", [128, 4160], BF16)
    WSLOT = sb("WSLOT", [128, NCH, 1536], BF16)
    WSLOT2 = sb("WSLOT2", [128, 4096], BF16)
    ROT = sb("ROT", [128, 4, NT, 64], BF16)
    MASKS = sb("MASKS", [128, 8, 512], BF16)
    KD = sb("KD", [128, 2, T], BF16)
    PT = sb("PT", [128, 4, 512], BF16)
    IDENT = sb("IDENT", [128, 128], F32)
    IDENTB = sb("IDENTB", [128, 128], BF16)
    BC1 = sb("BC1", [128, 512], F32)
    BC2 = sb("BC2", [128, 512], F32)
    TMP32 = sb("TMP32", [128, 512], F32)
    TMPA = sb("TMPA", [128, 512], BF16)
    TMPB = sb("TMPB", [128, 512], BF16)
    TMPC = sb("TMPC", [128, 512], BF16)
    TMPE = sb("TMPE", [128, 512], BF16)
    NRM3 = sb("NRM3", [128, 8], F32)
    NRM4 = sb("NRM4", [128, 8], F32)
    ROWS = sb("ROWS", [1, 1152], F32)
    STAT3 = sb("STAT3", [128, 3], BF16)
    HALF2 = sb("HALF2", [128, 2], BF16)
    ONESC = sb("ONESC", [1, 128], F32)
    ONESCB = sb("ONESCB", [1, 128], BF16)
    RECB = sb("RECB", [1, 1024], BF16)
    BIASC = sb("BIASC", [128, 20], F32)
    LGC = sb("LGC", [128, 8], F32)
    LBC = sb("LBC", [128, 8], F32)
    B2C = sb("B2C", [128, 8], F32)
    B1C = sb("B1C", [128, 44], F32)
    QSB = sb("QSB", [128, 2, 64], F32)
    QSBF = QSB[:, :, :].rearrange("p a b -> p (a b)")
    NRM = sb("NRM", [128, 8], F32)
    NRM2 = sb("NRM2", [128, 8], F32)
    ZSAVE = sb("ZSAVE", [128, NCH, 3], F32)
    EPSC = sb("EPSC", [4, 2], F32)

    SP = es.enter_context(nc.psum_tensor("SP", [128, 2048], F32))
    PVa = es.enter_context(nc.psum_tensor("PVa", [128, 512], F32))
    PVb = es.enter_context(nc.psum_tensor("PVb", [128, 512], F32))
    PBF = es.enter_context(nc.psum_tensor("PBF", [128, 1024], BF16))
    PBX = es.enter_context(nc.psum_tensor("PBX", [128, 512], F32))

    class PView:
        __slots__ = ("th", "c0")

        def __init__(self, th, c0):
            self.th, self.c0 = th, c0

        def __getitem__(self, idx):
            ps, fs = idx
            a = fs.start or 0
            b = fs.stop if fs.stop is not None else 512
            return self.th[ps, self.c0 + a:self.c0 + b]

    PB = [PView(SP, 0), PView(SP, 512), PView(SP, 1024), PView(SP, 1536),
          PView(PVa, 0), PView(PVb, 0)]

    sems = {}
    for name in ("t", "v", "s", "g0", "g1", "d0", "d1", "d2", "d3", "sp",
                 "spb", "dx", "w0a", "w0b", "w1a", "w1b", "w2a", "w2b",
                 "gc", "dq1", "dk1"):
        sems[name] = Sem(es.enter_context(nc.semaphore(name)), name)
    S = sems
    p = Prog()

    def dma(queue, sem, outap, inap, waits=()):
        p.do(queue, lambda e, o=outap, i=inap: e.dma_start(out=o, in_=i),
             waits=waits, inc=sem, incn=16)
        return sem.n

    def vop(emit, waits=(), incn=1):
        return p.do("vector", emit, waits=waits, inc=S["v"], incn=incn)

    def sop(emit, waits=(), incn=1):
        return p.do("scalar", emit, waits=waits, inc=S["s"], incn=incn)

    def top(emit, waits=(), incn=1):
        return p.do("tensor", emit, waits=waits, inc=S["t"], incn=incn)

    def gop(emit, waits=(), incn=1):
        return p.do("gpsimd", emit, waits=waits, inc=S["gc"], incn=incn)

    def Xs(c, lo, n):
        return X[:, c, lo:lo + n]

    def Bs(c, lo, n, p0=0, pn=128):
        return B[p0:p0 + pn, c * TG + lo: c * TG + lo + n]

    bank_guard = [None] * 6
    pbf_guard = [None]
    pbx_guard = [None]

    def bw(i):
        g = bank_guard[i]
        return [g] if g else []

    def pbfw():
        g = pbf_guard[0]
        return [g] if g else []

    def pbxw():
        g = pbx_guard[0]
        return [g] if g else []

    def barrier():
        fence = [(sm, sm.n) for sm in S.values()]
        for e in ENGS:
            for (s, n) in fence:
                p.do(e, None, waits=[(s, n)])
        return fence

    dma("sync", S["d0"], IDENT[:, :], ident_in[:, :])
    dma("gpsimd", S["g0"], IDENTB[:, :], ident_in[:, :])
    for r in range(8):
        dma("gpsimd", S["g0"], MASKS[:, r, :], masks_in[r, :, :])
    g_const = S["g0"].n

    vop(lambda e: e.memset(ONESC[:, :], 1.0))
    vop(lambda e: e.memset(ONESCB[:, :], 1.0), waits=[(S["v"], S["v"].n)])
    vop(lambda e: e.memset(STAT3[:, 0:1], 1.0 / 1024.0),
        waits=[(S["v"], S["v"].n)])
    vop(lambda e: e.memset(STAT3[:, 1:2], 0.0),
        waits=[(S["v"], S["v"].n)])
    vop(lambda e: e.memset(STAT3[:, 2:3], 1.0 / 1024.0),
        waits=[(S["v"], S["v"].n)])
    vop(lambda e: e.memset(HALF2[:, :], 0.0),
        waits=[(S["v"], S["v"].n)])
    vop(lambda e: e.memset(HALF2[0:64, 0:1], 1.0),
        waits=[(S["v"], S["v"].n)])
    vop(lambda e: e.memset(HALF2[64:128, 1:2], 1.0),
        waits=[(S["v"], S["v"].n)])
    vop(lambda e: e.memset(EPSC[:, 0:1], 1e-5),
        waits=[(S["v"], S["v"].n)])
    vop(lambda e: e.memset(EPSC[:, 1:2], 1.1920929e-07),
        waits=[(S["v"], S["v"].n)])
    vop(lambda e: e.memset(X[:, :, 0:1], 0.0),
        waits=[(S["v"], S["v"].n)])
    v_guard = vop(lambda e: e.memset(X[:, :, 2049:2050], 0.0),
                  waits=[(S["v"], S["v"].n)])

    dma("sync", S["d0"], BIASC[:, 0:8], abq_c[:, :])
    dma("sync", S["d0"], BIASC[:, 8:16], abo_c[:, :])
    dma("sync", S["d0"], BIASC[:, 16:17], abv_c[:, :])
    dma("sync", S["d0"], BIASC[:, 17:18], bqnw_c[:, :])
    d_bias = dma("sync", S["d0"], BIASC[:, 18:19], bknw_c[:, :])

    t_x0 = S["t"].n
    v_x0 = S["v"].n
    xd = [S["dx"], S["d1"], S["d2"], S["d3"]]
    xslot = [BC1[:, 0:128], BC1[:, 128:256], BC2[:, 0:128], BC2[:, 128:256]]
    dcnt = [0, 0, 0, 0]
    for c in range(NCH):
        for tt in range(NT):
            i = c * NT + tt
            k = i % 4
            waits = [(S["t"], t_x0 + i - 3)] if i >= 4 else []
            dcnt[k] = dma("sync", xd[k], xslot[k],
                          x_in[128 * tt:128 * (tt + 1), 128 * c:128 * (c + 1)],
                          waits=waits)
            tn = top(lambda e, s=xslot[k], pb=PB[i % 2]: e.transpose(
                pb[:, 0:128], s, IDENT[:, :]),
                waits=[(xd[k], dcnt[k]), (S["v"], v_guard),
                       (S["v"], v_x0 + i - 1)] +
                ([(S["d0"], d_bias)] if i == 0 else []))
            vop(lambda e, pb=PB[i % 2], c=c, tt=tt: e.tensor_copy(
                Xs(c, 1 + 128 * tt, 128), pb[:, 0:128]),
                waits=[(S["t"], tn)])
    v_xload = S["v"].n
    bank_guard[0] = (S["v"], v_xload)
    bank_guard[1] = (S["v"], v_xload)

    dma("gpsimd", S["g0"], WSLOT[:, :, 1024:1088],
        aWkT[:, :].rearrange("(c p) o -> p c o", p=128))
    dma("gpsimd", S["g0"], WSLOT[:, :, 1088:1152],
        aWvT[:, :].rearrange("(c p) o -> p c o", p=128))
    g_awq = dma("gpsimd", S["g0"], WSLOT[:, :, 0:1024],
                aWqT[:, :].rearrange("(c p) o -> p c o", p=128))

    for l in range(2):
        for c in range(NCH):
            dma("gpsimd", S["g1"], wqkvC[l, 128 * c:128 * (c + 1), :],
                fwqkv[l, 128 * c:128 * (c + 1), :])
        for c in range(4):
            dma("gpsimd", S["g1"], woC[l, 128 * c:128 * (c + 1), :],
                fwo[l, 128 * c:128 * (c + 1), :])
        for pr in range(NPAIR):
            off, wid = A_OFFS[pr], A_WIDS[pr]
            for half in range(2):
                hoff = 2730 * half
                dma("gpsimd", S["g1"],
                    bass.AP(w1C.tensor,
                            (((l * NPAIR + pr) * 2 + half) * 128 * NCH * 128),
                            [[NCH * 128, 128], [128, NCH], [1, wid]]),
                    bass.AP(fw1.tensor, l * D * 5460 + hoff + off,
                            [[5460, 128], [128 * 5460, NCH], [1, wid]]))
        for c in range(22):
            w = min(128, HID - 128 * c)
            dma("gpsimd", S["g1"], w2C[l, 128 * c:128 * c + w, :],
                fw2[l, 128 * c:128 * c + w, :])
    g_wcache = S["g1"].n

    bc_guard = [None]

    def emit_stats_block(get_src, eps, dep_waits):
        RM = ROWS[:, 0:512]
        RV = ROWS[:, 512:1024]
        RI = ROWS[:, 0:512]
        sq_t = [0, 0]
        for c in range(NCH):
            src = get_src(c)
            sq = TMPA[:, 0:512] if c % 2 == 0 else TMPB[:, 0:512]
            w_sq = list(dep_waits) + [(S["s"], S["s"].n)]
            if c > 1:
                w_sq.append((S["t"], sq_t[c % 2]))
            scnt = sop(lambda e, s=src, sq=sq: e.activation(sq, s,
                                                            AF.Square),
                       waits=w_sq)
            top(lambda e, s=src, st=(c == 0), sp=(c == NCH - 1): e.matmul(
                PB[4][0:1, 0:512], STAT3[:, 0:1], s, start=st, stop=sp),
                waits=(bw(4) + bw(5) + list(dep_waits)) if c == 0 else [])
            sq_t[c % 2] = top(lambda e, sq=sq, st=(c == 0),
                              sp=(c == NCH - 1):
                              e.matmul(PB[5][0:1, 0:512], STAT3[:, 0:1], sq,
                                       start=st, stop=sp),
                              waits=[(S["s"], scnt)])
        last_sq_t = sq_t[(NCH - 1) % 2]
        v1 = vop(lambda e: e.tensor_copy(RM, PB[4][0:1, 0:512]),
                 waits=[(S["t"], last_sq_t), (S["v"], S["v"].n)])
        v1a = vop(lambda e: e.tensor_copy(RV, PB[5][0:1, 0:512]),
                  waits=[(S["v"], v1)])
        bank_guard[4] = (S["v"], v1)
        bank_guard[5] = (S["v"], v1a)
        t1 = top(lambda e: e.matmul(PBX[:, :], ONESC[:, :], RM,
                                    start=True, stop=True),
                 waits=pbxw() + [(S["v"], v1)])
        w_bc = [bc_guard[0]] if bc_guard[0] else []
        bc_guard[0] = None
        v4 = vop(lambda e: e.tensor_copy(BC2[:, :], PBX[:, :]),
                 waits=[(S["t"], t1), (S["v"], S["v"].n)] + w_bc)
        v1b = vop(lambda e: e.tensor_mul(RI, RM, RM),
                  waits=[(S["v"], v1a), (S["t"], t1)])
        v2 = vop(lambda e: e.tensor_sub(RV, RV, RI),
                 waits=[(S["v"], v1b)])
        s2 = sop(lambda e: e.activation(RV, RV, AF.Ln,
                                        bias=EPSC[0:1, 0:1]),
                 waits=[(S["v"], v2), (S["s"], S["s"].n)])
        s3 = sop(lambda e: e.activation(RI, RV, AF.Exp, scale=-0.5),
                 waits=[(S["s"], s2)])
        t2 = top(lambda e: e.matmul(PBX[:, :], ONESC[:, :], RI,
                                    start=True, stop=True),
                 waits=[(S["v"], v4), (S["s"], s3)])
        v5 = vop(lambda e: e.tensor_copy(BC1[:, :], PBX[:, :]),
                 waits=[(S["t"], t2), (S["v"], S["v"].n)])
        pbx_guard[0] = (S["v"], v5)
        return v5

    def rowbc(row_ap, dst, width, extra_waits=()):
        t1 = top(lambda e, r=row_ap, w=width: e.matmul(
            PBX[:, 0:w], ONESC[:, :], r, start=True, stop=True),
            waits=pbxw() + list(extra_waits))
        v1 = vop(lambda e, d=dst, w=width: e.tensor_copy(d, PBX[:, 0:w]),
                 waits=[(S["t"], t1), (S["v"], S["v"].n)])
        pbx_guard[0] = (S["v"], v1)
        return v1

    def ln_materialize(dst_fn, v_bc, lo):
        for c in range(NCH):
            v1 = vop(lambda e, c=c, lo=lo: e.tensor_sub(
                TMP32[:, :], Xs(c, lo, 512), BC2[:, :]),
                waits=[(S["v"], v_bc if c == 0 else S["v"].n)])
            vop(lambda e, c=c: e.tensor_mul(dst_fn(c), TMP32[:, :],
                                            BC1[:, :]),
                waits=[(S["v"], v1)])
        return S["v"].n

    def entry_ln(mat_dst, extra_mat_waits=None, gp=False, inplace=False):
        v_snap = [0] * NTB
        g_snap = [0] * NTB
        v_mat = [0] * NTB
        g_mat = [0] * NTB
        vchunks = range(0, 4 if gp else NCH)
        gchunks = range(4, NCH) if gp else []

        def snap(tb):
            lo = 1 + 512 * tb
            for c in vchunks:
                v_snap[tb] = vop(lambda e, c=c, lo=lo: e.tensor_copy(
                    Bs(c, lo, 512), Xs(c, lo, 512)),
                    waits=[(S["v"], S["v"].n)])
            for c in gchunks:
                g_snap[tb] = gop(lambda e, c=c, lo=lo: e.tensor_copy(
                    Bs(c, lo, 512), Xs(c, lo, 512)),
                    waits=[(S["gc"], S["gc"].n)])

        def stats(tb):
            lo = 1 + 512 * tb
            deps = [(S["v"], v_snap[tb])]
            if gp:
                deps.append((S["gc"], g_snap[tb]))
            return emit_stats_block(lambda c, lo=lo: Bs(c, lo, 512), 1e-5,
                                    deps)

        def mat(tb, v_bc):
            lo = 1 + 512 * tb
            w_extra = extra_mat_waits(tb) if extra_mat_waits else []
            for c in vchunks:
                if inplace:
                    v1 = vop(lambda e, c=c, lo=lo: e.tensor_sub(
                        Xs(c, lo, 512), Xs(c, lo, 512), BC2[:, :]),
                        waits=[(S["v"], v_bc if c == 0 else S["v"].n)] +
                        (w_extra if c == 0 else []))
                    v_mat[tb] = vop(lambda e, c=c, lo=lo: e.tensor_mul(
                        Xs(c, lo, 512), Xs(c, lo, 512), BC1[:, :]),
                        waits=[(S["v"], v1)])
                else:
                    v1 = vop(lambda e, c=c, lo=lo: e.tensor_sub(
                        TMP32[:, :], Xs(c, lo, 512), BC2[:, :]),
                        waits=[(S["v"], v_bc if c == 0 else S["v"].n)] +
                        (w_extra if c == 0 else []))
                    v_mat[tb] = vop(lambda e, c=c, lo=lo: e.tensor_mul(
                        mat_dst(c, lo), TMP32[:, :], BC1[:, :]),
                        waits=[(S["v"], v1)])
            for c in gchunks:
                if inplace:
                    g1 = gop(lambda e, c=c, lo=lo: e.tensor_sub(
                        Xs(c, lo, 512), Xs(c, lo, 512), BC2[:, :]),
                        waits=[(S["gc"], S["gc"].n), (S["v"], v_bc)] +
                        (w_extra if c == 4 else []))
                    g_mat[tb] = gop(lambda e, c=c, lo=lo: e.tensor_mul(
                        Xs(c, lo, 512), Xs(c, lo, 512), BC1[:, :]),
                        waits=[(S["gc"], g1)])
                    continue
                for hh in range(4):
                    o = 128 * hh
                    g1 = gop(lambda e, c=c, lo=lo, o=o: e.tensor_sub(
                        QSBF[:, :], Xs(c, lo + o, 128), BC2[:, o:o + 128]),
                        waits=[(S["gc"], S["gc"].n), (S["v"], v_bc)] +
                        (w_extra if c == 4 and hh == 0 else []))
                    g_mat[tb] = gop(
                        lambda e, c=c, lo=lo, o=o: e.tensor_mul(
                            mat_dst(c, lo)[:, o:o + 128],
                            QSBF[:, :], BC1[:, o:o + 128]),
                        waits=[(S["gc"], g1)])

        snap(0)
        vbc = stats(0)
        for tb in range(NTB):
            if tb + 1 < NTB:
                snap(tb + 1)
            mat(tb, vbc)
            if tb + 1 < NTB:
                if gp:
                    bc_guard[0] = (S["gc"], g_mat[tb])
                vbc = stats(tb + 1)
        return v_mat, g_mat

    def ctx_norm(pv_bank, dst_ap, t_pv):
        s1 = sop(lambda e, b=pv_bank: e.activation(
            ROWS[:, 0:512], PB[b][64:65, :], AF.Ln),
            waits=[(S["t"], t_pv), (S["s"], S["s"].n)])
        s2 = sop(lambda e: e.activation(RECB[:, 0:512], ROWS[:, 0:512],
                                        AF.Exp, scale=-1.0),
                 waits=[(S["s"], s1)])
        tb2 = top(lambda e: e.matmul(PBX[0:64, :], ONESCB[0:1, 0:64],
                                     RECB[:, 0:512], start=True, stop=True),
                  waits=pbxw() + [(S["s"], s2)])
        s3 = sop(lambda e: e.copy(TMP32[0:64, :], PBX[0:64, :]),
                 waits=[(S["t"], tb2), (S["s"], S["s"].n),
                        (S["v"], S["v"].n)])
        pbx_guard[0] = (S["s"], s3)
        vc = vop(lambda e, d=dst_ap, b=pv_bank: e.tensor_mul(
            d, PB[b][0:64, :], TMP32[0:64, :]),
            waits=[(S["s"], s3), (S["v"], S["v"].n)])
        bank_guard[pv_bank] = (S["v"], vc)
        return vc

    def ctx_norm2(dstA, dstB, t_pv):
        s1 = sop(lambda e: e.activation(ROWS[:, 0:512], PVa[64:65, :],
                                        AF.Ln),
                 waits=[(S["t"], t_pv), (S["s"], S["s"].n)])
        s1 = sop(lambda e: e.activation(ROWS[:, 512:1024], PVb[64:65, :],
                                        AF.Ln),
                 waits=[(S["s"], s1)])
        s2 = sop(lambda e: e.activation(RECB[:, :], ROWS[:, 0:1024],
                                        AF.Exp, scale=-1.0),
                 waits=[(S["s"], s1)])
        tb2 = top(lambda e: e.matmul(PBX[0:64, :], ONESCB[0:1, 0:64],
                                     RECB[:, 0:512], start=True, stop=True),
                  waits=pbxw() + [(S["s"], s2)])
        tb2 = top(lambda e: e.matmul(PBX[64:128, :], ONESCB[0:1, 64:128],
                                     RECB[:, 512:1024], start=True,
                                     stop=True))
        s3 = sop(lambda e: e.copy(TMP32[:, :], PBX[:, :]),
                 waits=[(S["t"], tb2), (S["s"], S["s"].n),
                        (S["v"], S["v"].n)])
        pbx_guard[0] = (S["s"], s3)
        vcA = vop(lambda e, d=dstA: e.tensor_mul(d, PVa[0:64, :],
                                                 TMP32[0:64, :]),
                  waits=[(S["s"], s3), (S["v"], S["v"].n)])
        vcB = vop(lambda e, d=dstB: e.tensor_mul(d, PVb[0:64, :],
                                                 TMP32[64:128, :]),
                  waits=[(S["v"], vcA)])
        bank_guard[4] = (S["v"], vcB)
        bank_guard[5] = (S["v"], vcB)
        return vcB

    def phase_attn_a():
        barrier()
        v_mat, _ = entry_ln(lambda c, lo: Bs(c, lo, 512))
        v_y1 = v_mat[NTB - 1]

        g_wq = g_awq

        for tb in range(NTB):
            lo = 1 + 512 * tb
            for c in range(NCH):
                top(lambda e, c=c, lo=lo: e.matmul(
                    PB[0][0:64, :], WSLOT[:, c, 1024:1088], Bs(c, lo, 512),
                    start=(c == 0), stop=(c == NCH - 1)),
                    waits=(bw(0) + [(S["g0"], g_wq), (S["v"], v_mat[tb])])
                    if c == 0 else [])
            tk = S["t"].n
            vk = vop(lambda e, tb=tb: e.tensor_copy(
                KD[0:64, 0, 512 * tb:512 * (tb + 1)], PB[0][0:64, :]),
                waits=[(S["t"], tk), (S["v"], S["v"].n)])
            bank_guard[0] = (S["v"], vk)
            for c in range(NCH):
                top(lambda e, c=c, lo=lo: e.matmul(
                    PB[1][0:64, :], WSLOT[:, c, 1088:1152], Bs(c, lo, 512),
                    start=(c == 0), stop=(c == NCH - 1)),
                    waits=(bw(1) + [(S["v"], v_mat[tb])]) if c == 0 else [])
            tv = S["t"].n
            vv = vop(lambda e: e.tensor_scalar_add(
                TMPB[0:64, 0:512], PB[1][0:64, :], BIASC[0:64, 16:17]),
                waits=[(S["t"], tv), (S["g0"], g_wq),
                       (S["v"], S["v"].n)])
            bank_guard[1] = (S["v"], vv)
            for q in range(4):
                tt = 4 * tb + q
                tq = top(lambda e, q=q: e.transpose(
                    PBF[:, 0:64], TMPB[0:64, 128 * q:128 * (q + 1)],
                    IDENTB[0:64, 0:64]),
                    waits=[(S["v"], vv)] + pbfw())
                vq = vop(lambda e, tt=tt: e.tensor_copy(
                    V65[:, 65 * tt:65 * tt + 64], PBF[:, 0:64]),
                    waits=[(S["t"], tq), (S["v"], S["v"].n)])
                pbf_guard[0] = (S["v"], vq)
        v_ones = vop(lambda e: e.memset(
            bass.AP(V65, 64, [[4160, 128], [65, NT], [1, 1]]), 1.0),
            waits=[(S["v"], S["v"].n)])

        for tb in range(NTB):
            lo = 1 + 512 * tb
            stage_v = []
            for ot in range(NCH):
                bank = 2 + ot % 2
                for c in range(NCH):
                    top(lambda e, c=c, ot=ot, lo=lo, bank=bank: e.matmul(
                        PB[bank][:, :], WSLOT[:, c, 128 * ot:128 * (ot + 1)],
                        Bs(c, lo, 512), start=(c == 0), stop=(c == NCH - 1)),
                        waits=(bw(bank) + [(S["g0"], g_wq),
                                           (S["d0"], d_bias)])
                        if c == 0 else [])
                tq = S["t"].n
                vq = vop(lambda e, ot=ot, bank=bank: e.tensor_scalar(
                    AR2[:, 512 * ot:512 * (ot + 1)], PB[bank][:, :],
                    BIASC[:, ot:ot + 1], 0.125, ALU.add, ALU.mult),
                    waits=[(S["t"], tq), (S["v"], S["v"].n)])
                bank_guard[bank] = (S["v"], vq)
                stage_v.append(vq)
            for ot in range(NCH):
                vop(lambda e, ot=ot, lo=lo: e.tensor_copy(
                    Bs(ot, lo, 512), AR2[:, 512 * ot:512 * (ot + 1)]),
                    waits=[(S["v"], stage_v[ot]), (S["v"], S["v"].n)])
        v_qdone = S["v"].n
        d_kdup = dma("sync", S["d0"], KD[64:128, 0, 0:T], KD[0:64, 0, 0:T],
                     waits=[(S["v"], v_qdone)])
        g_wo = dma("gpsimd", S["g0"], WSLOT[:, :, 0:1024],
                   aWoT[:, :].rearrange("(c p) o -> p c o", p=128),
                   waits=[(S["v"], v_qdone)])

        for tb in range(NTB):
            lo = 1 + 512 * tb
            prev_ctx = [None]
            for hp in range(NCH):
                sexp = [0, 0]
                tpv = [0, 0]
                tS = [0] * NT

                def emit_S(j, hp=hp, lo=lo):
                    sl = j % 2
                    w0 = []
                    if j < 2:
                        w0 += bw(2 * sl) + bw(2 * sl + 1)
                    else:
                        w0 += [(S["s"], sexp[sl])]
                    if j == 0 and hp == 0 and tb == 0:
                        w0 += [(S["d0"], d_kdup), (S["v"], v_ones)]
                    top(lambda e, j=j, hp=hp, lo=lo, sl=sl: e.matmul(
                        SP[:, 1024 * sl:1024 * sl + 512],
                        KD[0:64, 0, 128 * j:128 * (j + 1)],
                        Bs(hp, lo, 512, 0, 64), start=True, stop=True),
                        waits=w0)
                    tS[j] = top(lambda e, j=j, hp=hp, lo=lo, sl=sl: e.matmul(
                        SP[:, 1024 * sl + 512:1024 * sl + 1024],
                        KD[64:128, 0, 128 * j:128 * (j + 1)],
                        Bs(hp, lo, 512, 64, 64), start=True, stop=True))

                emit_S(0)
                emit_S(1)
                if prev_ctx[0] is not None:
                    prev_ctx[0]()
                for j in range(NT):
                    sl = j % 2
                    wE = [(S["t"], tS[j]), (S["s"], S["s"].n)]
                    if j >= 2:
                        wE.append((S["t"], tpv[sl]))
                    se = sop(lambda e, sl=sl: e.activation(
                        PT[:, 2 * sl:2 * sl + 2, :],
                        SP[:, 1024 * sl:1024 * sl + 1024], AF.Exp),
                        waits=wE)
                    sexp[sl] = se
                    bank_guard[2 * sl] = (S["s"], se)
                    bank_guard[2 * sl + 1] = (S["s"], se)
                    if j + 2 < NT:
                        emit_S(j + 2)
                    top(lambda e, j=j, sl=sl: e.matmul(
                        PVa[0:65, :], V65[:, 65 * j:65 * j + 65],
                        PT[:, 2 * sl, :], start=(j == 0),
                        stop=(j == NT - 1)),
                        waits=[(S["s"], se)] + (bw(4) if j == 0 else []))
                    tpv[sl] = top(lambda e, j=j, sl=sl: e.matmul(
                        PVb[0:65, :], V65[:, 65 * j:65 * j + 65],
                        PT[:, 2 * sl + 1, :], start=(j == 0),
                        stop=(j == NT - 1)),
                        waits=(bw(5) if j == 0 else []))
                t_pv = S["t"].n

                def mk_ctx(hp=hp, t_pv=t_pv):
                    def go():
                        ctx_norm2(AR2[0:64, 512 * hp:512 * hp + 512],
                                  AR2[64:128, 512 * hp:512 * hp + 512],
                                  t_pv)
                    return go
                prev_ctx[0] = mk_ctx()
            prev_ctx[0]()
            v_ctx = S["v"].n
            for ot in range(NCH):
                bank = ot % 2
                for c in range(NCH):
                    top(lambda e, c=c, ot=ot, bank=bank: e.matmul(
                        PB[bank][:, :], WSLOT[:, c, 128 * ot:128 * (ot + 1)],
                        AR2[:, 512 * c:512 * (c + 1)], start=(c == 0),
                        stop=(c == NCH - 1)),
                        waits=(bw(bank) + [(S["v"], v_ctx),
                                           (S["g0"], g_wo)])
                        if c == 0 else [])
                tn = S["t"].n
                vr = vop(lambda e, ot=ot, lo=lo, bank=bank:
                         e.scalar_tensor_tensor(
                             Xs(ot, lo, 512), PB[bank][:, :],
                             BIASC[:, 8 + ot:9 + ot], Xs(ot, lo, 512),
                             ALU.add, ALU.add),
                         waits=[(S["t"], tn), (S["v"], S["v"].n)])
                bank_guard[bank] = (S["v"], vr)

    def phase_ffn(app):
        barrier()
        pt_guard = [0]
        psem = [S["d0"], S["d1"], S["d2"], S["d3"]]
        d_park = [0] * NTB
        for tb in range(NTB):
            lo = 1 + 512 * tb
            for c in range(NCH):
                d_park[tb] = dma("scalar", psem[tb],
                                 xpark[c, :, lo:lo + 512],
                                 Xs(c, lo, 512),
                                 waits=[(S["v"], S["v"].n)] if (tb == 0 and
                                                               c == 0) else [])
        v_mat, g_mat = entry_ln(
            lambda c, lo: Xs(c, lo, 512),
            extra_mat_waits=lambda tb: [(psem[tb], d_park[tb])], gp=True,
            inplace=True)
        v_z0 = v_mat[NTB - 1]
        g_z0 = g_mat[NTB - 1]

        def layer_loads(l):
            dma("sync", S["d1"], ROWS[:, 1024:1088], fqs[l, :][None, :],
                waits=[(S["s"], S["s"].n), (S["t"], S["t"].n)])
            d_qs = dma("sync", S["d1"], ROWS[:, 1088:1152],
                       fks[l, :][None, :])
            rowbc(ROWS[:, 1024:1088], QSB[:, 0, :], 64, [(S["d1"], d_qs)])
            v_qs = rowbc(ROWS[:, 1088:1152], QSB[:, 1, :], 64)
            g_rot = 0
            gate = [(S["v"], S["v"].n), (S["t"], S["t"].n)]
            for tbl in range(4):
                g_rot = dma("gpsimd", S["g0"], ROT[:, tbl, :, :],
                            rotc[tbl, :, :]
                            .rearrange("(t p) d -> p t d", p=128),
                            waits=gate)
                gate = ()
            g_wqkv = dma("sync", S["d2"], WSLOT[:, :, :],
                         wqkvC[l, :, :].rearrange("(c p) o -> p c o", p=128),
                         waits=[(S["g1"], g_wcache), (S["t"], S["t"].n)])
            g_wo = dma("sync", S["d2"],
                       WSLOT2[:, :].rearrange("p (c o) -> p c o", c=4),
                       woC[l, :, :].rearrange("(c p) o -> p c o", p=128))
            return g_wqkv, g_wo, g_rot, v_qs

        def layer_scales(g_rot, v_qs):
            v_tbl = 0
            for tbl in range(4):
                qk = 0 if tbl < 2 else 1
                for tt in range(NT):
                    if tbl % 2 == 0:
                        v_tbl = vop(lambda e, tbl=tbl, tt=tt, qk=qk:
                                    e.tensor_mul(ROT[:, tbl, tt, :],
                                                 ROT[:, tbl, tt, :],
                                                 QSB[:, qk, :]),
                                    waits=[(S["g0"], g_rot), (S["v"], v_qs),
                                           (S["v"], S["v"].n)])
                    else:
                        v_tbl = vop(lambda e, tbl=tbl, tt=tt, qk=qk:
                                    e.tensor_mul(ROT[:, tbl, tt, 0:32],
                                                 ROT[:, tbl, tt, 0:32],
                                                 QSB[:, qk, 32:64]),
                                    waits=[(S["g0"], g_rot), (S["v"], v_qs),
                                           (S["v"], S["v"].n)])
                        v_tbl = vop(lambda e, tbl=tbl, tt=tt, qk=qk:
                                    e.tensor_mul(ROT[:, tbl, tt, 32:64],
                                                 ROT[:, tbl, tt, 32:64],
                                                 QSB[:, qk, 0:32]),
                                    waits=[(S["v"], S["v"].n)])
            return v_tbl

        pending = layer_loads(0)
        for l in range(2):
            fence = barrier()
            g_wqkv, g_wo, g_rot_p, v_qs_p = pending
            dma("sync", S["d3"], LGC[:, :], flng_c[l, :, :],
                waits=[(S["v"], S["v"].n)])
            dma("sync", S["d3"], LBC[:, :], flnb_c[l, :, :])
            dma("sync", S["d3"], B2C[:, :], fb2_c[l, :, :])
            d_cols = dma("sync", S["d3"], B1C[:, :], fb1_c[l, :, :])

            ZBOFF = 0
            QTOFF = 8256
            VXOFF = 12352

            def ZBr(c, ring, lo2, n, p0=0, pn=128):
                off = ZBOFF + 1032 * c + 516 * ring + lo2
                return B[p0:p0 + pn, off:off + n]

            def KTr(ring, cb, lo2, n, p0=0, pn=128):
                off = 2048 * ring + 512 * cb + lo2
                return AR1[p0:p0 + pn, off:off + n]

            def QTc(ring, cb, lo2, n, p0=0, pn=128):
                off = QTOFF + 2048 * ring + 512 * cb + lo2
                return B[p0:p0 + pn, off:off + n]

            def Qtm(tt, lo2, n):
                off = 6144 + 512 * tt + lo2
                return AR1[:, off:off + n]

            def Ktm(tt, lo2, n):
                return AR2[:, 512 * tt + lo2:512 * tt + lo2 + n]

            def CTXc(cb, p0=0, pn=128):
                return AR2[p0:p0 + pn, 2048 + 512 * cb:2048 + 512 * (cb + 1)]

            def v65s(slot, h, n=65):
                if slot < 8:
                    return V65[:, 520 * slot + 65 * h:520 * slot + 65 * h + n]
                off = VXOFF + 520 * (slot - 8) + 65 * h
                return B[:, off:off + n]

            v_tbl = layer_scales(g_rot_p, v_qs_p)
            v65o = vop(lambda e: e.memset(
                bass.AP(V65, 64, [[4160, 128], [520, 8], [65, 8], [1, 1]]),
                1.0), waits=[(S["v"], S["v"].n)])
            v65o = vop(lambda e: e.memset(
                bass.AP(B, VXOFF + 64,
                        [[NCH * TG, 128], [520, 4], [65, 8], [1, 1]]),
                1.0), waits=[(S["v"], v65o)])

            zb_guard = [0, 0]
            qt_guard = [0, 0]
            kt_guard = [0, 0, 0]
            v65_guard = [0, 0, 0]
            prep_state = {}

            def prepA(w):
                wlo = 1 + 512 * w
                ring = w % 2
                wid = 514 if w == 3 else 516
                v_zb = 0
                for c in range(NCH):
                    v_zb = vop(lambda e, c=c, wlo=wlo, ring=ring, wid=wid:
                               e.tensor_copy(ZBr(c, ring, 0, wid),
                                             Xs(c, wlo - 1, wid)),
                               waits=[(S["v"], S["v"].n),
                                      (S["t"], zb_guard[ring])] +
                               ([(S["gc"], g_z0)] if c == 4 else []))
                v_rotq = [0] * 4
                g_rotk = [0] * 4
                for q in range(4):
                    for part in range(3):
                        bank = part
                        for c in range(NCH):
                            sh = 0 if c < 4 else -1
                            top(lambda e, c=c, q=q, part=part, bank=bank,
                                sh=sh, ring=ring: e.matmul(
                                    PB[bank][:, :],
                                    ZBr(c, ring, 1 + 128 * q + sh, 128),
                                    WSLOT[:, c, 512 * part:512 * (part + 1)],
                                    start=(c == 0), stop=(c == NCH - 1)),
                                waits=(bw(bank) +
                                       [(S["d2"], g_wo),
                                        (S["v"], v_zb)]) if c == 0 else [])
                    t_qkv = S["t"].n
                    slot = (4 * w + q) % 12
                    vq = vop(lambda e, q=q: e.tensor_copy(
                        Qtm(q, 0, 512), PB[0][:, :]),
                        waits=[(S["t"], t_qkv), (S["v"], S["v"].n)])
                    bank_guard[0] = (S["v"], vq)
                    sk = sop(lambda e, q=q: e.copy(
                        Ktm(q, 0, 512), PB[1][:, :]),
                        waits=[(S["t"], t_qkv), (S["s"], S["s"].n),
                               (S["gc"], S["gc"].n)])
                    bank_guard[1] = (S["s"], sk)
                    vv = vop(lambda e, slot=slot: e.tensor_copy(
                        bass.AP(V65.th if False else V65, 520 * slot,
                                [[4160, 128], [65, 8], [1, 64]])
                        if slot < 8 else
                        bass.AP(B, VXOFF + 520 * (slot - 8),
                                [[NCH * TG, 128], [65, 8], [1, 64]]),
                        PB[2][:, :].rearrange("p (h d) -> p h d", h=8)),
                        waits=[(S["v"], vq), (S["v"], v65o),
                               (S["t"], v65_guard[w % 3])])
                    bank_guard[2] = (S["v"], vv)
                    tt_ = 4 * w + q

                    vx = vop(lambda e, q=q: e.tensor_mul(
                        TMP32[:, :], Qtm(q, 0, 512), Qtm(q, 0, 512)),
                        waits=[(S["v"], vq), (S["v"], S["v"].n)])
                    vx = vop(lambda e: e.tensor_reduce(
                        NRM[:, :],
                        TMP32[:, :].rearrange("p (h d) -> p h d", h=8),
                        mybir.AxisListType.X, ALU.add),
                        waits=[(S["v"], vx)])
                    vx = vop(lambda e: e.tensor_scalar_max(
                        NRM[:, :], NRM[:, :], 1e-24),
                        waits=[(S["v"], vx)])
                    sx = sop(lambda e: e.activation(
                        NRM[:, :], NRM[:, :], AF.Ln),
                        waits=[(S["v"], vx), (S["s"], S["s"].n)])
                    sxq = sop(lambda e: e.activation(
                        NRM2[:, :], NRM[:, :], AF.Exp, scale=-0.5),
                        waits=[(S["s"], sx)])
                    vx = vop(lambda e, q=q: e.tensor_mul(
                        TMPA[:, 0:512].rearrange("p (h d) -> p h d", h=8),
                        Qtm(q, 0, 512).rearrange("p (h d) -> p h d", h=8),
                        bass.AP(NRM2, 0, [[8, 128], [1, 8], [0, 64]])),
                        waits=[(S["v"], S["v"].n), (S["s"], sxq)])
                    vx = vop(lambda e, tt_=tt_: e.tensor_mul(
                        TMPB[:, 0:512].rearrange("p (h d) -> p h d", h=8),
                        TMPA[:, 0:512].rearrange("p (h d) -> p h d", h=8),
                        bass.AP(ROT, (0 * NT + tt_) * 64,
                                [[4 * NT * 64, 128], [0, 8], [1, 64]])),
                        waits=[(S["v"], vx), (S["v"], v_tbl)])
                    vx1 = vop(lambda e, q=q, tt_=tt_: e.tensor_mul(
                        Qtm(q, 0, 512).rearrange(
                            "p (h d) -> p h d", h=8)[:, :, 0:32],
                        TMPA[:, 0:512].rearrange(
                            "p (h d) -> p h d", h=8)[:, :, 32:64],
                        bass.AP(ROT, (1 * NT + tt_) * 64,
                                [[4 * NT * 64, 128], [0, 8], [1, 32]])),
                        waits=[(S["v"], vx)])
                    vx2 = vop(lambda e, q=q, tt_=tt_: e.tensor_mul(
                        Qtm(q, 0, 512).rearrange(
                            "p (h d) -> p h d", h=8)[:, :, 32:64],
                        TMPA[:, 0:512].rearrange(
                            "p (h d) -> p h d", h=8)[:, :, 0:32],
                        bass.AP(ROT, (1 * NT + tt_) * 64 + 32,
                                [[4 * NT * 64, 128], [0, 8], [1, 32]])),
                        waits=[(S["v"], vx1)])
                    v_rotq[q] = vop(lambda e, q=q: e.tensor_add(
                        Qtm(q, 0, 512), Qtm(q, 0, 512), TMPB[:, 0:512]),
                        waits=[(S["v"], vx2)])

                    gx = gop(lambda e, q=q: e.tensor_mul(
                        TMPE[:, :], Ktm(q, 0, 512), Ktm(q, 0, 512)),
                        waits=[(S["s"], sk), (S["gc"], S["gc"].n)])
                    vkr = vop(lambda e: e.tensor_reduce(
                        NRM3[:, :],
                        TMPE[:, :].rearrange("p (h d) -> p h d", h=8),
                        mybir.AxisListType.X, ALU.add),
                        waits=[(S["gc"], gx), (S["v"], S["v"].n)])
                    gx = gop(lambda e: e.tensor_scalar_max(
                        NRM3[:, :], NRM3[:, :], 1e-24),
                        waits=[(S["v"], vkr), (S["gc"], S["gc"].n)])
                    sx = sop(lambda e: e.activation(
                        NRM3[:, :], NRM3[:, :], AF.Ln),
                        waits=[(S["gc"], gx), (S["s"], S["s"].n)])
                    sxk = sop(lambda e: e.activation(
                        NRM4[:, :], NRM3[:, :], AF.Exp, scale=-0.5),
                        waits=[(S["s"], sx)])
                    gx = gop(lambda e, q=q: e.tensor_mul(
                        TMPC[:, 0:512].rearrange("p (h d) -> p h d", h=8),
                        Ktm(q, 0, 512).rearrange("p (h d) -> p h d", h=8),
                        bass.AP(NRM4, 0, [[8, 128], [1, 8], [0, 64]])),
                        waits=[(S["gc"], S["gc"].n), (S["s"], sxk)])
                    gx = gop(lambda e, tt_=tt_: e.tensor_mul(
                        TMPE[:, :].rearrange("p (h d) -> p h d", h=8),
                        TMPC[:, 0:512].rearrange("p (h d) -> p h d", h=8),
                        bass.AP(ROT, (2 * NT + tt_) * 64,
                                [[4 * NT * 64, 128], [0, 8], [1, 64]])),
                        waits=[(S["gc"], gx), (S["v"], v_tbl)])
                    gx1 = gop(lambda e, q=q, tt_=tt_: e.tensor_mul(
                        Ktm(q, 0, 512).rearrange(
                            "p (h d) -> p h d", h=8)[:, :, 0:32],
                        TMPC[:, 0:512].rearrange(
                            "p (h d) -> p h d", h=8)[:, :, 32:64],
                        bass.AP(ROT, (3 * NT + tt_) * 64,
                                [[4 * NT * 64, 128], [0, 8], [1, 32]])),
                        waits=[(S["gc"], gx)])
                    gx2 = gop(lambda e, q=q, tt_=tt_: e.tensor_mul(
                        Ktm(q, 0, 512).rearrange(
                            "p (h d) -> p h d", h=8)[:, :, 32:64],
                        TMPC[:, 0:512].rearrange(
                            "p (h d) -> p h d", h=8)[:, :, 0:32],
                        bass.AP(ROT, (3 * NT + tt_) * 64 + 32,
                                [[4 * NT * 64, 128], [0, 8], [1, 32]])),
                        waits=[(S["gc"], gx1)])
                    g_rotk[q] = gop(lambda e, q=q: e.tensor_add(
                        Ktm(q, 0, 512), Ktm(q, 0, 512), TMPE[:, :]),
                        waits=[(S["gc"], gx2)])
                prep_state[w] = (v_rotq, g_rotk)
                zb_guard[ring] = S["t"].n

            def prepB(w):
                v_rotq, g_rotk = prep_state[w]
                for q in range(4):
                    for cb in range(4):
                        tq = top(lambda e, q=q, cb=cb: e.transpose(
                            PBF[:, 0:128], Qtm(q, 128 * cb, 128),
                            IDENTB[:, :]),
                            waits=[(S["v"], v_rotq[q]),
                                   (S["t"], qt_guard[w % 2])] + pbfw())
                        vq2 = vop(lambda e, q=q, cb=cb, w=w: e.tensor_copy(
                            QTc(w % 2, cb, 128 * q, 128), PBF[:, 0:128]),
                            waits=[(S["t"], tq), (S["v"], S["v"].n)])
                        pbf_guard[0] = (S["v"], vq2)
                        tk2 = top(lambda e, q=q, cb=cb: e.transpose(
                            PBF[:, 128:256], Ktm(q, 128 * cb, 128),
                            IDENTB[:, :]),
                            waits=[(S["gc"], g_rotk[q]),
                                   (S["t"], kt_guard[w % 3])] + pbfw())
                        vk2 = vop(lambda e, q=q, cb=cb, w=w: e.tensor_copy(
                            KTr(w % 3, cb, 128 * q, 128), PBF[:, 128:256]),
                            waits=[(S["t"], tk2), (S["v"], S["v"].n)])
                        pbf_guard[0] = (S["v"], vk2)
                prep_state[w] = S["v"].n

            def attn_win(w):
                kts = list(range(8)) if w > 0 else list(range(4, 8))
                v_qkT = prep_state[w]
                prev_ctx2 = [None]
                for cb in range(4):
                    sfree = {}
                    tpv = {}
                    nj = len(kts)
                    tS = [0] * nj

                    def emit_S(jj, cb=cb, kts=kts, w=w, v_qkT=v_qkT):
                        kt = kts[jj]
                        ring = (w - 1) % 3 if kt < 4 else w % 3
                        ktt = kt % 4
                        sl = jj % 2
                        w0 = []
                        if jj < 2:
                            w0 += bw(2 * sl) + bw(2 * sl + 1)
                        else:
                            w0 += [(S["s"], sfree[sl])]
                        if jj == 0:
                            w0 += [(S["v"], v_qkT)]
                        top(lambda e, ring=ring, cb=cb, ktt=ktt, sl=sl,
                            w=w: e.matmul(
                                SP[:, 1024 * sl:1024 * sl + 512],
                                KTr(ring, cb, 128 * ktt, 128, 0, 64),
                                QTc(w % 2, cb, 0, 512, 0, 64),
                                start=True, stop=True),
                            waits=w0)
                        tS[jj] = top(
                            lambda e, ring=ring, cb=cb, ktt=ktt, sl=sl, w=w:
                            e.matmul(SP[:, 1024 * sl + 512:1024 * sl + 1024],
                                     KTr(ring, cb, 128 * ktt, 128, 64, 64),
                                     QTc(w % 2, cb, 0, 512, 64, 64),
                                     start=True, stop=True))

                    emit_S(0)
                    if nj > 1:
                        emit_S(1)
                    if prev_ctx2[0] is not None:
                        prev_ctx2[0]()
                    for jj, kt in enumerate(kts):
                        sl = jj % 2
                        wE = [(S["t"], tS[jj]), (S["s"], S["s"].n)]
                        if jj >= 2:
                            wE.append((S["t"], tpv[sl]))
                        se = sop(lambda e, sl=sl: e.activation(
                            PT[:, 2 * sl:2 * sl + 2, :],
                            SP[:, 1024 * sl:1024 * sl + 1024], AF.Exp,
                            scale=8.0), waits=wE)
                        sfree[sl] = se
                        bank_guard[2 * sl] = (S["s"], se)
                        bank_guard[2 * sl + 1] = (S["s"], se)
                        if jj + 2 < nj:
                            emit_S(jj + 2)
                        vm = vop(lambda e, sl=sl, kt=kt: e.tensor_mul(
                            PT[:, 2 * sl:2 * sl + 2, :],
                            PT[:, 2 * sl:2 * sl + 2, :],
                            bass.AP(MASKS, kt * 512,
                                    [[8 * 512, 128], [0, 2], [1, 512]])),
                            waits=[(S["s"], se), (S["v"], S["v"].n)])
                        wv = w - 1 if kt < 4 else w
                        ktt = kt % 4
                        slot = (4 * wv + ktt) % 12
                        top(lambda e, cb=cb, slot=slot, jj=jj, nj=nj, sl=sl:
                            e.matmul(
                                PVa[0:65, :], v65s(slot, 2 * cb),
                                PT[:, 2 * sl, :], start=(jj == 0),
                                stop=(jj == nj - 1)),
                            waits=[(S["v"], vm)] + (bw(4) if jj == 0 else []))
                        tpv[sl] = top(
                            lambda e, cb=cb, slot=slot, jj=jj, nj=nj, sl=sl:
                            e.matmul(
                                PVb[0:65, :], v65s(slot, 2 * cb + 1),
                                PT[:, 2 * sl + 1, :], start=(jj == 0),
                                stop=(jj == nj - 1)),
                            waits=(bw(5) if jj == 0 else []))
                    t_pv = S["t"].n

                    def mk_ctx2(cb=cb, t_pv=t_pv):
                        def go():
                            ctx_norm2(CTXc(cb, 0, 64), CTXc(cb, 64, 64),
                                      t_pv)
                        return go
                    prev_ctx2[0] = mk_ctx2()
                prev_ctx2[0]()
                qt_guard[w % 2] = S["t"].n
                kt_guard[(w - 1) % 3] = S["t"].n
                if w > 0:
                    v65_guard[(w - 1) % 3] = S["t"].n

            def wo_win(w):
                wlo = 1 + 512 * w
                v_ctx = S["v"].n
                for ot in range(NCH):
                    bank = ot % 2
                    for cb in range(4):
                        top(lambda e, cb=cb, ot=ot, bank=bank: e.matmul(
                            PB[bank][:, :],
                            WSLOT2[:, 1024 * cb + 128 * ot:
                                   1024 * cb + 128 * (ot + 1)],
                            CTXc(cb), start=(cb == 0), stop=(cb == 3)),
                            waits=(bw(bank) + [(S["d2"], g_wo),
                                               (S["v"], v_ctx)])
                            if cb == 0 else [])
                    tn = S["t"].n
                    vr = vop(lambda e, ot=ot, wlo=wlo, bank=bank:
                             e.tensor_add(
                                 Xs(ot, wlo, 512), Xs(ot, wlo, 512),
                                 PB[bank][:, :]),
                             waits=[(S["t"], tn), (S["v"], S["v"].n)])
                    bank_guard[bank] = (S["v"], vr)

            prepA(0)
            prepB(0)
            prepA(1)
            for w in range(NTB):
                attn_win(w)
                if w + 1 < NTB:
                    prepB(w + 1)
                wo_win(w)
                if w + 2 < NTB:
                    prepA(w + 2)
            v_attn = S["v"].n
            if l == 0:
                pending = layer_loads(1)
                pending = (pending[0], pending[1], pending[2], pending[3])

            v_zs = vop(lambda e: e.tensor_copy(
                ZSAVE[:, :, :],
                bass.AP(X, 512, [[NCH * TG, 128], [TG, NCH], [512, 3]])),
                waits=[(S["v"], v_attn)])
            h2_guard = {0: 0, 1: 0}
            zw_guard = [0, 0]

            def H2(pp, c):
                if pp == 0:
                    return AR2[:, 512 * c:512 * (c + 1)]
                return B[:, 11264 + 512 * c:11264 + 512 * (c + 1)]

            def mlp_prep(tb):
                pp = tb % 2
                lo = 1 + 512 * tb
                v_zw = 0
                for c in range(NCH):
                    v_zw = vop(lambda e, c=c, lo=lo: e.tensor_copy(
                        AR1[:, 513 * c + 1:513 * c + 513], Xs(c, lo, 512)),
                        waits=[(S["v"], v_zs if (tb == 0 and c == 0)
                                else S["v"].n),
                               (S["t"], zw_guard[0]),
                               (S["s"], zw_guard[1])] if c == 0 else
                        [(S["v"], S["v"].n)])
                    if tb == 0:
                        v_zw = vop(lambda e, c=c: e.memset(
                            AR1[:, 513 * c:513 * c + 1], 0.0),
                            waits=[(S["v"], v_zw)])
                    else:
                        v_zw = vop(lambda e, c=c, tb=tb: e.tensor_copy(
                            AR1[:, 513 * c:513 * c + 1],
                            ZSAVE[:, c, tb - 1:tb]),
                            waits=[(S["v"], v_zw)])
                v_bc = emit_stats_block(
                    lambda c: AR1[:, 513 * c + (1 if c < 4 else 0):
                                  513 * c + (1 if c < 4 else 0) + 512],
                    1e-5, [(S["v"], v_zw)])
                zw_guard[0] = S["t"].n
                zw_guard[1] = S["s"].n
                hg = h2_guard[pp]
                v_h2c = [0] * NCH
                for c in range(NCH):
                    if c < 4:
                        v1 = vop(lambda e, c=c, lo=lo: e.tensor_sub(
                            TMP32[:, :], Xs(c, lo, 512), BC2[:, :]),
                            waits=[(S["v"], v_bc if c == 0 else S["v"].n)])
                    elif tb == 0:
                        v1 = vop(lambda e, c=c, lo=lo: e.tensor_sub(
                            TMP32[:, :], Xs(c, lo - 1, 512), BC2[:, :]),
                            waits=[(S["v"], v_bc if c == 0 else S["v"].n)])
                    else:
                        vop(lambda e, c=c, lo=lo: e.tensor_sub(
                            TMP32[:, 1:512], Xs(c, lo, 511), BC2[:, 1:512]),
                            waits=[(S["v"], S["v"].n)])
                        v1 = vop(lambda e, c=c, tb=tb: e.tensor_sub(
                            TMP32[:, 0:1], ZSAVE[:, c, tb - 1:tb],
                            BC2[:, 0:1]),
                            waits=[(S["v"], S["v"].n)])
                    v2 = vop(lambda e, c=c: e.scalar_tensor_tensor(
                        TMP32[:, :], TMP32[:, :], LGC[:, c:c + 1], BC1[:, :],
                        ALU.mult, ALU.mult),
                        waits=[(S["v"], v1), (S["d3"], d_cols)])
                    v_h2c[c] = vop(lambda e, c=c, pp=pp: e.tensor_scalar_add(
                        H2(pp, c), TMP32[:, :], LBC[:, c:c + 1]),
                        waits=[(S["v"], v2), (S["t"], hg)])
                return v_h2c

            vh2 = {0: mlp_prep(0)}
            for tb in range(NTB):
                lo = 1 + 512 * tb
                pp = tb % 2
                v_h2c = vh2[tb]
                v_h2 = v_h2c[NCH - 1]

                tg_last = [0, 0]
                last_va = 0
                for pr in range(NPAIR):
                    par = pr % 2
                    off, wid = A_OFFS[pr], A_WIDS[pr]
                    sA = S["w0a"] if par == 0 else S["w1a"]
                    sB = S["w0b"] if par == 0 else S["w1b"]
                    wld = [(S["t"], tg_last[par])] if pr >= 2 else \
                        [(S["g1"], g_wcache), (S["t"], h2_guard[pp])]
                    dna = dma("sync", sA,
                              bass.AP(KD, 2048 * par,
                                      [[2 * T, 128], [256, NCH], [1, wid]]),
                              bass.AP(w1C.tensor,
                                      (((l * NPAIR + pr) * 2 + 0) * 128 * NCH * 128),
                                      [[NCH * 128, 128], [128, NCH], [1, wid]]),
                              waits=wld)
                    dnb = dma("sync", sB,
                              bass.AP(KD, 2048 * par + 128,
                                      [[2 * T, 128], [256, NCH], [1, wid]]),
                              bass.AP(w1C.tensor,
                                      (((l * NPAIR + pr) * 2 + 1) * 128 * NCH * 128),
                                      [[NCH * 128, 128], [128, NCH], [1, wid]]))
                    ba, bg = 2 + 2 * par, 3 + 2 * par
                    tg_l = 0
                    for c in range(NCH):
                        top(lambda e, c=c, par=par, wid=wid, ba=ba,
                            pp=pp: e.matmul(
                            PB[ba][0:wid, :],
                            bass.AP(KD, 2048 * par + 256 * c,
                                    [[2 * T, 128], [1, wid]]),
                            H2(pp, c),
                            start=(c == 0), stop=(c == NCH - 1)),
                            waits=(bw(ba) + [(sA, dna), (sB, dnb)]
                                   if c == 0 else []) +
                            ([(S["v"], v_h2c[c])] if pr < 2 else []))
                        tg_l = top(lambda e, c=c, par=par, wid=wid, bg=bg,
                                   pp=pp: e.matmul(
                                       PB[bg][0:wid, :],
                                       bass.AP(KD, 2048 * par + 256 * c + 128,
                                               [[2 * T, 128], [1, wid]]),
                                       H2(pp, c),
                                       start=(c == 0), stop=(c == NCH - 1)),
                                   waits=(bw(bg)) if c == 0 else [])
                    tg_last[par] = tg_l
                    sg = sop(lambda e, pr=pr, wid=wid, bg=bg: e.activation(
                        TMPB[0:wid, 0:512], PB[bg][0:wid, :], AF.Gelu,
                        bias=B1C[0:wid, 22 + pr:23 + pr]),
                        waits=[(S["t"], tg_l), (S["s"], S["s"].n),
                               (S["v"], last_va)])
                    bank_guard[bg] = (S["s"], sg)
                    va = vop(lambda e, pr=pr, wid=wid, ba=ba:
                             e.scalar_tensor_tensor(
                                 B[0:wid, 512 * pr:512 * (pr + 1)],
                                 PB[ba][0:wid, :], B1C[0:wid, pr:pr + 1],
                                 TMPB[0:wid, 0:512], ALU.add, ALU.mult),
                             waits=[(S["s"], sg), (S["v"], S["v"].n)])
                    bank_guard[ba] = (S["v"], va)
                    last_va = va
                v_actn = S["v"].n
                h2_guard[pp] = S["t"].n
                if tb + 1 < NTB:
                    vh2[tb + 1] = mlp_prep(tb + 1)

                for half in range(2):
                    ots = list(range(4 * half, 4 * half + 4))
                    tw_last = [0, 0]
                    for pr in range(NPAIR):
                        par = pr % 2
                        off, wid = A_OFFS[pr], A_WIDS[pr]
                        dsem = S["w2a"] if par == 0 else S["w2b"]
                        wld = [(S["t"], tw_last[par])] if pr >= 2 else \
                            [(S["t"], pt_guard[0])]
                        d_w2 = dma("sync", dsem,
                                   bass.AP(PT, 1024 * par,
                                           [[4 * 512, wid], [1, 1024]]),
                                   w2C[l, off:off + wid, :], waits=wld)
                        tw = 0
                        for oi, ot in enumerate(ots):
                            w0 = []
                            if pr == 0:
                                w0 += bw(oi) + [(S["v"], v_actn)]
                            if oi == 0:
                                w0 += [(dsem, d_w2)]
                            tw = top(lambda e, pr=pr, par=par, wid=wid,
                                     ot=ot, oi=oi: e.matmul(
                                         PB[oi][:, :],
                                         bass.AP(PT, 1024 * par + 128 * ot,
                                                 [[4 * 512, wid], [1, 128]]),
                                         B[0:wid, 512 * pr:512 * (pr + 1)],
                                         start=(pr == 0),
                                         stop=(pr == NPAIR - 1)),
                                     waits=w0)
                        tw_last[par] = tw
                    t_done = S["t"].n
                    pt_guard[0] = t_done
                    for oi, ot in enumerate(ots):
                        vr = vop(lambda e, ot=ot, lo=lo, oi=oi:
                                 e.scalar_tensor_tensor(
                                     Xs(ot, lo, 512), PB[oi][:, :],
                                     B2C[:, ot:ot + 1], Xs(ot, lo, 512),
                                     ALU.add, ALU.add),
                                 waits=[(S["t"], t_done),
                                        (S["v"], S["v"].n)])
                        bank_guard[oi] = (S["v"], vr)
            v_z0 = S["v"].n

        v_fin = S["v"].n
        slots = [BC1[:, 0:512], BC2[:, 0:512], TMP32[:, 0:512]]
        ssem = [S["d0"], S["d1"], S["d2"]]
        vres = [0, 0, 0]
        for c in range(NCH):
            for tb in range(NTB):
                i = c * NTB + tb
                k = i % 3
                waits = [(S["v"], v_fin)]
                if i >= 3:
                    waits.append((S["v"], vres[k]))
                dx = dma("sync", ssem[k], slots[k],
                         xpark[c, :, 1 + 512 * tb:1 + 512 * (tb + 1)],
                         waits=waits)
                vres[k] = vop(lambda e, c=c, tb=tb, k=k: e.tensor_add(
                    Xs(c, 1 + 512 * tb, 512), Xs(c, 1 + 512 * tb, 512),
                    slots[k]),
                    waits=[(ssem[k], dx), (S["v"], S["v"].n)])

    def phase_attn_b():
        barrier()
        gw0 = dma("gpsimd", S["g0"], WSLOT2[0:64, 0:512], bDQT[:, :])
        dma("gpsimd", S["g0"], WSLOT2[64:128, 0:512], bDQT[:, :])
        dma("gpsimd", S["g0"], WSLOT2[0:64, 512:1536], bDKVT[:, :])
        dma("gpsimd", S["g0"], WSLOT2[64:128, 512:1536], bDKVT[:, :])
        dma("gpsimd", S["g0"],
            WSLOT2[:, 1536:1792].rearrange("p (c o) -> p c o", c=4),
            bUQT[:, :].rearrange("(c p) o -> p c o", p=128))
        dma("gpsimd", S["g0"],
            WSLOT2[:, 2048:3072].rearrange("p (c o) -> p c o", c=8),
            bUKVT[:, :].rearrange("(c p) o -> p c o", p=128))
        dma("gpsimd", S["g0"], WSLOT2[0:64, 3584:3648], bQRT[:, :])
        dma("gpsimd", S["g0"], WSLOT2[0:64, 3648:3712], bKRT[:, :])
        dma("gpsimd", S["g0"], WSLOT2[64:128, 3648:3712], bKRT[:, :])
        dma("gpsimd", S["g0"], WSLOT2[0:64, 3712:3776], bWOT[:, :])
        gw = dma("gpsimd", S["g0"], WSLOT2[64:128, 3712:3776], bWOT[:, :])

        v_mat, g_mat = entry_ln(lambda c, lo: Bs(c, lo, 512), gp=True)

        def Qh(h01, lo2, n, p0=0, pn=128):
            return AR1[p0:p0 + pn, 2048 * h01 + lo2:2048 * h01 + lo2 + n]

        def CQb(par, h01, p0=0, pn=128):
            off = 4096 + 1024 * par + 512 * h01
            return AR1[p0:p0 + pn, off:off + 512]

        def CKVb(par, h01, p0=0, pn=128):
            off = 6144 + 1024 * par + 512 * h01
            return AR1[p0:p0 + pn, off:off + 512]

        dqr_g = {0: 0, 1: 0}
        dkr_g = {0: 0, 1: 0}
        dv_g = [0]
        tre_g = {0: 0, 1: 0}

        def rms2_rows(t_ms_a, t_ms_b):
            v1 = vop(lambda e: e.tensor_copy(ROWS[:, 0:512], PBX[0:1, :]),
                     waits=[(S["t"], t_ms_a), (S["v"], S["v"].n)])
            v1a = vop(lambda e: e.tensor_copy(ROWS[:, 512:1024],
                                              PBX[64:65, :]),
                      waits=[(S["t"], t_ms_b), (S["v"], v1)])
            pbx_guard[0] = (S["v"], v1a)
            s1 = sop(lambda e: e.activation(
                ROWS[:, 0:1024], ROWS[:, 0:1024], AF.Ln,
                bias=EPSC[0:1, 1:2], scale=1.0 / 64.0),
                waits=[(S["v"], v1a), (S["s"], S["s"].n)])
            s1b = sop(lambda e: e.activation(
                RECB[:, :], ROWS[:, 0:1024], AF.Exp, scale=-0.5),
                waits=[(S["s"], s1)])
            return s1b

        def rms2_bcast(s1b):
            tb1 = top(lambda e: e.matmul(
                PBX[0:64, :], ONESCB[0:1, 0:64], RECB[:, 0:512],
                start=True, stop=True),
                waits=pbxw() + [(S["s"], s1b)])
            tb2 = top(lambda e: e.matmul(
                PBX[64:128, :], ONESCB[0:1, 64:128], RECB[:, 512:1024],
                start=True, stop=True))
            s3 = sop(lambda e: e.copy(TMP32[:, :], PBX[:, :]),
                     waits=[(S["t"], tb2), (S["s"], S["s"].n),
                            (S["v"], S["v"].n)])
            pbx_guard[0] = (S["s"], s3)
            return s3

        for hp in range(NCH):
            for tb in range(NTB):
                lo = 1 + 512 * tb

                def emit_cq(lt, hp=hp, lo=lo):
                    par = lt % 2
                    bA, bB = 2 * par, 2 * par + 1
                    w0 = bw(bA) + bw(bB)
                    if lt == 0:
                        w0 += [(S["v"], v_mat[tb]) if hp < 4 else
                               (S["gc"], g_mat[tb])]
                    if hp == 0 and tb == 0 and lt == 0:
                        w0 += [(S["g0"], gw)]
                    top(lambda e, hp=hp, lo=lo, lt=lt, bA=bA: e.matmul(
                        PB[bA][:, :], WSLOT2[0:64, 128 * lt:128 * (lt + 1)],
                        Bs(hp, lo, 512, 0, 64), start=True, stop=True),
                        waits=w0)
                    return top(lambda e, hp=hp, lo=lo, lt=lt, bB=bB:
                               e.matmul(
                                   PB[bB][:, :],
                                   WSLOT2[64:128, 128 * lt:128 * (lt + 1)],
                                   Bs(hp, lo, 512, 64, 64),
                                   start=True, stop=True))

                tcq = [0] * 4
                qc_done = [0, 0]
                tcq[0] = emit_cq(0)
                for lt in range(4):
                    par = lt % 2
                    bA, bB = 2 * par, 2 * par + 1
                    if lt + 1 < 4:
                        tcq[lt + 1] = emit_cq(lt + 1)
                    vA = vop(lambda e, par=par, bA=bA: e.tensor_copy(
                        CQb(par, 0), PB[bA][:, :]),
                        waits=[(S["t"], tcq[lt]), (S["v"], S["v"].n),
                               (S["t"], qc_done[par])])
                    sB = sop(lambda e, par=par, bB=bB: e.copy(
                        CQb(par, 1), PB[bB][:, :]),
                        waits=[(S["t"], tcq[lt]), (S["s"], S["s"].n),
                               (S["t"], qc_done[par])])
                    bank_guard[bA] = (S["v"], vA)
                    bank_guard[bB] = (S["s"], sB)
                    top(lambda e, lt=lt: e.matmul(
                        PVa[0:64, :],
                        WSLOT2[:, 1536 + 64 * lt:1600 + 64 * lt],
                        CQb(lt % 2, 0), start=(lt == 0), stop=(lt == 3)),
                        waits=[(S["v"], vA)] + (bw(4) if lt == 0 else []))
                    tqc = top(lambda e, lt=lt: e.matmul(
                        PVb[0:64, :],
                        WSLOT2[:, 1536 + 64 * lt:1600 + 64 * lt],
                        CQb(lt % 2, 1), start=(lt == 0), stop=(lt == 3)),
                        waits=[(S["s"], sB)] + (bw(5) if lt == 0 else []))
                    qc_done[par] = tqc

                sqA = sop(lambda e: e.activation(TMPA[0:64, 0:512],
                                                 PVa[0:64, :], AF.Square),
                          waits=[(S["t"], tqc), (S["s"], S["s"].n)])
                sqB = sop(lambda e: e.activation(TMPA[64:128, 0:512],
                                                 PVb[0:64, :], AF.Square),
                          waits=[(S["s"], sqA)])
                tmsA = top(lambda e: e.matmul(
                    PBX[0:1, :], HALF2[0:64, 0:1], TMPA[0:64, 0:512],
                    start=True, stop=True),
                    waits=[(S["s"], sqA)] + pbxw())
                tmsB = top(lambda e: e.matmul(
                    PBX[64:65, :], HALF2[64:128, 1:2], TMPA[64:128, 0:512],
                    start=True, stop=True),
                    waits=[(S["s"], sqB)])
                s1b = rms2_rows(tmsA, tmsB)
                tkv_started = []

                def emit_ckv(lt, hp=hp, lo=lo):
                    par = lt % 2
                    bA, bB = 2 * par, 2 * par + 1
                    top(lambda e, hp=hp, lo=lo, lt=lt, bA=bA: e.matmul(
                        PB[bA][:, :],
                        WSLOT2[0:64, 512 + 128 * lt:512 + 128 * (lt + 1)],
                        Bs(hp, lo, 512, 0, 64), start=True, stop=True),
                        waits=bw(bA) + bw(bB))
                    return top(lambda e, hp=hp, lo=lo, lt=lt, bB=bB:
                               e.matmul(
                                   PB[bB][:, :],
                                   WSLOT2[64:128,
                                          512 + 128 * lt:512 + 128 * (lt + 1)],
                                   Bs(hp, lo, 512, 64, 64),
                                   start=True, stop=True))

                tckv = [0] * 8
                tckv[0] = emit_ckv(0)
                tckv[1] = emit_ckv(1)
                s3 = rms2_bcast(s1b)
                vqA = vop(lambda e, lo=lo: e.scalar_tensor_tensor(
                    Qh(0, lo - 1, 512, 0, 64), PVa[0:64, :],
                    BIASC[0:64, 17:18], TMP32[0:64, :],
                    ALU.mult, ALU.mult),
                    waits=[(S["s"], s3), (S["v"], S["v"].n)])
                vqB = vop(lambda e, lo=lo: e.scalar_tensor_tensor(
                    Qh(1, lo - 1, 512, 0, 64), PVb[0:64, :],
                    BIASC[64:128, 17:18], TMP32[64:128, :],
                    ALU.mult, ALU.mult),
                    waits=[(S["v"], vqA)])
                bank_guard[4] = (S["v"], vqB)
                bank_guard[5] = (S["v"], vqB)
                for h01 in range(2):
                    bank = 4 + h01
                    tr = top(lambda e, h01=h01, lo=lo, bank=bank: e.matmul(
                        PB[bank][0:64, :], WSLOT2[0:64, 3584:3648],
                        Qh(h01, lo - 1, 512, 0, 64), start=True, stop=True),
                        waits=bw(bank) + [(S["v"], vqB)])
                    qsem = S["d0"] if h01 == 0 else S["dq1"]
                    ksem = S["d2"] if h01 == 0 else S["dk1"]
                    vr = vop(lambda e, h01=h01, bank=bank: e.tensor_copy(
                        TMPC[64 * h01:64 * h01 + 64, 0:512],
                        PB[bank][0:64, :]),
                        waits=[(S["t"], tr), (S["v"], S["v"].n),
                               (qsem, dqr_g[h01]),
                               (ksem, dkr_g[h01])])
                    bank_guard[bank] = (S["v"], vr)
                    dqr_g[h01] = dma("sync", qsem,
                                     Qh(h01, lo - 1, 512, 64, 64),
                                     TMPC[64 * h01:64 * h01 + 64, 0:512],
                                     waits=[(S["v"], vr)])

                tkv = 0
                kv_done = [0, 0]
                for lt in range(NCH):
                    par = lt % 2
                    bA, bB = 2 * par, 2 * par + 1
                    vA = vop(lambda e, par=par, bA=bA: e.tensor_copy(
                        CKVb(par, 0), PB[bA][:, :]),
                        waits=[(S["t"], tckv[lt]), (S["v"], S["v"].n),
                               (S["t"], kv_done[par])])
                    sB = sop(lambda e, par=par, bB=bB: e.copy(
                        CKVb(par, 1), PB[bB][:, :]),
                        waits=[(S["t"], tckv[lt]), (S["s"], S["s"].n),
                               (S["t"], kv_done[par])])
                    bank_guard[bA] = (S["v"], vA)
                    bank_guard[bB] = (S["s"], sB)
                    if lt + 2 < NCH:
                        tckv[lt + 2] = emit_ckv(lt + 2)
                    top(lambda e, lt=lt: e.matmul(
                        PVa[:, :],
                        WSLOT2[:, 2048 + 128 * lt:2176 + 128 * lt],
                        CKVb(lt % 2, 0), start=(lt == 0), stop=(lt == 7)),
                        waits=[(S["v"], vA)] + (bw(4) if lt == 0 else []))
                    tkv = top(lambda e, lt=lt: e.matmul(
                        PVb[:, :],
                        WSLOT2[:, 2048 + 128 * lt:2176 + 128 * lt],
                        CKVb(lt % 2, 1), start=(lt == 0), stop=(lt == 7)),
                        waits=[(S["s"], sB)] + (bw(5) if lt == 0 else []))
                    kv_done[par] = tkv

                sqA = sop(lambda e: e.activation(TMPA[0:64, 0:512],
                                                 PVa[0:64, :], AF.Square),
                          waits=[(S["t"], tkv), (S["s"], S["s"].n)])
                sqB = sop(lambda e: e.activation(TMPA[64:128, 0:512],
                                                 PVb[0:64, :], AF.Square),
                          waits=[(S["s"], sqA)])
                tmsA = top(lambda e: e.matmul(
                    PBX[0:1, :], HALF2[0:64, 0:1], TMPA[0:64, 0:512],
                    start=True, stop=True),
                    waits=[(S["s"], sqA)] + pbxw())
                tmsB = top(lambda e: e.matmul(
                    PBX[64:65, :], HALF2[64:128, 1:2], TMPA[64:128, 0:512],
                    start=True, stop=True),
                    waits=[(S["s"], sqB)])
                s1b = rms2_rows(tmsA, tmsB)
                s3 = rms2_bcast(s1b)
                for h01 in range(2):
                    pv = PVa if h01 == 0 else PVb
                    vk = vop(lambda e, h01=h01, tb=tb, pv=pv:
                             e.scalar_tensor_tensor(
                                 KD[0:64, h01, 512 * tb:512 * (tb + 1)],
                                 pv[0:64, :],
                                 BIASC[64 * h01:64 * h01 + 64, 18:19],
                                 TMP32[64 * h01:64 * h01 + 64, :],
                                 ALU.mult, ALU.mult),
                             waits=[(S["s"], s3), (S["v"], S["v"].n)])
                    vv = vop(lambda e, pv=pv: e.tensor_copy(
                        TMPB[64:128, 0:512], pv[64:128, :]),
                        waits=[(S["v"], vk), (S["d1"], dv_g[0])])
                    bank_guard[4 + h01] = (S["v"], vv)
                    dv_g[0] = dma("sync", S["d1"],
                                  TMPE[64 * h01:64 * h01 + 64, 0:512],
                                  TMPB[64:128, 0:512],
                                  waits=[(S["v"], vv), (S["t"], tre_g[h01])])
                    for q in range(4):
                        tt = 4 * tb + q
                        tq = top(lambda e, q=q, h01=h01: e.transpose(
                            PBF[:, 0:64],
                            TMPE[64 * h01:64 * h01 + 64,
                                 128 * q:128 * (q + 1)],
                            IDENTB[0:64, 0:64] if h01 == 0 else
                            IDENTB[64:128, 64:128]),
                            waits=[(S["d1"], dv_g[0])] + pbfw())
                        vq2 = vop(lambda e, tt=tt, h01=h01: e.tensor_copy(
                            V65[:, 1040 * h01 + 65 * tt:
                                1040 * h01 + 65 * tt + 64], PBF[:, 0:64]),
                            waits=[(S["t"], tq), (S["v"], S["v"].n)])
                        pbf_guard[0] = (S["v"], vq2)
                    tre_g[h01] = S["t"].n
                for h01 in range(2):
                    bank = h01
                    tr = top(lambda e, hp=hp, lo=lo, h01=h01, bank=bank:
                             e.matmul(
                                 PB[bank][0:64, :],
                                 WSLOT2[64 * h01:64 * h01 + 64, 3648:3712],
                                 Bs(hp, lo, 512, 64 * h01, 64),
                                 start=True, stop=True),
                             waits=bw(bank))
                    qsem = S["d0"] if h01 == 0 else S["dq1"]
                    ksem = S["d2"] if h01 == 0 else S["dk1"]
                    vr = vop(lambda e, bank=bank, h01=h01: e.tensor_copy(
                        TMPC[64 * h01:64 * h01 + 64, 0:512],
                        PB[bank][0:64, :]),
                        waits=[(S["t"], tr), (S["v"], S["v"].n),
                               (ksem, dkr_g[h01]),
                               (qsem, dqr_g[h01])])
                    bank_guard[bank] = (S["v"], vr)
                    dkr_g[h01] = dma("sync", ksem,
                                     KD[64:128, h01, 512 * tb:512 * (tb + 1)],
                                     TMPC[64 * h01:64 * h01 + 64, 0:512],
                                     waits=[(S["v"], vr)])
            d_proj = [(S["d0"], S["d0"].n), (S["d1"], S["d1"].n),
                      (S["d2"], S["d2"].n), (S["dq1"], S["dq1"].n),
                      (S["dk1"], S["dk1"].n)]
            v65o = vop(lambda e: e.memset(
                bass.AP(V65, 64, [[4160, 128], [1040, 2], [65, NT], [1, 1]]),
                1.0), waits=[(S["v"], S["v"].n)])

            prev_ctxb = [None]
            for tb in range(NTB):
                for h01 in range(2):
                    njt = 4 * tb + 4
                    npair = njt // 2
                    sfree = {}
                    tpvp = {}
                    tS = [0] * npair

                    def emit_S(jp, h01=h01, tb=tb):
                        sl = jp % 2
                        j0, j1 = 2 * jp, 2 * jp + 1
                        w0 = []
                        if jp < 2:
                            w0 += bw(2 * sl) + bw(2 * sl + 1)
                        else:
                            w0 += [(S["s"], sfree[sl])]
                        if jp == 0:
                            w0 += [(S["v"], v65o)] + d_proj
                        top(lambda e, j=j0, h01=h01, tb=tb, sl=sl:
                            e.matmul(
                                SP[:, 1024 * sl:1024 * sl + 512],
                                KD[:, h01, 128 * j:128 * (j + 1)],
                                Qh(h01, 512 * tb, 512),
                                start=True, stop=True),
                            waits=w0)
                        tS[jp] = top(lambda e, j=j1, h01=h01, tb=tb, sl=sl:
                                     e.matmul(
                                         SP[:, 1024 * sl + 512:
                                            1024 * sl + 1024],
                                         KD[:, h01, 128 * j:128 * (j + 1)],
                                         Qh(h01, 512 * tb, 512),
                                         start=True, stop=True))

                    emit_S(0)
                    if npair > 1:
                        emit_S(1)
                    if prev_ctxb[0] is not None:
                        prev_ctxb[0]()
                    for jp in range(npair):
                        sl = jp % 2
                        j0, j1 = 2 * jp, 2 * jp + 1
                        wE = [(S["t"], tS[jp]), (S["s"], S["s"].n)]
                        if jp >= 2:
                            wE.append((S["t"], tpvp[sl]))
                        se = sop(lambda e, sl=sl: e.activation(
                            PT[:, 2 * sl:2 * sl + 2, :],
                            SP[:, 1024 * sl:1024 * sl + 1024], AF.Exp,
                            scale=0.125), waits=wE)
                        sfree[sl] = se
                        bank_guard[2 * sl] = (S["s"], se)
                        bank_guard[2 * sl + 1] = (S["s"], se)
                        if jp + 2 < npair:
                            emit_S(jp + 2)
                        if j0 >= 4 * tb:
                            moff = 4 + j0 - 4 * tb
                            vm = vop(lambda e, sl=sl, moff=moff: e.tensor_mul(
                                PT[:, 2 * sl:2 * sl + 2, :],
                                PT[:, 2 * sl:2 * sl + 2, :],
                                MASKS[:, moff:moff + 2, :]),
                                waits=[(S["s"], se), (S["v"], S["v"].n)])
                            wPV = [(S["v"], vm)]
                        else:
                            wPV = [(S["s"], se)]
                        top(lambda e, j=j0, h01=h01, sl=sl, jp=jp:
                            e.matmul(
                                PVa[0:65, :],
                                V65[:, 1040 * h01 + 65 * j:
                                    1040 * h01 + 65 * j + 65],
                                PT[:, 2 * sl, :], start=(jp == 0),
                                stop=False),
                            waits=wPV + (bw(4) if jp == 0 else []))
                        tpvp[sl] = top(lambda e, j=j1, h01=h01, sl=sl,
                                       jp=jp, npair=npair: e.matmul(
                                           PVa[0:65, :],
                                           V65[:, 1040 * h01 + 65 * j:
                                               1040 * h01 + 65 * j + 65],
                                           PT[:, 2 * sl + 1, :], start=False,
                                           stop=(jp == npair - 1)))
                    t_pv = S["t"].n

                    def mk_ctxb(h01=h01, t_pv=t_pv):
                        def go():
                            ctx_norm(4, AR2[64 * h01:64 * h01 + 64, 0:512],
                                     t_pv)
                        return go
                    prev_ctxb[0] = mk_ctxb()
                if prev_ctxb[0] is not None:
                    prev_ctxb[0]()
                    prev_ctxb[0] = None
                vboth = S["v"].n
                top(lambda e: e.matmul(
                    PB[5][0:64, :], WSLOT2[0:64, 3712:3776],
                    AR2[0:64, 0:512], start=True, stop=True),
                    waits=bw(5) + [(S["v"], vboth)])
                to2 = top(lambda e: e.matmul(
                    PB[5][64:128, :], WSLOT2[64:128, 3712:3776],
                    AR2[64:128, 0:512], start=True, stop=True))
                vr1 = vop(lambda e, hp=hp, tb=tb: e.tensor_add(
                    Xs(hp, 1 + 512 * tb, 512), Xs(hp, 1 + 512 * tb, 512),
                    PB[5][:, :]),
                    waits=[(S["t"], to2), (S["v"], S["v"].n)])
                bank_guard[5] = (S["v"], vr1)

    def phase_out():
        barrier()
        v_fin = S["v"].n
        dstore = {}
        ssems = [S["sp"], S["spb"], S["dq1"], S["dk1"]]
        slots = [BC1[:, 0:128], BC1[:, 128:256], BC2[:, 0:128],
                 BC2[:, 128:256]]
        for c in range(NCH):
            for tt in range(NT):
                i = c * NT + tt
                k = i % 4
                tw = [(S["v"], v_fin)] + (bw(i % 4) if i < 4 else [])
                if i >= 4:
                    tw.append((ssems[k], dstore[i - 4]))
                tn = top(lambda e, c=c, tt=tt, pb=PB[i % 4]: e.transpose(
                    pb[:, 0:128], Xs(c, 1 + 128 * tt, 128), IDENT[:, :]),
                    waits=tw)
                vv = vop(lambda e, pb=PB[i % 4], k=k: e.tensor_copy(
                    slots[k], pb[:, 0:128]),
                    waits=[(S["t"], tn), (S["v"], S["v"].n)])
                dstore[i] = dma("scalar", ssems[k],
                                out[128 * tt:128 * (tt + 1),
                                    128 * c:128 * (c + 1)],
                                slots[k],
                                waits=[(S["v"], vv)])
        p.do("scalar", None, waits=[(s, s.n) for s in ssems])

    if 1 in phases:
        phase_attn_a()
    if 2 in phases:
        phase_ffn(0)
    if 3 in phases:
        phase_attn_b()
    if 4 in phases:
        phase_ffn(1)
    phase_out()

    block = es.enter_context(nc.Block())
    with es:
        p.realize(block)
    return nc


def _prep_inputs(inputs):
    f32 = np.float32

    def c(a):
        return np.ascontiguousarray(np.asarray(a, dtype=f32))

    shared = {
        "aWqT": c(np.asarray(inputs["L0_attn_a_Wq"]).T),
        "aWkT": c(np.asarray(inputs["L0_attn_a_Wk"]).T),
        "aWvT": c(np.asarray(inputs["L0_attn_a_Wv"]).T),
        "aWoT": c(np.asarray(inputs["L0_attn_a_Wo"]).T),
        "abq_c": c(np.asarray(inputs["L0_attn_a_bq"]).reshape(8, 128).T),
        "abo_c": c(np.asarray(inputs["L0_attn_a_bo"]).reshape(8, 128).T),
        "bDQT": c(np.asarray(inputs["L0_attn_b_W_DQ"]).T),
        "bUQT": c(np.asarray(inputs["L0_attn_b_W_UQ"]).T),
        "bDKVT": c(np.asarray(inputs["L0_attn_b_W_DKV"]).T),
        "bUKVT": c(np.asarray(inputs["L0_attn_b_W_UKV"]).T),
        "bQRT": c(np.asarray(inputs["L0_attn_b_W_QR"]).T),
        "bKRT": c(np.asarray(inputs["L0_attn_b_W_KR"]).T),
        "bWOT": c(np.asarray(inputs["L0_attn_b_W_O"]).T),
        "fwqkv": c(inputs["L0_ffn_wqkv"]),
        "fqs": c(inputs["L0_ffn_q_scale"]),
        "fks": c(inputs["L0_ffn_k_scale"]),
        "fwo": c(inputs["L0_ffn_wo"]),
        "flng_c": c(np.asarray(inputs["L0_ffn_ln_g"]).reshape(2, 8, 128)
                    .transpose(0, 2, 1)),
        "flnb_c": c(np.asarray(inputs["L0_ffn_ln_b"]).reshape(2, 8, 128)
                    .transpose(0, 2, 1)),
        "fw1": c(inputs["L0_ffn_w1"]),
        "fw2": c(inputs["L0_ffn_w2"]),
        "fb2_c": c(np.asarray(inputs["L0_ffn_b2"]).reshape(2, 8, 128)
                   .transpose(0, 2, 1)),
        "ident_in": np.eye(128, dtype=f32),
        "rotc": _host_rotary_tables(),
        "masks_in": _host_masks(),
    }
    bv = np.zeros((128, 1), f32)
    bv[0:64, 0] = np.asarray(inputs["L0_attn_a_bv"], dtype=f32)
    shared["abv_c"] = bv
    qn = np.zeros((128, 1), f32)
    qn[0:64, 0] = np.asarray(inputs["L0_attn_b_q_norm_w"], dtype=f32)
    qn[64:128, 0] = qn[0:64, 0]
    shared["bqnw_c"] = qn
    kn = np.zeros((128, 1), f32)
    kn[0:64, 0] = np.asarray(inputs["L0_attn_b_k_norm_w"], dtype=f32)
    kn[64:128, 0] = kn[0:64, 0]
    shared["bknw_c"] = kn
    b1 = np.asarray(inputs["L0_ffn_b1"], dtype=f32)
    b1c = np.zeros((2, 128, 44), f32)
    for pr in range(NPAIR):
        off, wid = A_OFFS[pr], A_WIDS[pr]
        b1c[:, 0:wid, pr] = b1[:, off:off + wid]
        b1c[:, 0:wid, 22 + pr] = b1[:, 2730 + off:2730 + off + wid]
    shared["fb1_c"] = b1c
    return shared


def kernel(**inputs):
    shared = _prep_inputs(inputs)
    x = np.ascontiguousarray(np.asarray(inputs["L0_attn_a_hidden_states"],
                                        dtype=np.float32))
    nc = build()
    in_maps = [dict(shared, x=x[i]) for i in range(8)]
    res = run_bass_kernel_spmd(nc, in_maps, core_ids=list(range(8)))
    return np.stack([r["out"] for r in res.results], axis=0)


# version 17 (final)
# speedup vs baseline: 1.0157x; 1.0096x over previous
import numpy as np
import ml_dtypes
from contextlib import ExitStack

import concourse.bass as bass
import concourse.mybir as mybir
from concourse.bass_utils import run_bass_kernel_spmd

F32 = mybir.dt.float32
BF16 = mybir.dt.bfloat16
AF = mybir.ActivationFunctionType
ALU = mybir.AluOpType

T = 2048
D = 1024
NCH = 8
TG = 2050
NT = 16
NTB = 4
HID = 2730
A_OFFS = [128 * i for i in range(21)] + [2688]
A_WIDS = [128] * 21 + [42]
NPAIR = 22

ENGS = ("sync", "gpsimd", "scalar", "vector", "tensor")


class Sem:
    __slots__ = ("h", "n", "name")

    def __init__(self, h, name):
        self.h, self.n, self.name = h, 0, name


class Prog:
    def __init__(self):
        self.q = {e: [] for e in ENGS}

    def do(self, eng, emit, waits=(), inc=None, incn=1):
        ws = []
        for (s, n) in waits:
            if n <= 0:
                continue
            assert n <= s.n, f"wait {s.name}>={n} but only {s.n} emitted"
            ws.append((s, n))
        self.q[eng].append((tuple(ws), emit, inc, incn))
        if inc is not None:
            inc.n += incn
            return inc.n
        return None

    def realize(self, block):
        def mk(engname):
            def body(eng):
                for waits, emit, inc, incn in self.q[engname]:
                    for (s, n) in waits:
                        eng.wait_ge(s.h, n)
                    if emit is not None:
                        inst = emit(eng)
                        if inc is not None:
                            inst.then_inc(inc.h, incn)
            return body
        for e in ENGS:
            getattr(block, e)(mk(e))


def _host_rotary_tables():
    n, dh, win = T, 64, 512
    half = np.arange(0, dh, 2, dtype=np.float64)
    inv_freq = 1.0 / 10000.0 ** (half / dh)
    t = np.arange(n, dtype=np.float64)
    freqs = t[:, None] * inv_freq[None, :]
    freqs = np.concatenate([freqs, freqs], axis=-1)
    base = (half + 0.4 * dh) / (1.4 * dh)
    power = (t - n // 2) / float(win // 2)
    scale = base[None, :] ** power[:, None]
    scale = np.concatenate([scale, scale], axis=-1)
    cos, sin = np.cos(freqs), np.sin(freqs)
    sgn = np.concatenate([-np.ones(32), np.ones(32)])[None, :]
    return np.stack([cos * scale, sgn * sin * scale,
                     cos / scale, sgn * sin / scale]).astype(np.float32)


def _host_masks():
    m = np.zeros((8, 128, 512), dtype=np.float32)
    sl = np.arange(128)[:, None]
    tr = np.arange(512)[None, :]
    for r in range(4):
        m[r] = (tr <= 128 * r + sl).astype(np.float32)
    for r in range(4, 8):
        m[r] = (tr >= 128 * (r - 4) + sl).astype(np.float32)
    return m


def build(phases=(1, 2, 3, 4)):
    nc = bass.Bass("TRN2", target_bir_lowering=False)

    def din(name, shape):
        return nc.dram_tensor(name, list(shape), F32, kind="ExternalInput").ap()

    x_in = din("x", (T, D))
    aWqT = din("aWqT", (D, D))
    aWkT = din("aWkT", (D, 64))
    aWvT = din("aWvT", (D, 64))
    aWoT = din("aWoT", (D, D))
    abq_c = din("abq_c", (128, 8))
    abv_c = din("abv_c", (128, 1))
    abo_c = din("abo_c", (128, 8))
    bDQT = din("bDQT", (64, 512))
    bUQT = din("bUQT", (512, 64))
    bDKVT = din("bDKVT", (64, 1024))
    bUKVT = din("bUKVT", (1024, 128))
    bQRT = din("bQRT", (64, 64))
    bKRT = din("bKRT", (64, 64))
    bWOT = din("bWOT", (64, 64))
    bqnw_c = din("bqnw_c", (128, 1))
    bknw_c = din("bknw_c", (128, 1))
    fwqkv = din("fwqkv", (2, D, 1536))
    fqs = din("fqs", (2, 64))
    fks = din("fks", (2, 64))
    fwo = din("fwo", (2, 512, D))
    flng_c = din("flng_c", (2, 128, 8))
    flnb_c = din("flnb_c", (2, 128, 8))
    fw1 = din("fw1", (2, D, 5460))
    fb1_c = din("fb1_c", (2, 128, 44))
    fw2 = din("fw2", (2, HID, D))
    fb2_c = din("fb2_c", (2, 128, 8))
    ident_in = din("ident_in", (128, 128))
    rotc = din("rotc", (4, T, 64))
    masks_in = din("masks_in", (8, 128, 512))

    out = nc.dram_tensor("out", [T, D], F32, kind="ExternalOutput").ap()
    xpark = nc.dram_tensor("xpark", [NCH, 128, TG], F32, kind="Internal").ap()
    w1C = nc.dram_tensor("w1C", [2, NPAIR, 2, 128, NCH, 128], BF16,
                         kind="Internal").ap()
    wqkvC = nc.dram_tensor("wqkvC", [2, D, 1536], BF16, kind="Internal").ap()
    woC = nc.dram_tensor("woC", [2, 512, D], BF16, kind="Internal").ap()
    w2C = nc.dram_tensor("w2C", [2, HID, D], BF16, kind="Internal").ap()

    es = ExitStack()

    def sb(name, shape, dt=F32):
        return es.enter_context(nc.sbuf_tensor(name, list(shape), dt))

    X = sb("X", [128, NCH, TG], F32)
    B = sb("B", [128, NCH * TG], BF16)
    AR1 = sb("AR1", [128, 8192], BF16)
    AR2 = sb("AR2", [128, 4096], BF16)
    V65 = sb("V65", [128, 4160], BF16)
    WSLOT = sb("WSLOT", [128, NCH, 1536], BF16)
    WSLOT2 = sb("WSLOT2", [128, 4096], BF16)
    ROT = sb("ROT", [128, 4, NT, 64], BF16)
    MASKS = sb("MASKS", [128, 8, 512], BF16)
    KD = sb("KD", [128, 2, T], BF16)
    PT = sb("PT", [128, 4, 512], BF16)
    IDENT = sb("IDENT", [128, 128], F32)
    IDENTB = sb("IDENTB", [128, 128], BF16)
    BC1 = sb("BC1", [128, 512], F32)
    BC2 = sb("BC2", [128, 512], F32)
    TMP32 = sb("TMP32", [128, 512], F32)
    TMPA = sb("TMPA", [128, 512], BF16)
    TMPB = sb("TMPB", [128, 512], BF16)
    TMPC = sb("TMPC", [128, 512], BF16)
    TMPE = sb("TMPE", [128, 512], BF16)
    NRM3 = sb("NRM3", [128, 8], F32)
    NRM4 = sb("NRM4", [128, 8], F32)
    ROWS = sb("ROWS", [1, 1152], F32)
    STAT3 = sb("STAT3", [128, 3], BF16)
    HALF2 = sb("HALF2", [128, 2], BF16)
    ONESC = sb("ONESC", [1, 128], F32)
    ONESCB = sb("ONESCB", [1, 128], BF16)
    RECB = sb("RECB", [1, 1024], BF16)
    BIASC = sb("BIASC", [128, 20], F32)
    LGC = sb("LGC", [128, 8], F32)
    LBC = sb("LBC", [128, 8], F32)
    B2C = sb("B2C", [128, 8], F32)
    B1C = sb("B1C", [128, 44], F32)
    QSB = sb("QSB", [128, 2, 64], F32)
    QSBF = QSB[:, :, :].rearrange("p a b -> p (a b)")
    NRM = sb("NRM", [128, 8], F32)
    NRM2 = sb("NRM2", [128, 8], F32)
    ZSAVE = sb("ZSAVE", [128, NCH, 3], F32)
    EPSC = sb("EPSC", [4, 2], F32)

    SP = es.enter_context(nc.psum_tensor("SP", [128, 2048], F32))
    PVa = es.enter_context(nc.psum_tensor("PVa", [128, 512], F32))
    PVb = es.enter_context(nc.psum_tensor("PVb", [128, 512], F32))
    PBF = es.enter_context(nc.psum_tensor("PBF", [128, 1024], BF16))
    PBX = es.enter_context(nc.psum_tensor("PBX", [128, 512], F32))

    class PView:
        __slots__ = ("th", "c0")

        def __init__(self, th, c0):
            self.th, self.c0 = th, c0

        def __getitem__(self, idx):
            ps, fs = idx
            a = fs.start or 0
            b = fs.stop if fs.stop is not None else 512
            return self.th[ps, self.c0 + a:self.c0 + b]

    PB = [PView(SP, 0), PView(SP, 512), PView(SP, 1024), PView(SP, 1536),
          PView(PVa, 0), PView(PVb, 0)]

    sems = {}
    for name in ("t", "v", "s", "g0", "g1", "d0", "d1", "d2", "d3", "sp",
                 "spb", "dx", "w0a", "w0b", "w1a", "w1b", "w2a", "w2b",
                 "gc", "dq1", "dk1"):
        sems[name] = Sem(es.enter_context(nc.semaphore(name)), name)
    S = sems
    p = Prog()

    def dma(queue, sem, outap, inap, waits=()):
        p.do(queue, lambda e, o=outap, i=inap: e.dma_start(out=o, in_=i),
             waits=waits, inc=sem, incn=16)
        return sem.n

    def vop(emit, waits=(), incn=1):
        return p.do("vector", emit, waits=waits, inc=S["v"], incn=incn)

    def sop(emit, waits=(), incn=1):
        return p.do("scalar", emit, waits=waits, inc=S["s"], incn=incn)

    def top(emit, waits=(), incn=1):
        return p.do("tensor", emit, waits=waits, inc=S["t"], incn=incn)

    def gop(emit, waits=(), incn=1):
        return p.do("gpsimd", emit, waits=waits, inc=S["gc"], incn=incn)

    def Xs(c, lo, n):
        return X[:, c, lo:lo + n]

    def Bs(c, lo, n, p0=0, pn=128):
        return B[p0:p0 + pn, c * TG + lo: c * TG + lo + n]

    bank_guard = [None] * 6
    pbf_guard = [None]
    pbx_guard = [None]

    def bw(i):
        g = bank_guard[i]
        return [g] if g else []

    def pbfw():
        g = pbf_guard[0]
        return [g] if g else []

    def pbxw():
        g = pbx_guard[0]
        return [g] if g else []

    def barrier():
        fence = [(sm, sm.n) for sm in S.values()]
        for e in ENGS:
            for (s, n) in fence:
                p.do(e, None, waits=[(s, n)])
        return fence

    dma("sync", S["d0"], IDENT[:, :], ident_in[:, :])
    dma("gpsimd", S["g0"], IDENTB[:, :], ident_in[:, :])
    for r in range(8):
        dma("gpsimd", S["g0"], MASKS[:, r, :], masks_in[r, :, :])
    g_const = S["g0"].n

    vop(lambda e: e.memset(ONESC[:, :], 1.0))
    vop(lambda e: e.memset(ONESCB[:, :], 1.0), waits=[(S["v"], S["v"].n)])
    vop(lambda e: e.memset(STAT3[:, 0:1], 1.0 / 1024.0),
        waits=[(S["v"], S["v"].n)])
    vop(lambda e: e.memset(STAT3[:, 1:2], 0.0),
        waits=[(S["v"], S["v"].n)])
    vop(lambda e: e.memset(STAT3[:, 2:3], 1.0 / 1024.0),
        waits=[(S["v"], S["v"].n)])
    vop(lambda e: e.memset(HALF2[:, :], 0.0),
        waits=[(S["v"], S["v"].n)])
    vop(lambda e: e.memset(HALF2[0:64, 0:1], 1.0),
        waits=[(S["v"], S["v"].n)])
    vop(lambda e: e.memset(HALF2[64:128, 1:2], 1.0),
        waits=[(S["v"], S["v"].n)])
    vop(lambda e: e.memset(EPSC[:, 0:1], 1e-5),
        waits=[(S["v"], S["v"].n)])
    vop(lambda e: e.memset(EPSC[:, 1:2], 1.1920929e-07),
        waits=[(S["v"], S["v"].n)])
    vop(lambda e: e.memset(X[:, :, 0:1], 0.0),
        waits=[(S["v"], S["v"].n)])
    v_guard = vop(lambda e: e.memset(X[:, :, 2049:2050], 0.0),
                  waits=[(S["v"], S["v"].n)])

    dma("sync", S["d0"], BIASC[:, 0:8], abq_c[:, :])
    dma("sync", S["d0"], BIASC[:, 8:16], abo_c[:, :])
    dma("sync", S["d0"], BIASC[:, 16:17], abv_c[:, :])
    dma("sync", S["d0"], BIASC[:, 17:18], bqnw_c[:, :])
    d_bias = dma("sync", S["d0"], BIASC[:, 18:19], bknw_c[:, :])

    t_x0 = S["t"].n
    v_x0 = S["v"].n
    xd = [S["dx"], S["d1"], S["d2"], S["d3"]]
    xslot = [BC1[:, 0:128], BC1[:, 128:256], BC2[:, 0:128], BC2[:, 128:256]]
    dcnt = [0, 0, 0, 0]
    for c in range(NCH):
        for tt in range(NT):
            i = c * NT + tt
            k = i % 4
            waits = [(S["t"], t_x0 + i - 3)] if i >= 4 else []
            dcnt[k] = dma("sync", xd[k], xslot[k],
                          x_in[128 * tt:128 * (tt + 1), 128 * c:128 * (c + 1)],
                          waits=waits)
            tn = top(lambda e, s=xslot[k], pb=PB[i % 2]: e.transpose(
                pb[:, 0:128], s, IDENT[:, :]),
                waits=[(xd[k], dcnt[k]), (S["v"], v_guard),
                       (S["v"], v_x0 + i - 1)] +
                ([(S["d0"], d_bias)] if i == 0 else []))
            vop(lambda e, pb=PB[i % 2], c=c, tt=tt: e.tensor_copy(
                Xs(c, 1 + 128 * tt, 128), pb[:, 0:128]),
                waits=[(S["t"], tn)])
    v_xload = S["v"].n
    bank_guard[0] = (S["v"], v_xload)
    bank_guard[1] = (S["v"], v_xload)

    dma("gpsimd", S["g0"], WSLOT[:, :, 1024:1088],
        aWkT[:, :].rearrange("(c p) o -> p c o", p=128))
    dma("gpsimd", S["g0"], WSLOT[:, :, 1088:1152],
        aWvT[:, :].rearrange("(c p) o -> p c o", p=128))
    g_awq = dma("gpsimd", S["g0"], WSLOT[:, :, 0:1024],
                aWqT[:, :].rearrange("(c p) o -> p c o", p=128))

    for l in range(2):
        for c in range(NCH):
            dma("gpsimd", S["g1"], wqkvC[l, 128 * c:128 * (c + 1), :],
                fwqkv[l, 128 * c:128 * (c + 1), :])
        for c in range(4):
            dma("gpsimd", S["g1"], woC[l, 128 * c:128 * (c + 1), :],
                fwo[l, 128 * c:128 * (c + 1), :])
        for pr in range(NPAIR):
            off, wid = A_OFFS[pr], A_WIDS[pr]
            for half in range(2):
                hoff = 2730 * half
                dma("gpsimd", S["g1"],
                    bass.AP(w1C.tensor,
                            (((l * NPAIR + pr) * 2 + half) * 128 * NCH * 128),
                            [[NCH * 128, 128], [128, NCH], [1, wid]]),
                    bass.AP(fw1.tensor, l * D * 5460 + hoff + off,
                            [[5460, 128], [128 * 5460, NCH], [1, wid]]))
        for c in range(22):
            w = min(128, HID - 128 * c)
            dma("gpsimd", S["g1"], w2C[l, 128 * c:128 * c + w, :],
                fw2[l, 128 * c:128 * c + w, :])
    g_wcache = S["g1"].n

    bc_guard = [None]

    def emit_stats_block(get_src, eps, dep_waits):
        RM = ROWS[:, 0:512]
        RV = ROWS[:, 512:1024]
        RI = ROWS[:, 0:512]
        sq_t = [0, 0]
        for c in range(NCH):
            src = get_src(c)
            sq = TMPA[:, 0:512] if c % 2 == 0 else TMPB[:, 0:512]
            w_sq = list(dep_waits) + [(S["s"], S["s"].n)]
            if c > 1:
                w_sq.append((S["t"], sq_t[c % 2]))
            scnt = sop(lambda e, s=src, sq=sq: e.activation(sq, s,
                                                            AF.Square),
                       waits=w_sq)
            top(lambda e, s=src, st=(c == 0), sp=(c == NCH - 1): e.matmul(
                PB[4][0:1, 0:512], STAT3[:, 0:1], s, start=st, stop=sp),
                waits=(bw(4) + bw(5) + list(dep_waits)) if c == 0 else [])
            sq_t[c % 2] = top(lambda e, sq=sq, st=(c == 0),
                              sp=(c == NCH - 1):
                              e.matmul(PB[5][0:1, 0:512], STAT3[:, 0:1], sq,
                                       start=st, stop=sp),
                              waits=[(S["s"], scnt)])
        last_sq_t = sq_t[(NCH - 1) % 2]
        v1 = vop(lambda e: e.tensor_copy(RM, PB[4][0:1, 0:512]),
                 waits=[(S["t"], last_sq_t), (S["v"], S["v"].n)])
        v1a = vop(lambda e: e.tensor_copy(RV, PB[5][0:1, 0:512]),
                  waits=[(S["v"], v1)])
        bank_guard[4] = (S["v"], v1)
        bank_guard[5] = (S["v"], v1a)
        t1 = top(lambda e: e.matmul(PBX[:, :], ONESC[:, :], RM,
                                    start=True, stop=True),
                 waits=pbxw() + [(S["v"], v1)])
        w_bc = [bc_guard[0]] if bc_guard[0] else []
        bc_guard[0] = None
        v4 = vop(lambda e: e.tensor_copy(BC2[:, :], PBX[:, :]),
                 waits=[(S["t"], t1), (S["v"], S["v"].n)] + w_bc)
        v1b = vop(lambda e: e.tensor_mul(RI, RM, RM),
                  waits=[(S["v"], v1a), (S["t"], t1)])
        v2 = vop(lambda e: e.tensor_sub(RV, RV, RI),
                 waits=[(S["v"], v1b)])
        s2 = sop(lambda e: e.activation(RV, RV, AF.Ln,
                                        bias=EPSC[0:1, 0:1]),
                 waits=[(S["v"], v2), (S["s"], S["s"].n)])
        s3 = sop(lambda e: e.activation(RI, RV, AF.Exp, scale=-0.5),
                 waits=[(S["s"], s2)])
        t2 = top(lambda e: e.matmul(PBX[:, :], ONESC[:, :], RI,
                                    start=True, stop=True),
                 waits=[(S["v"], v4), (S["s"], s3)])
        v5 = vop(lambda e: e.tensor_copy(BC1[:, :], PBX[:, :]),
                 waits=[(S["t"], t2), (S["v"], S["v"].n)])
        pbx_guard[0] = (S["v"], v5)
        return v5

    def rowbc(row_ap, dst, width, extra_waits=()):
        t1 = top(lambda e, r=row_ap, w=width: e.matmul(
            PBX[:, 0:w], ONESC[:, :], r, start=True, stop=True),
            waits=pbxw() + list(extra_waits))
        v1 = vop(lambda e, d=dst, w=width: e.tensor_copy(d, PBX[:, 0:w]),
                 waits=[(S["t"], t1), (S["v"], S["v"].n)])
        pbx_guard[0] = (S["v"], v1)
        return v1

    def ln_materialize(dst_fn, v_bc, lo):
        for c in range(NCH):
            v1 = vop(lambda e, c=c, lo=lo: e.tensor_sub(
                TMP32[:, :], Xs(c, lo, 512), BC2[:, :]),
                waits=[(S["v"], v_bc if c == 0 else S["v"].n)])
            vop(lambda e, c=c: e.tensor_mul(dst_fn(c), TMP32[:, :],
                                            BC1[:, :]),
                waits=[(S["v"], v1)])
        return S["v"].n

    def entry_ln(mat_dst, extra_mat_waits=None, gp=False, inplace=False):
        v_snap = [0] * NTB
        s_snap = [0] * NTB
        v_mat = [0] * NTB
        g_mat = [0] * NTB
        vchunks = range(0, 5 if gp else NCH)
        gchunks = range(5, NCH) if gp else []

        def snap(tb):
            lo = 1 + 512 * tb
            for c in range(4):
                s_snap[tb] = sop(lambda e, c=c, lo=lo: e.copy(
                    Bs(c, lo, 512), Xs(c, lo, 512)),
                    waits=[(S["s"], S["s"].n)])
            for c in range(4, NCH):
                v_snap[tb] = vop(lambda e, c=c, lo=lo: e.tensor_copy(
                    Bs(c, lo, 512), Xs(c, lo, 512)),
                    waits=[(S["v"], S["v"].n)])

        def stats(tb):
            lo = 1 + 512 * tb
            deps = [(S["v"], v_snap[tb]), (S["s"], s_snap[tb])]
            return emit_stats_block(lambda c, lo=lo: Bs(c, lo, 512), 1e-5,
                                    deps)

        def mat(tb, v_bc):
            lo = 1 + 512 * tb
            w_extra = extra_mat_waits(tb) if extra_mat_waits else []
            for c in vchunks:
                if inplace:
                    v1 = vop(lambda e, c=c, lo=lo: e.tensor_sub(
                        Xs(c, lo, 512), Xs(c, lo, 512), BC2[:, :]),
                        waits=[(S["v"], v_bc if c == 0 else S["v"].n)] +
                        (w_extra if c == 0 else []))
                    v_mat[tb] = vop(lambda e, c=c, lo=lo: e.tensor_mul(
                        Xs(c, lo, 512), Xs(c, lo, 512), BC1[:, :]),
                        waits=[(S["v"], v1)])
                else:
                    v1 = vop(lambda e, c=c, lo=lo: e.tensor_sub(
                        TMP32[:, :], Xs(c, lo, 512), BC2[:, :]),
                        waits=[(S["v"], v_bc if c == 0 else S["v"].n)] +
                        (w_extra if c == 0 else []))
                    v_mat[tb] = vop(lambda e, c=c, lo=lo: e.tensor_mul(
                        mat_dst(c, lo), TMP32[:, :], BC1[:, :]),
                        waits=[(S["v"], v1)])
            for c in gchunks:
                if inplace:
                    g1 = gop(lambda e, c=c, lo=lo: e.tensor_sub(
                        Xs(c, lo, 512), Xs(c, lo, 512), BC2[:, :]),
                        waits=[(S["gc"], S["gc"].n), (S["v"], v_bc)] +
                        (w_extra if c == 5 else []))
                    g_mat[tb] = gop(lambda e, c=c, lo=lo: e.tensor_mul(
                        Xs(c, lo, 512), Xs(c, lo, 512), BC1[:, :]),
                        waits=[(S["gc"], g1)])
                    continue
                for hh in range(4):
                    o = 128 * hh
                    g1 = gop(lambda e, c=c, lo=lo, o=o: e.tensor_sub(
                        QSBF[:, :], Xs(c, lo + o, 128), BC2[:, o:o + 128]),
                        waits=[(S["gc"], S["gc"].n), (S["v"], v_bc)] +
                        (w_extra if c == 5 and hh == 0 else []))
                    g_mat[tb] = gop(
                        lambda e, c=c, lo=lo, o=o: e.tensor_mul(
                            mat_dst(c, lo)[:, o:o + 128],
                            QSBF[:, :], BC1[:, o:o + 128]),
                        waits=[(S["gc"], g1)])

        snap(0)
        vbc = stats(0)
        for tb in range(NTB):
            if tb + 1 < NTB:
                snap(tb + 1)
            mat(tb, vbc)
            if tb + 1 < NTB:
                if gp:
                    bc_guard[0] = (S["gc"], g_mat[tb])
                vbc = stats(tb + 1)
        return v_mat, g_mat

    def ctx_norm(pv_bank, dst_ap, t_pv):
        s1 = sop(lambda e, b=pv_bank: e.activation(
            ROWS[:, 0:512], PB[b][64:65, :], AF.Ln),
            waits=[(S["t"], t_pv), (S["s"], S["s"].n)])
        s2 = sop(lambda e: e.activation(RECB[:, 0:512], ROWS[:, 0:512],
                                        AF.Exp, scale=-1.0),
                 waits=[(S["s"], s1)])
        tb2 = top(lambda e: e.matmul(PBX[0:64, :], ONESCB[0:1, 0:64],
                                     RECB[:, 0:512], start=True, stop=True),
                  waits=pbxw() + [(S["s"], s2)])
        s3 = sop(lambda e: e.copy(TMP32[0:64, :], PBX[0:64, :]),
                 waits=[(S["t"], tb2), (S["s"], S["s"].n),
                        (S["v"], S["v"].n)])
        pbx_guard[0] = (S["s"], s3)
        vc = vop(lambda e, d=dst_ap, b=pv_bank: e.tensor_mul(
            d, PB[b][0:64, :], TMP32[0:64, :]),
            waits=[(S["s"], s3), (S["v"], S["v"].n)])
        bank_guard[pv_bank] = (S["v"], vc)
        return vc

    def ctx_norm2(dstA, dstB, t_pv):
        s1 = sop(lambda e: e.activation(ROWS[:, 0:512], PVa[64:65, :],
                                        AF.Ln),
                 waits=[(S["t"], t_pv), (S["s"], S["s"].n)])
        s1 = sop(lambda e: e.activation(ROWS[:, 512:1024], PVb[64:65, :],
                                        AF.Ln),
                 waits=[(S["s"], s1)])
        s2 = sop(lambda e: e.activation(RECB[:, :], ROWS[:, 0:1024],
                                        AF.Exp, scale=-1.0),
                 waits=[(S["s"], s1)])
        tb2 = top(lambda e: e.matmul(PBX[0:64, :], ONESCB[0:1, 0:64],
                                     RECB[:, 0:512], start=True, stop=True),
                  waits=pbxw() + [(S["s"], s2)])
        tb2 = top(lambda e: e.matmul(PBX[64:128, :], ONESCB[0:1, 64:128],
                                     RECB[:, 512:1024], start=True,
                                     stop=True))
        s3 = sop(lambda e: e.copy(TMP32[:, :], PBX[:, :]),
                 waits=[(S["t"], tb2), (S["s"], S["s"].n),
                        (S["v"], S["v"].n)])
        pbx_guard[0] = (S["s"], s3)
        vcA = vop(lambda e, d=dstA: e.tensor_mul(d, PVa[0:64, :],
                                                 TMP32[0:64, :]),
                  waits=[(S["s"], s3), (S["v"], S["v"].n)])
        vcB = vop(lambda e, d=dstB: e.tensor_mul(d, PVb[0:64, :],
                                                 TMP32[64:128, :]),
                  waits=[(S["v"], vcA)])
        bank_guard[4] = (S["v"], vcB)
        bank_guard[5] = (S["v"], vcB)
        return vcB

    def phase_attn_a():
        barrier()
        v_mat, _ = entry_ln(lambda c, lo: Bs(c, lo, 512))
        v_y1 = v_mat[NTB - 1]

        g_wq = g_awq

        for tb in range(NTB):
            lo = 1 + 512 * tb
            for c in range(NCH):
                top(lambda e, c=c, lo=lo: e.matmul(
                    PB[0][0:64, :], WSLOT[:, c, 1024:1088], Bs(c, lo, 512),
                    start=(c == 0), stop=(c == NCH - 1)),
                    waits=(bw(0) + [(S["g0"], g_wq), (S["v"], v_mat[tb])])
                    if c == 0 else [])
            tk = S["t"].n
            vk = vop(lambda e, tb=tb: e.tensor_copy(
                KD[0:64, 0, 512 * tb:512 * (tb + 1)], PB[0][0:64, :]),
                waits=[(S["t"], tk), (S["v"], S["v"].n)])
            bank_guard[0] = (S["v"], vk)
            for c in range(NCH):
                top(lambda e, c=c, lo=lo: e.matmul(
                    PB[1][0:64, :], WSLOT[:, c, 1088:1152], Bs(c, lo, 512),
                    start=(c == 0), stop=(c == NCH - 1)),
                    waits=(bw(1) + [(S["v"], v_mat[tb])]) if c == 0 else [])
            tv = S["t"].n
            vv = vop(lambda e: e.tensor_scalar_add(
                TMPB[0:64, 0:512], PB[1][0:64, :], BIASC[0:64, 16:17]),
                waits=[(S["t"], tv), (S["g0"], g_wq),
                       (S["v"], S["v"].n)])
            bank_guard[1] = (S["v"], vv)
            for q in range(4):
                tt = 4 * tb + q
                tq = top(lambda e, q=q: e.transpose(
                    PBF[:, 0:64], TMPB[0:64, 128 * q:128 * (q + 1)],
                    IDENTB[0:64, 0:64]),
                    waits=[(S["v"], vv)] + pbfw())
                vq = vop(lambda e, tt=tt: e.tensor_copy(
                    V65[:, 65 * tt:65 * tt + 64], PBF[:, 0:64]),
                    waits=[(S["t"], tq), (S["v"], S["v"].n)])
                pbf_guard[0] = (S["v"], vq)
        v_ones = vop(lambda e: e.memset(
            bass.AP(V65, 64, [[4160, 128], [65, NT], [1, 1]]), 1.0),
            waits=[(S["v"], S["v"].n)])

        for tb in range(NTB):
            lo = 1 + 512 * tb
            stage_v = []
            for ot in range(NCH):
                bank = 2 + ot % 2
                for c in range(NCH):
                    top(lambda e, c=c, ot=ot, lo=lo, bank=bank: e.matmul(
                        PB[bank][:, :], WSLOT[:, c, 128 * ot:128 * (ot + 1)],
                        Bs(c, lo, 512), start=(c == 0), stop=(c == NCH - 1)),
                        waits=(bw(bank) + [(S["g0"], g_wq),
                                           (S["d0"], d_bias)])
                        if c == 0 else [])
                tq = S["t"].n
                vq = vop(lambda e, ot=ot, bank=bank: e.tensor_scalar(
                    AR2[:, 512 * ot:512 * (ot + 1)], PB[bank][:, :],
                    BIASC[:, ot:ot + 1], 0.125, ALU.add, ALU.mult),
                    waits=[(S["t"], tq), (S["v"], S["v"].n)])
                bank_guard[bank] = (S["v"], vq)
                stage_v.append(vq)
            for ot in range(NCH):
                vop(lambda e, ot=ot, lo=lo: e.tensor_copy(
                    Bs(ot, lo, 512), AR2[:, 512 * ot:512 * (ot + 1)]),
                    waits=[(S["v"], stage_v[ot]), (S["v"], S["v"].n)])
        v_qdone = S["v"].n
        d_kdup = dma("sync", S["d0"], KD[64:128, 0, 0:T], KD[0:64, 0, 0:T],
                     waits=[(S["v"], v_qdone)])
        g_wo = dma("gpsimd", S["g0"], WSLOT[:, :, 0:1024],
                   aWoT[:, :].rearrange("(c p) o -> p c o", p=128),
                   waits=[(S["v"], v_qdone)])

        for tb in range(NTB):
            lo = 1 + 512 * tb
            prev_ctx = [None]
            for hp in range(NCH):
                sexp = [0, 0]
                tpv = [0, 0]
                tS = [0] * NT

                def emit_S(j, hp=hp, lo=lo):
                    sl = j % 2
                    w0 = []
                    if j < 2:
                        w0 += bw(2 * sl) + bw(2 * sl + 1)
                    else:
                        w0 += [(S["s"], sexp[sl])]
                    if j == 0 and hp == 0 and tb == 0:
                        w0 += [(S["d0"], d_kdup), (S["v"], v_ones)]
                    top(lambda e, j=j, hp=hp, lo=lo, sl=sl: e.matmul(
                        SP[:, 1024 * sl:1024 * sl + 512],
                        KD[0:64, 0, 128 * j:128 * (j + 1)],
                        Bs(hp, lo, 512, 0, 64), start=True, stop=True),
                        waits=w0)
                    tS[j] = top(lambda e, j=j, hp=hp, lo=lo, sl=sl: e.matmul(
                        SP[:, 1024 * sl + 512:1024 * sl + 1024],
                        KD[64:128, 0, 128 * j:128 * (j + 1)],
                        Bs(hp, lo, 512, 64, 64), start=True, stop=True))

                emit_S(0)
                emit_S(1)
                if prev_ctx[0] is not None:
                    prev_ctx[0]()
                for j in range(NT):
                    sl = j % 2
                    wE = [(S["t"], tS[j]), (S["s"], S["s"].n)]
                    if j >= 2:
                        wE.append((S["t"], tpv[sl]))
                    se = sop(lambda e, sl=sl: e.activation(
                        PT[:, 2 * sl:2 * sl + 2, :],
                        SP[:, 1024 * sl:1024 * sl + 1024], AF.Exp),
                        waits=wE)
                    sexp[sl] = se
                    bank_guard[2 * sl] = (S["s"], se)
                    bank_guard[2 * sl + 1] = (S["s"], se)
                    if j + 2 < NT:
                        emit_S(j + 2)
                    top(lambda e, j=j, sl=sl: e.matmul(
                        PVa[0:65, :], V65[:, 65 * j:65 * j + 65],
                        PT[:, 2 * sl, :], start=(j == 0),
                        stop=(j == NT - 1)),
                        waits=[(S["s"], se)] + (bw(4) if j == 0 else []))
                    tpv[sl] = top(lambda e, j=j, sl=sl: e.matmul(
                        PVb[0:65, :], V65[:, 65 * j:65 * j + 65],
                        PT[:, 2 * sl + 1, :], start=(j == 0),
                        stop=(j == NT - 1)),
                        waits=(bw(5) if j == 0 else []))
                t_pv = S["t"].n

                def mk_ctx(hp=hp, t_pv=t_pv):
                    def go():
                        ctx_norm2(AR2[0:64, 512 * hp:512 * hp + 512],
                                  AR2[64:128, 512 * hp:512 * hp + 512],
                                  t_pv)
                    return go
                prev_ctx[0] = mk_ctx()
            prev_ctx[0]()
            v_ctx = S["v"].n
            for ot in range(NCH):
                bank = ot % 2
                for c in range(NCH):
                    top(lambda e, c=c, ot=ot, bank=bank: e.matmul(
                        PB[bank][:, :], WSLOT[:, c, 128 * ot:128 * (ot + 1)],
                        AR2[:, 512 * c:512 * (c + 1)], start=(c == 0),
                        stop=(c == NCH - 1)),
                        waits=(bw(bank) + [(S["v"], v_ctx),
                                           (S["g0"], g_wo)])
                        if c == 0 else [])
                tn = S["t"].n
                vr = vop(lambda e, ot=ot, lo=lo, bank=bank:
                         e.scalar_tensor_tensor(
                             Xs(ot, lo, 512), PB[bank][:, :],
                             BIASC[:, 8 + ot:9 + ot], Xs(ot, lo, 512),
                             ALU.add, ALU.add),
                         waits=[(S["t"], tn), (S["v"], S["v"].n)])
                bank_guard[bank] = (S["v"], vr)

    def phase_ffn(app):
        barrier()
        pt_guard = [0]
        psem = [S["d0"], S["d1"], S["d2"], S["d3"]]
        d_park = [0] * NTB
        for tb in range(NTB):
            lo = 1 + 512 * tb
            for c in range(NCH):
                d_park[tb] = dma("sync", psem[tb],
                                 xpark[c, :, lo:lo + 512],
                                 Xs(c, lo, 512),
                                 waits=[(S["v"], S["v"].n)] if (tb == 0 and
                                                               c == 0) else [])
        v_mat, g_mat = entry_ln(
            lambda c, lo: Xs(c, lo, 512),
            extra_mat_waits=lambda tb: [(psem[tb], d_park[tb])], gp=True,
            inplace=True)
        v_z0 = v_mat[NTB - 1]
        g_z0 = g_mat[NTB - 1]

        def layer_loads(l):
            dma("sync", S["d1"], ROWS[:, 1024:1088], fqs[l, :][None, :],
                waits=[(S["s"], S["s"].n), (S["t"], S["t"].n)])
            d_qs = dma("sync", S["d1"], ROWS[:, 1088:1152],
                       fks[l, :][None, :])
            rowbc(ROWS[:, 1024:1088], QSB[:, 0, :], 64, [(S["d1"], d_qs)])
            v_qs = rowbc(ROWS[:, 1088:1152], QSB[:, 1, :], 64)
            g_rot = 0
            gate = [(S["v"], S["v"].n), (S["t"], S["t"].n)]
            for tbl in range(4):
                g_rot = dma("gpsimd", S["g0"], ROT[:, tbl, :, :],
                            rotc[tbl, :, :]
                            .rearrange("(t p) d -> p t d", p=128),
                            waits=gate)
                gate = ()
            g_wqkv = dma("sync", S["d2"], WSLOT[:, :, :],
                         wqkvC[l, :, :].rearrange("(c p) o -> p c o", p=128),
                         waits=[(S["g1"], g_wcache), (S["t"], S["t"].n)])
            g_wo = dma("sync", S["d2"],
                       WSLOT2[:, :].rearrange("p (c o) -> p c o", c=4),
                       woC[l, :, :].rearrange("(c p) o -> p c o", p=128))
            return g_wqkv, g_wo, g_rot, v_qs

        def layer_scales(g_rot, v_qs):
            v_tbl = 0
            for tbl in range(4):
                qk = 0 if tbl < 2 else 1
                for tt in range(NT):
                    if tbl % 2 == 0:
                        v_tbl = vop(lambda e, tbl=tbl, tt=tt, qk=qk:
                                    e.tensor_mul(ROT[:, tbl, tt, :],
                                                 ROT[:, tbl, tt, :],
                                                 QSB[:, qk, :]),
                                    waits=[(S["g0"], g_rot), (S["v"], v_qs),
                                           (S["v"], S["v"].n)])
                    else:
                        v_tbl = vop(lambda e, tbl=tbl, tt=tt, qk=qk:
                                    e.tensor_mul(ROT[:, tbl, tt, 0:32],
                                                 ROT[:, tbl, tt, 0:32],
                                                 QSB[:, qk, 32:64]),
                                    waits=[(S["g0"], g_rot), (S["v"], v_qs),
                                           (S["v"], S["v"].n)])
                        v_tbl = vop(lambda e, tbl=tbl, tt=tt, qk=qk:
                                    e.tensor_mul(ROT[:, tbl, tt, 32:64],
                                                 ROT[:, tbl, tt, 32:64],
                                                 QSB[:, qk, 0:32]),
                                    waits=[(S["v"], S["v"].n)])
            return v_tbl

        pending = layer_loads(0)
        for l in range(2):
            fence = barrier()
            g_wqkv, g_wo, g_rot_p, v_qs_p = pending
            dma("sync", S["d3"], LGC[:, :], flng_c[l, :, :],
                waits=[(S["v"], S["v"].n)])
            dma("sync", S["d3"], LBC[:, :], flnb_c[l, :, :])
            dma("sync", S["d3"], B2C[:, :], fb2_c[l, :, :])
            d_cols = dma("sync", S["d3"], B1C[:, :], fb1_c[l, :, :])

            ZBOFF = 0
            QTOFF = 8256
            VXOFF = 12352

            def ZBr(c, ring, lo2, n, p0=0, pn=128):
                off = ZBOFF + 1032 * c + 516 * ring + lo2
                return B[p0:p0 + pn, off:off + n]

            def KTr(ring, cb, lo2, n, p0=0, pn=128):
                off = 2048 * ring + 512 * cb + lo2
                return AR1[p0:p0 + pn, off:off + n]

            def QTc(ring, cb, lo2, n, p0=0, pn=128):
                off = QTOFF + 2048 * ring + 512 * cb + lo2
                return B[p0:p0 + pn, off:off + n]

            def Qtm(tt, lo2, n):
                off = 6144 + 512 * tt + lo2
                return AR1[:, off:off + n]

            def Ktm(tt, lo2, n):
                return AR2[:, 512 * tt + lo2:512 * tt + lo2 + n]

            def CTXc(cb, p0=0, pn=128):
                return AR2[p0:p0 + pn, 2048 + 512 * cb:2048 + 512 * (cb + 1)]

            def v65s(slot, h, n=65):
                if slot < 8:
                    return V65[:, 520 * slot + 65 * h:520 * slot + 65 * h + n]
                off = VXOFF + 520 * (slot - 8) + 65 * h
                return B[:, off:off + n]

            v_tbl = layer_scales(g_rot_p, v_qs_p)
            v65o = vop(lambda e: e.memset(
                bass.AP(V65, 64, [[4160, 128], [520, 8], [65, 8], [1, 1]]),
                1.0), waits=[(S["v"], S["v"].n)])
            v65o = vop(lambda e: e.memset(
                bass.AP(B, VXOFF + 64,
                        [[NCH * TG, 128], [520, 4], [65, 8], [1, 1]]),
                1.0), waits=[(S["v"], v65o)])

            zb_guard = [0, 0]
            qt_guard = [0, 0]
            kt_guard = [0, 0, 0]
            v65_guard = [0, 0, 0]
            prep_state = {}

            def prepA(w):
                wlo = 1 + 512 * w
                ring = w % 2
                wid = 514 if w == 3 else 516
                v_zb = 0
                for c in range(NCH):
                    v_zb = vop(lambda e, c=c, wlo=wlo, ring=ring, wid=wid:
                               e.tensor_copy(ZBr(c, ring, 0, wid),
                                             Xs(c, wlo - 1, wid)),
                               waits=[(S["v"], S["v"].n),
                                      (S["t"], zb_guard[ring])] +
                               ([(S["gc"], g_z0)] if c == 4 else []))
                v_rotq = [0] * 4
                g_rotk = [0] * 4
                for q in range(4):
                    for part in range(3):
                        bank = part
                        for c in range(NCH):
                            sh = 0 if c < 4 else -1
                            top(lambda e, c=c, q=q, part=part, bank=bank,
                                sh=sh, ring=ring: e.matmul(
                                    PB[bank][:, :],
                                    ZBr(c, ring, 1 + 128 * q + sh, 128),
                                    WSLOT[:, c, 512 * part:512 * (part + 1)],
                                    start=(c == 0), stop=(c == NCH - 1)),
                                waits=(bw(bank) +
                                       [(S["d2"], g_wo),
                                        (S["v"], v_zb)]) if c == 0 else [])
                    t_qkv = S["t"].n
                    slot = (4 * w + q) % 12
                    vq = vop(lambda e, q=q: e.tensor_copy(
                        Qtm(q, 0, 512), PB[0][:, :]),
                        waits=[(S["t"], t_qkv), (S["v"], S["v"].n)])
                    bank_guard[0] = (S["v"], vq)
                    sk = sop(lambda e, q=q: e.copy(
                        Ktm(q, 0, 512), PB[1][:, :]),
                        waits=[(S["t"], t_qkv), (S["s"], S["s"].n),
                               (S["gc"], S["gc"].n)])
                    bank_guard[1] = (S["s"], sk)
                    vv = vop(lambda e, slot=slot: e.tensor_copy(
                        bass.AP(V65.th if False else V65, 520 * slot,
                                [[4160, 128], [65, 8], [1, 64]])
                        if slot < 8 else
                        bass.AP(B, VXOFF + 520 * (slot - 8),
                                [[NCH * TG, 128], [65, 8], [1, 64]]),
                        PB[2][:, :].rearrange("p (h d) -> p h d", h=8)),
                        waits=[(S["v"], vq), (S["v"], v65o),
                               (S["t"], v65_guard[w % 3])])
                    bank_guard[2] = (S["v"], vv)
                    tt_ = 4 * w + q

                    vx = vop(lambda e, q=q: e.tensor_mul(
                        TMP32[:, :], Qtm(q, 0, 512), Qtm(q, 0, 512)),
                        waits=[(S["v"], vq), (S["v"], S["v"].n)])
                    vx = vop(lambda e: e.tensor_reduce(
                        NRM[:, :],
                        TMP32[:, :].rearrange("p (h d) -> p h d", h=8),
                        mybir.AxisListType.X, ALU.add),
                        waits=[(S["v"], vx)])
                    vx = vop(lambda e: e.tensor_scalar_max(
                        NRM[:, :], NRM[:, :], 1e-24),
                        waits=[(S["v"], vx)])
                    sx = sop(lambda e: e.activation(
                        NRM[:, :], NRM[:, :], AF.Ln),
                        waits=[(S["v"], vx), (S["s"], S["s"].n)])
                    sxq = sop(lambda e: e.activation(
                        NRM2[:, :], NRM[:, :], AF.Exp, scale=-0.5),
                        waits=[(S["s"], sx)])
                    vx = vop(lambda e, q=q: e.tensor_mul(
                        TMPA[:, 0:512].rearrange("p (h d) -> p h d", h=8),
                        Qtm(q, 0, 512).rearrange("p (h d) -> p h d", h=8),
                        bass.AP(NRM2, 0, [[8, 128], [1, 8], [0, 64]])),
                        waits=[(S["v"], S["v"].n), (S["s"], sxq)])
                    vx = vop(lambda e, tt_=tt_: e.tensor_mul(
                        TMPB[:, 0:512].rearrange("p (h d) -> p h d", h=8),
                        TMPA[:, 0:512].rearrange("p (h d) -> p h d", h=8),
                        bass.AP(ROT, (0 * NT + tt_) * 64,
                                [[4 * NT * 64, 128], [0, 8], [1, 64]])),
                        waits=[(S["v"], vx), (S["v"], v_tbl)])
                    vx1 = vop(lambda e, q=q, tt_=tt_: e.tensor_mul(
                        Qtm(q, 0, 512).rearrange(
                            "p (h d) -> p h d", h=8)[:, :, 0:32],
                        TMPA[:, 0:512].rearrange(
                            "p (h d) -> p h d", h=8)[:, :, 32:64],
                        bass.AP(ROT, (1 * NT + tt_) * 64,
                                [[4 * NT * 64, 128], [0, 8], [1, 32]])),
                        waits=[(S["v"], vx)])
                    vx2 = vop(lambda e, q=q, tt_=tt_: e.tensor_mul(
                        Qtm(q, 0, 512).rearrange(
                            "p (h d) -> p h d", h=8)[:, :, 32:64],
                        TMPA[:, 0:512].rearrange(
                            "p (h d) -> p h d", h=8)[:, :, 0:32],
                        bass.AP(ROT, (1 * NT + tt_) * 64 + 32,
                                [[4 * NT * 64, 128], [0, 8], [1, 32]])),
                        waits=[(S["v"], vx1)])
                    v_rotq[q] = vop(lambda e, q=q: e.tensor_add(
                        Qtm(q, 0, 512), Qtm(q, 0, 512), TMPB[:, 0:512]),
                        waits=[(S["v"], vx2)])

                    gx = gop(lambda e, q=q: e.tensor_mul(
                        TMPE[:, :], Ktm(q, 0, 512), Ktm(q, 0, 512)),
                        waits=[(S["s"], sk), (S["gc"], S["gc"].n)])
                    vkr = vop(lambda e: e.tensor_reduce(
                        NRM3[:, :],
                        TMPE[:, :].rearrange("p (h d) -> p h d", h=8),
                        mybir.AxisListType.X, ALU.add),
                        waits=[(S["gc"], gx), (S["v"], S["v"].n)])
                    gx = gop(lambda e: e.tensor_scalar_max(
                        NRM3[:, :], NRM3[:, :], 1e-24),
                        waits=[(S["v"], vkr), (S["gc"], S["gc"].n)])
                    sx = sop(lambda e: e.activation(
                        NRM3[:, :], NRM3[:, :], AF.Ln),
                        waits=[(S["gc"], gx), (S["s"], S["s"].n)])
                    sxk = sop(lambda e: e.activation(
                        NRM4[:, :], NRM3[:, :], AF.Exp, scale=-0.5),
                        waits=[(S["s"], sx)])
                    gx = gop(lambda e, q=q: e.tensor_mul(
                        TMPC[:, 0:512].rearrange("p (h d) -> p h d", h=8),
                        Ktm(q, 0, 512).rearrange("p (h d) -> p h d", h=8),
                        bass.AP(NRM4, 0, [[8, 128], [1, 8], [0, 64]])),
                        waits=[(S["gc"], S["gc"].n), (S["s"], sxk)])
                    gx = gop(lambda e, tt_=tt_: e.tensor_mul(
                        TMPE[:, :].rearrange("p (h d) -> p h d", h=8),
                        TMPC[:, 0:512].rearrange("p (h d) -> p h d", h=8),
                        bass.AP(ROT, (2 * NT + tt_) * 64,
                                [[4 * NT * 64, 128], [0, 8], [1, 64]])),
                        waits=[(S["gc"], gx), (S["v"], v_tbl)])
                    gx1 = gop(lambda e, q=q, tt_=tt_: e.tensor_mul(
                        Ktm(q, 0, 512).rearrange(
                            "p (h d) -> p h d", h=8)[:, :, 0:32],
                        TMPC[:, 0:512].rearrange(
                            "p (h d) -> p h d", h=8)[:, :, 32:64],
                        bass.AP(ROT, (3 * NT + tt_) * 64,
                                [[4 * NT * 64, 128], [0, 8], [1, 32]])),
                        waits=[(S["gc"], gx)])
                    gx2 = gop(lambda e, q=q, tt_=tt_: e.tensor_mul(
                        Ktm(q, 0, 512).rearrange(
                            "p (h d) -> p h d", h=8)[:, :, 32:64],
                        TMPC[:, 0:512].rearrange(
                            "p (h d) -> p h d", h=8)[:, :, 0:32],
                        bass.AP(ROT, (3 * NT + tt_) * 64 + 32,
                                [[4 * NT * 64, 128], [0, 8], [1, 32]])),
                        waits=[(S["gc"], gx1)])
                    g_rotk[q] = gop(lambda e, q=q: e.tensor_add(
                        Ktm(q, 0, 512), Ktm(q, 0, 512), TMPE[:, :]),
                        waits=[(S["gc"], gx2)])
                prep_state[w] = (v_rotq, g_rotk)
                zb_guard[ring] = S["t"].n

            def prepB(w):
                v_rotq, g_rotk = prep_state[w]
                for q in range(4):
                    for cb in range(4):
                        tq = top(lambda e, q=q, cb=cb: e.transpose(
                            PBF[:, 0:128], Qtm(q, 128 * cb, 128),
                            IDENTB[:, :]),
                            waits=[(S["v"], v_rotq[q]),
                                   (S["t"], qt_guard[w % 2])] + pbfw())
                        vq2 = vop(lambda e, q=q, cb=cb, w=w: e.tensor_copy(
                            QTc(w % 2, cb, 128 * q, 128), PBF[:, 0:128]),
                            waits=[(S["t"], tq), (S["v"], S["v"].n)])
                        pbf_guard[0] = (S["v"], vq2)
                        tk2 = top(lambda e, q=q, cb=cb: e.transpose(
                            PBF[:, 128:256], Ktm(q, 128 * cb, 128),
                            IDENTB[:, :]),
                            waits=[(S["gc"], g_rotk[q]),
                                   (S["t"], kt_guard[w % 3])] + pbfw())
                        vk2 = vop(lambda e, q=q, cb=cb, w=w: e.tensor_copy(
                            KTr(w % 3, cb, 128 * q, 128), PBF[:, 128:256]),
                            waits=[(S["t"], tk2), (S["v"], S["v"].n)])
                        pbf_guard[0] = (S["v"], vk2)
                prep_state[w] = S["v"].n

            def attn_win(w):
                kts = list(range(8)) if w > 0 else list(range(4, 8))
                v_qkT = prep_state[w]
                prev_ctx2 = [None]
                for cb in range(4):
                    sfree = {}
                    tpv = {}
                    nj = len(kts)
                    tS = [0] * nj

                    def emit_S(jj, cb=cb, kts=kts, w=w, v_qkT=v_qkT):
                        kt = kts[jj]
                        ring = (w - 1) % 3 if kt < 4 else w % 3
                        ktt = kt % 4
                        sl = jj % 2
                        w0 = []
                        if jj < 2:
                            w0 += bw(2 * sl) + bw(2 * sl + 1)
                        else:
                            w0 += [(S["s"], sfree[sl])]
                        if jj == 0:
                            w0 += [(S["v"], v_qkT)]
                        top(lambda e, ring=ring, cb=cb, ktt=ktt, sl=sl,
                            w=w: e.matmul(
                                SP[:, 1024 * sl:1024 * sl + 512],
                                KTr(ring, cb, 128 * ktt, 128, 0, 64),
                                QTc(w % 2, cb, 0, 512, 0, 64),
                                start=True, stop=True),
                            waits=w0)
                        tS[jj] = top(
                            lambda e, ring=ring, cb=cb, ktt=ktt, sl=sl, w=w:
                            e.matmul(SP[:, 1024 * sl + 512:1024 * sl + 1024],
                                     KTr(ring, cb, 128 * ktt, 128, 64, 64),
                                     QTc(w % 2, cb, 0, 512, 64, 64),
                                     start=True, stop=True))

                    emit_S(0)
                    if nj > 1:
                        emit_S(1)
                    if prev_ctx2[0] is not None:
                        prev_ctx2[0]()
                    for jj, kt in enumerate(kts):
                        sl = jj % 2
                        wE = [(S["t"], tS[jj]), (S["s"], S["s"].n)]
                        if jj >= 2:
                            wE.append((S["t"], tpv[sl]))
                        se = sop(lambda e, sl=sl: e.activation(
                            PT[:, 2 * sl:2 * sl + 2, :],
                            SP[:, 1024 * sl:1024 * sl + 1024], AF.Exp,
                            scale=8.0), waits=wE)
                        sfree[sl] = se
                        bank_guard[2 * sl] = (S["s"], se)
                        bank_guard[2 * sl + 1] = (S["s"], se)
                        if jj + 2 < nj:
                            emit_S(jj + 2)
                        vm = vop(lambda e, sl=sl, kt=kt: e.tensor_mul(
                            PT[:, 2 * sl:2 * sl + 2, :],
                            PT[:, 2 * sl:2 * sl + 2, :],
                            bass.AP(MASKS, kt * 512,
                                    [[8 * 512, 128], [0, 2], [1, 512]])),
                            waits=[(S["s"], se), (S["v"], S["v"].n)])
                        wv = w - 1 if kt < 4 else w
                        ktt = kt % 4
                        slot = (4 * wv + ktt) % 12
                        top(lambda e, cb=cb, slot=slot, jj=jj, nj=nj, sl=sl:
                            e.matmul(
                                PVa[0:65, :], v65s(slot, 2 * cb),
                                PT[:, 2 * sl, :], start=(jj == 0),
                                stop=(jj == nj - 1)),
                            waits=[(S["v"], vm)] + (bw(4) if jj == 0 else []))
                        tpv[sl] = top(
                            lambda e, cb=cb, slot=slot, jj=jj, nj=nj, sl=sl:
                            e.matmul(
                                PVb[0:65, :], v65s(slot, 2 * cb + 1),
                                PT[:, 2 * sl + 1, :], start=(jj == 0),
                                stop=(jj == nj - 1)),
                            waits=(bw(5) if jj == 0 else []))
                    t_pv = S["t"].n

                    def mk_ctx2(cb=cb, t_pv=t_pv):
                        def go():
                            ctx_norm2(CTXc(cb, 0, 64), CTXc(cb, 64, 64),
                                      t_pv)
                        return go
                    prev_ctx2[0] = mk_ctx2()
                prev_ctx2[0]()
                qt_guard[w % 2] = S["t"].n
                kt_guard[(w - 1) % 3] = S["t"].n
                if w > 0:
                    v65_guard[(w - 1) % 3] = S["t"].n

            def wo_win(w):
                wlo = 1 + 512 * w
                v_ctx = S["v"].n
                for ot in range(NCH):
                    bank = ot % 2
                    for cb in range(4):
                        top(lambda e, cb=cb, ot=ot, bank=bank: e.matmul(
                            PB[bank][:, :],
                            WSLOT2[:, 1024 * cb + 128 * ot:
                                   1024 * cb + 128 * (ot + 1)],
                            CTXc(cb), start=(cb == 0), stop=(cb == 3)),
                            waits=(bw(bank) + [(S["d2"], g_wo),
                                               (S["v"], v_ctx)])
                            if cb == 0 else [])
                    tn = S["t"].n
                    vr = vop(lambda e, ot=ot, wlo=wlo, bank=bank:
                             e.tensor_add(
                                 Xs(ot, wlo, 512), Xs(ot, wlo, 512),
                                 PB[bank][:, :]),
                             waits=[(S["t"], tn), (S["v"], S["v"].n)])
                    bank_guard[bank] = (S["v"], vr)

            prepA(0)
            prepB(0)
            prepA(1)
            for w in range(NTB):
                attn_win(w)
                if w + 1 < NTB:
                    prepB(w + 1)
                wo_win(w)
                if w + 2 < NTB:
                    prepA(w + 2)
            v_attn = S["v"].n
            if l == 0:
                pending = layer_loads(1)
                pending = (pending[0], pending[1], pending[2], pending[3])

            v_zs = vop(lambda e: e.tensor_copy(
                ZSAVE[:, :, :],
                bass.AP(X, 512, [[NCH * TG, 128], [TG, NCH], [512, 3]])),
                waits=[(S["v"], v_attn)])
            h2_guard = {0: 0, 1: 0}
            zw_guard = [0, 0]

            def H2(pp, c):
                if pp == 0:
                    return AR2[:, 512 * c:512 * (c + 1)]
                return B[:, 11264 + 512 * c:11264 + 512 * (c + 1)]

            def mlp_prep(tb):
                pp = tb % 2
                lo = 1 + 512 * tb
                v_zw = 0
                for c in range(NCH):
                    v_zw = vop(lambda e, c=c, lo=lo: e.tensor_copy(
                        AR1[:, 513 * c + 1:513 * c + 513], Xs(c, lo, 512)),
                        waits=[(S["v"], v_zs if (tb == 0 and c == 0)
                                else S["v"].n),
                               (S["t"], zw_guard[0]),
                               (S["s"], zw_guard[1])] if c == 0 else
                        [(S["v"], S["v"].n)])
                    if tb == 0:
                        v_zw = vop(lambda e, c=c: e.memset(
                            AR1[:, 513 * c:513 * c + 1], 0.0),
                            waits=[(S["v"], v_zw)])
                    else:
                        v_zw = vop(lambda e, c=c, tb=tb: e.tensor_copy(
                            AR1[:, 513 * c:513 * c + 1],
                            ZSAVE[:, c, tb - 1:tb]),
                            waits=[(S["v"], v_zw)])
                v_bc = emit_stats_block(
                    lambda c: AR1[:, 513 * c + (1 if c < 4 else 0):
                                  513 * c + (1 if c < 4 else 0) + 512],
                    1e-5, [(S["v"], v_zw)])
                zw_guard[0] = S["t"].n
                zw_guard[1] = S["s"].n
                hg = h2_guard[pp]
                v_h2c = [0] * NCH
                for c in range(NCH):
                    if c < 4:
                        v1 = vop(lambda e, c=c, lo=lo: e.tensor_sub(
                            TMP32[:, :], Xs(c, lo, 512), BC2[:, :]),
                            waits=[(S["v"], v_bc if c == 0 else S["v"].n)])
                    elif tb == 0:
                        v1 = vop(lambda e, c=c, lo=lo: e.tensor_sub(
                            TMP32[:, :], Xs(c, lo - 1, 512), BC2[:, :]),
                            waits=[(S["v"], v_bc if c == 0 else S["v"].n)])
                    else:
                        vop(lambda e, c=c, lo=lo: e.tensor_sub(
                            TMP32[:, 1:512], Xs(c, lo, 511), BC2[:, 1:512]),
                            waits=[(S["v"], S["v"].n)])
                        v1 = vop(lambda e, c=c, tb=tb: e.tensor_sub(
                            TMP32[:, 0:1], ZSAVE[:, c, tb - 1:tb],
                            BC2[:, 0:1]),
                            waits=[(S["v"], S["v"].n)])
                    v2 = vop(lambda e, c=c: e.scalar_tensor_tensor(
                        TMP32[:, :], TMP32[:, :], LGC[:, c:c + 1], BC1[:, :],
                        ALU.mult, ALU.mult),
                        waits=[(S["v"], v1), (S["d3"], d_cols)])
                    v_h2c[c] = vop(lambda e, c=c, pp=pp: e.tensor_scalar_add(
                        H2(pp, c), TMP32[:, :], LBC[:, c:c + 1]),
                        waits=[(S["v"], v2), (S["t"], hg)])
                return v_h2c

            vh2 = {0: mlp_prep(0)}
            for tb in range(NTB):
                lo = 1 + 512 * tb
                pp = tb % 2
                v_h2c = vh2[tb]
                v_h2 = v_h2c[NCH - 1]

                tg_last = [0, 0]
                last_va = 0
                for pr in range(NPAIR):
                    par = pr % 2
                    off, wid = A_OFFS[pr], A_WIDS[pr]
                    sA = S["w0a"] if par == 0 else S["w1a"]
                    sB = S["w0b"] if par == 0 else S["w1b"]
                    wld = [(S["t"], tg_last[par])] if pr >= 2 else \
                        [(S["g1"], g_wcache), (S["t"], h2_guard[pp])]
                    dna = dma("sync", sA,
                              bass.AP(KD, 2048 * par,
                                      [[2 * T, 128], [256, NCH], [1, wid]]),
                              bass.AP(w1C.tensor,
                                      (((l * NPAIR + pr) * 2 + 0) * 128 * NCH * 128),
                                      [[NCH * 128, 128], [128, NCH], [1, wid]]),
                              waits=wld)
                    dnb = dma("sync", sB,
                              bass.AP(KD, 2048 * par + 128,
                                      [[2 * T, 128], [256, NCH], [1, wid]]),
                              bass.AP(w1C.tensor,
                                      (((l * NPAIR + pr) * 2 + 1) * 128 * NCH * 128),
                                      [[NCH * 128, 128], [128, NCH], [1, wid]]))
                    ba, bg = 2 + 2 * par, 3 + 2 * par
                    tg_l = 0
                    for c in range(NCH):
                        top(lambda e, c=c, par=par, wid=wid, ba=ba,
                            pp=pp: e.matmul(
                            PB[ba][0:wid, :],
                            bass.AP(KD, 2048 * par + 256 * c,
                                    [[2 * T, 128], [1, wid]]),
                            H2(pp, c),
                            start=(c == 0), stop=(c == NCH - 1)),
                            waits=(bw(ba) + [(sA, dna), (sB, dnb)]
                                   if c == 0 else []) +
                            ([(S["v"], v_h2c[c])] if pr < 2 else []))
                        tg_l = top(lambda e, c=c, par=par, wid=wid, bg=bg,
                                   pp=pp: e.matmul(
                                       PB[bg][0:wid, :],
                                       bass.AP(KD, 2048 * par + 256 * c + 128,
                                               [[2 * T, 128], [1, wid]]),
                                       H2(pp, c),
                                       start=(c == 0), stop=(c == NCH - 1)),
                                   waits=(bw(bg)) if c == 0 else [])
                    tg_last[par] = tg_l
                    sg = sop(lambda e, pr=pr, wid=wid, bg=bg: e.activation(
                        TMPB[0:wid, 0:512], PB[bg][0:wid, :], AF.Gelu,
                        bias=B1C[0:wid, 22 + pr:23 + pr]),
                        waits=[(S["t"], tg_l), (S["s"], S["s"].n),
                               (S["v"], last_va)])
                    bank_guard[bg] = (S["s"], sg)
                    va = vop(lambda e, pr=pr, wid=wid, ba=ba:
                             e.scalar_tensor_tensor(
                                 B[0:wid, 512 * pr:512 * (pr + 1)],
                                 PB[ba][0:wid, :], B1C[0:wid, pr:pr + 1],
                                 TMPB[0:wid, 0:512], ALU.add, ALU.mult),
                             waits=[(S["s"], sg), (S["v"], S["v"].n)])
                    bank_guard[ba] = (S["v"], va)
                    last_va = va
                v_actn = S["v"].n
                h2_guard[pp] = S["t"].n
                if tb + 1 < NTB:
                    vh2[tb + 1] = mlp_prep(tb + 1)

                for half in range(2):
                    ots = list(range(4 * half, 4 * half + 4))
                    tw_last = [0, 0]
                    for pr in range(NPAIR):
                        par = pr % 2
                        off, wid = A_OFFS[pr], A_WIDS[pr]
                        dsem = S["w2a"] if par == 0 else S["w2b"]
                        wld = [(S["t"], tw_last[par])] if pr >= 2 else \
                            [(S["t"], pt_guard[0])]
                        d_w2 = dma("sync", dsem,
                                   bass.AP(PT, 1024 * par,
                                           [[4 * 512, wid], [1, 1024]]),
                                   w2C[l, off:off + wid, :], waits=wld)
                        tw = 0
                        for oi, ot in enumerate(ots):
                            w0 = []
                            if pr == 0:
                                w0 += bw(oi) + [(S["v"], v_actn)]
                            if oi == 0:
                                w0 += [(dsem, d_w2)]
                            tw = top(lambda e, pr=pr, par=par, wid=wid,
                                     ot=ot, oi=oi: e.matmul(
                                         PB[oi][:, :],
                                         bass.AP(PT, 1024 * par + 128 * ot,
                                                 [[4 * 512, wid], [1, 128]]),
                                         B[0:wid, 512 * pr:512 * (pr + 1)],
                                         start=(pr == 0),
                                         stop=(pr == NPAIR - 1)),
                                     waits=w0)
                        tw_last[par] = tw
                    t_done = S["t"].n
                    pt_guard[0] = t_done
                    for oi, ot in enumerate(ots):
                        vr = vop(lambda e, ot=ot, lo=lo, oi=oi:
                                 e.scalar_tensor_tensor(
                                     Xs(ot, lo, 512), PB[oi][:, :],
                                     B2C[:, ot:ot + 1], Xs(ot, lo, 512),
                                     ALU.add, ALU.add),
                                 waits=[(S["t"], t_done),
                                        (S["v"], S["v"].n)])
                        bank_guard[oi] = (S["v"], vr)
            v_z0 = S["v"].n

        v_fin = S["v"].n
        slots = [BC1[:, 0:512], BC2[:, 0:512], TMP32[:, 0:512]]
        ssem = [S["d0"], S["d1"], S["d2"]]
        vres = [0, 0, 0]
        for c in range(NCH):
            for tb in range(NTB):
                i = c * NTB + tb
                k = i % 3
                waits = [(S["v"], v_fin)]
                if i >= 3:
                    waits.append((S["v"], vres[k]))
                dx = dma("sync", ssem[k], slots[k],
                         xpark[c, :, 1 + 512 * tb:1 + 512 * (tb + 1)],
                         waits=waits)
                vres[k] = vop(lambda e, c=c, tb=tb, k=k: e.tensor_add(
                    Xs(c, 1 + 512 * tb, 512), Xs(c, 1 + 512 * tb, 512),
                    slots[k]),
                    waits=[(ssem[k], dx), (S["v"], S["v"].n)])

    def phase_attn_b():
        barrier()
        gw0 = dma("gpsimd", S["g0"], WSLOT2[0:64, 0:512], bDQT[:, :])
        dma("gpsimd", S["g0"], WSLOT2[64:128, 0:512], bDQT[:, :])
        dma("gpsimd", S["g0"], WSLOT2[0:64, 512:1536], bDKVT[:, :])
        dma("gpsimd", S["g0"], WSLOT2[64:128, 512:1536], bDKVT[:, :])
        dma("gpsimd", S["g0"],
            WSLOT2[:, 1536:1792].rearrange("p (c o) -> p c o", c=4),
            bUQT[:, :].rearrange("(c p) o -> p c o", p=128))
        dma("gpsimd", S["g0"],
            WSLOT2[:, 2048:3072].rearrange("p (c o) -> p c o", c=8),
            bUKVT[:, :].rearrange("(c p) o -> p c o", p=128))
        dma("gpsimd", S["g0"], WSLOT2[0:64, 3584:3648], bQRT[:, :])
        dma("gpsimd", S["g0"], WSLOT2[0:64, 3648:3712], bKRT[:, :])
        dma("gpsimd", S["g0"], WSLOT2[64:128, 3648:3712], bKRT[:, :])
        dma("gpsimd", S["g0"], WSLOT2[0:64, 3712:3776], bWOT[:, :])
        gw = dma("gpsimd", S["g0"], WSLOT2[64:128, 3712:3776], bWOT[:, :])

        v_mat, g_mat = entry_ln(lambda c, lo: Bs(c, lo, 512), gp=True)

        def Qh(h01, lo2, n, p0=0, pn=128):
            return AR1[p0:p0 + pn, 2048 * h01 + lo2:2048 * h01 + lo2 + n]

        def CQb(par, h01, p0=0, pn=128):
            off = 4096 + 1024 * par + 512 * h01
            return AR1[p0:p0 + pn, off:off + 512]

        def CKVb(par, h01, p0=0, pn=128):
            off = 6144 + 1024 * par + 512 * h01
            return AR1[p0:p0 + pn, off:off + 512]

        dqr_g = {0: 0, 1: 0}
        dkr_g = {0: 0, 1: 0}
        dv_g = [0]
        tre_g = {0: 0, 1: 0}

        def rms2_rows(t_ms_a, t_ms_b):
            v1 = vop(lambda e: e.tensor_copy(ROWS[:, 0:512], PBX[0:1, :]),
                     waits=[(S["t"], t_ms_a), (S["v"], S["v"].n)])
            v1a = vop(lambda e: e.tensor_copy(ROWS[:, 512:1024],
                                              PBX[64:65, :]),
                      waits=[(S["t"], t_ms_b), (S["v"], v1)])
            pbx_guard[0] = (S["v"], v1a)
            s1 = sop(lambda e: e.activation(
                ROWS[:, 0:1024], ROWS[:, 0:1024], AF.Ln,
                bias=EPSC[0:1, 1:2], scale=1.0 / 64.0),
                waits=[(S["v"], v1a), (S["s"], S["s"].n)])
            s1b = sop(lambda e: e.activation(
                RECB[:, :], ROWS[:, 0:1024], AF.Exp, scale=-0.5),
                waits=[(S["s"], s1)])
            return s1b

        def rms2_bcast(s1b):
            tb1 = top(lambda e: e.matmul(
                PBX[0:64, :], ONESCB[0:1, 0:64], RECB[:, 0:512],
                start=True, stop=True),
                waits=pbxw() + [(S["s"], s1b)])
            tb2 = top(lambda e: e.matmul(
                PBX[64:128, :], ONESCB[0:1, 64:128], RECB[:, 512:1024],
                start=True, stop=True))
            s3 = sop(lambda e: e.copy(TMP32[:, :], PBX[:, :]),
                     waits=[(S["t"], tb2), (S["s"], S["s"].n),
                            (S["v"], S["v"].n)])
            pbx_guard[0] = (S["s"], s3)
            return s3

        for hp in range(NCH):
            for tb in range(NTB):
                lo = 1 + 512 * tb

                def emit_cq(lt, hp=hp, lo=lo):
                    par = lt % 2
                    bA, bB = 2 * par, 2 * par + 1
                    w0 = bw(bA) + bw(bB)
                    if lt == 0:
                        w0 += [(S["v"], v_mat[tb]) if hp < 5 else
                               (S["gc"], g_mat[tb])]
                    if hp == 0 and tb == 0 and lt == 0:
                        w0 += [(S["g0"], gw)]
                    top(lambda e, hp=hp, lo=lo, lt=lt, bA=bA: e.matmul(
                        PB[bA][:, :], WSLOT2[0:64, 128 * lt:128 * (lt + 1)],
                        Bs(hp, lo, 512, 0, 64), start=True, stop=True),
                        waits=w0)
                    return top(lambda e, hp=hp, lo=lo, lt=lt, bB=bB:
                               e.matmul(
                                   PB[bB][:, :],
                                   WSLOT2[64:128, 128 * lt:128 * (lt + 1)],
                                   Bs(hp, lo, 512, 64, 64),
                                   start=True, stop=True))

                tcq = [0] * 4
                qc_done = [0, 0]
                tcq[0] = emit_cq(0)
                for lt in range(4):
                    par = lt % 2
                    bA, bB = 2 * par, 2 * par + 1
                    if lt + 1 < 4:
                        tcq[lt + 1] = emit_cq(lt + 1)
                    vA = vop(lambda e, par=par, bA=bA: e.tensor_copy(
                        CQb(par, 0), PB[bA][:, :]),
                        waits=[(S["t"], tcq[lt]), (S["v"], S["v"].n),
                               (S["t"], qc_done[par])])
                    sB = sop(lambda e, par=par, bB=bB: e.copy(
                        CQb(par, 1), PB[bB][:, :]),
                        waits=[(S["t"], tcq[lt]), (S["s"], S["s"].n),
                               (S["t"], qc_done[par])])
                    bank_guard[bA] = (S["v"], vA)
                    bank_guard[bB] = (S["s"], sB)
                    top(lambda e, lt=lt: e.matmul(
                        PVa[0:64, :],
                        WSLOT2[:, 1536 + 64 * lt:1600 + 64 * lt],
                        CQb(lt % 2, 0), start=(lt == 0), stop=(lt == 3)),
                        waits=[(S["v"], vA)] + (bw(4) if lt == 0 else []))
                    tqc = top(lambda e, lt=lt: e.matmul(
                        PVb[0:64, :],
                        WSLOT2[:, 1536 + 64 * lt:1600 + 64 * lt],
                        CQb(lt % 2, 1), start=(lt == 0), stop=(lt == 3)),
                        waits=[(S["s"], sB)] + (bw(5) if lt == 0 else []))
                    qc_done[par] = tqc

                sqA = sop(lambda e: e.activation(TMPA[0:64, 0:512],
                                                 PVa[0:64, :], AF.Square),
                          waits=[(S["t"], tqc), (S["s"], S["s"].n)])
                sqB = sop(lambda e: e.activation(TMPA[64:128, 0:512],
                                                 PVb[0:64, :], AF.Square),
                          waits=[(S["s"], sqA)])
                tmsA = top(lambda e: e.matmul(
                    PBX[0:1, :], HALF2[0:64, 0:1], TMPA[0:64, 0:512],
                    start=True, stop=True),
                    waits=[(S["s"], sqA)] + pbxw())
                tmsB = top(lambda e: e.matmul(
                    PBX[64:65, :], HALF2[64:128, 1:2], TMPA[64:128, 0:512],
                    start=True, stop=True),
                    waits=[(S["s"], sqB)])
                s1b = rms2_rows(tmsA, tmsB)
                tkv_started = []

                def emit_ckv(lt, hp=hp, lo=lo):
                    par = lt % 2
                    bA, bB = 2 * par, 2 * par + 1
                    top(lambda e, hp=hp, lo=lo, lt=lt, bA=bA: e.matmul(
                        PB[bA][:, :],
                        WSLOT2[0:64, 512 + 128 * lt:512 + 128 * (lt + 1)],
                        Bs(hp, lo, 512, 0, 64), start=True, stop=True),
                        waits=bw(bA) + bw(bB))
                    return top(lambda e, hp=hp, lo=lo, lt=lt, bB=bB:
                               e.matmul(
                                   PB[bB][:, :],
                                   WSLOT2[64:128,
                                          512 + 128 * lt:512 + 128 * (lt + 1)],
                                   Bs(hp, lo, 512, 64, 64),
                                   start=True, stop=True))

                tckv = [0] * 8
                tckv[0] = emit_ckv(0)
                tckv[1] = emit_ckv(1)
                s3 = rms2_bcast(s1b)
                vqA = vop(lambda e, lo=lo: e.scalar_tensor_tensor(
                    Qh(0, lo - 1, 512, 0, 64), PVa[0:64, :],
                    BIASC[0:64, 17:18], TMP32[0:64, :],
                    ALU.mult, ALU.mult),
                    waits=[(S["s"], s3), (S["v"], S["v"].n)])
                vqB = vop(lambda e, lo=lo: e.scalar_tensor_tensor(
                    Qh(1, lo - 1, 512, 0, 64), PVb[0:64, :],
                    BIASC[64:128, 17:18], TMP32[64:128, :],
                    ALU.mult, ALU.mult),
                    waits=[(S["v"], vqA)])
                bank_guard[4] = (S["v"], vqB)
                bank_guard[5] = (S["v"], vqB)
                for h01 in range(2):
                    bank = 4 + h01
                    tr = top(lambda e, h01=h01, lo=lo, bank=bank: e.matmul(
                        PB[bank][0:64, :], WSLOT2[0:64, 3584:3648],
                        Qh(h01, lo - 1, 512, 0, 64), start=True, stop=True),
                        waits=bw(bank) + [(S["v"], vqB)])
                    qsem = S["d0"] if h01 == 0 else S["dq1"]
                    ksem = S["d2"] if h01 == 0 else S["dk1"]
                    vr = vop(lambda e, h01=h01, bank=bank: e.tensor_copy(
                        TMPC[64 * h01:64 * h01 + 64, 0:512],
                        PB[bank][0:64, :]),
                        waits=[(S["t"], tr), (S["v"], S["v"].n),
                               (qsem, dqr_g[h01]),
                               (ksem, dkr_g[h01])])
                    bank_guard[bank] = (S["v"], vr)
                    dqr_g[h01] = dma("sync", qsem,
                                     Qh(h01, lo - 1, 512, 64, 64),
                                     TMPC[64 * h01:64 * h01 + 64, 0:512],
                                     waits=[(S["v"], vr)])

                tkv = 0
                kv_done = [0, 0]
                for lt in range(NCH):
                    par = lt % 2
                    bA, bB = 2 * par, 2 * par + 1
                    vA = vop(lambda e, par=par, bA=bA: e.tensor_copy(
                        CKVb(par, 0), PB[bA][:, :]),
                        waits=[(S["t"], tckv[lt]), (S["v"], S["v"].n),
                               (S["t"], kv_done[par])])
                    sB = sop(lambda e, par=par, bB=bB: e.copy(
                        CKVb(par, 1), PB[bB][:, :]),
                        waits=[(S["t"], tckv[lt]), (S["s"], S["s"].n),
                               (S["t"], kv_done[par])])
                    bank_guard[bA] = (S["v"], vA)
                    bank_guard[bB] = (S["s"], sB)
                    if lt + 2 < NCH:
                        tckv[lt + 2] = emit_ckv(lt + 2)
                    top(lambda e, lt=lt: e.matmul(
                        PVa[:, :],
                        WSLOT2[:, 2048 + 128 * lt:2176 + 128 * lt],
                        CKVb(lt % 2, 0), start=(lt == 0), stop=(lt == 7)),
                        waits=[(S["v"], vA)] + (bw(4) if lt == 0 else []))
                    tkv = top(lambda e, lt=lt: e.matmul(
                        PVb[:, :],
                        WSLOT2[:, 2048 + 128 * lt:2176 + 128 * lt],
                        CKVb(lt % 2, 1), start=(lt == 0), stop=(lt == 7)),
                        waits=[(S["s"], sB)] + (bw(5) if lt == 0 else []))
                    kv_done[par] = tkv

                sqA = sop(lambda e: e.activation(TMPA[0:64, 0:512],
                                                 PVa[0:64, :], AF.Square),
                          waits=[(S["t"], tkv), (S["s"], S["s"].n)])
                sqB = sop(lambda e: e.activation(TMPA[64:128, 0:512],
                                                 PVb[0:64, :], AF.Square),
                          waits=[(S["s"], sqA)])
                tmsA = top(lambda e: e.matmul(
                    PBX[0:1, :], HALF2[0:64, 0:1], TMPA[0:64, 0:512],
                    start=True, stop=True),
                    waits=[(S["s"], sqA)] + pbxw())
                tmsB = top(lambda e: e.matmul(
                    PBX[64:65, :], HALF2[64:128, 1:2], TMPA[64:128, 0:512],
                    start=True, stop=True),
                    waits=[(S["s"], sqB)])
                s1b = rms2_rows(tmsA, tmsB)
                s3 = rms2_bcast(s1b)
                for h01 in range(2):
                    pv = PVa if h01 == 0 else PVb
                    vk = vop(lambda e, h01=h01, tb=tb, pv=pv:
                             e.scalar_tensor_tensor(
                                 KD[0:64, h01, 512 * tb:512 * (tb + 1)],
                                 pv[0:64, :],
                                 BIASC[64 * h01:64 * h01 + 64, 18:19],
                                 TMP32[64 * h01:64 * h01 + 64, :],
                                 ALU.mult, ALU.mult),
                             waits=[(S["s"], s3), (S["v"], S["v"].n)])
                    vv = vop(lambda e, pv=pv: e.tensor_copy(
                        TMPB[64:128, 0:512], pv[64:128, :]),
                        waits=[(S["v"], vk), (S["d1"], dv_g[0])])
                    bank_guard[4 + h01] = (S["v"], vv)
                    dv_g[0] = dma("sync", S["d1"],
                                  TMPE[64 * h01:64 * h01 + 64, 0:512],
                                  TMPB[64:128, 0:512],
                                  waits=[(S["v"], vv), (S["t"], tre_g[h01])])
                    for q in range(4):
                        tt = 4 * tb + q
                        tq = top(lambda e, q=q, h01=h01: e.transpose(
                            PBF[:, 0:64],
                            TMPE[64 * h01:64 * h01 + 64,
                                 128 * q:128 * (q + 1)],
                            IDENTB[0:64, 0:64] if h01 == 0 else
                            IDENTB[64:128, 64:128]),
                            waits=[(S["d1"], dv_g[0])] + pbfw())
                        vq2 = vop(lambda e, tt=tt, h01=h01: e.tensor_copy(
                            V65[:, 1040 * h01 + 65 * tt:
                                1040 * h01 + 65 * tt + 64], PBF[:, 0:64]),
                            waits=[(S["t"], tq), (S["v"], S["v"].n)])
                        pbf_guard[0] = (S["v"], vq2)
                    tre_g[h01] = S["t"].n
                for h01 in range(2):
                    bank = h01
                    tr = top(lambda e, hp=hp, lo=lo, h01=h01, bank=bank:
                             e.matmul(
                                 PB[bank][0:64, :],
                                 WSLOT2[64 * h01:64 * h01 + 64, 3648:3712],
                                 Bs(hp, lo, 512, 64 * h01, 64),
                                 start=True, stop=True),
                             waits=bw(bank))
                    qsem = S["d0"] if h01 == 0 else S["dq1"]
                    ksem = S["d2"] if h01 == 0 else S["dk1"]
                    vr = vop(lambda e, bank=bank, h01=h01: e.tensor_copy(
                        TMPC[64 * h01:64 * h01 + 64, 0:512],
                        PB[bank][0:64, :]),
                        waits=[(S["t"], tr), (S["v"], S["v"].n),
                               (ksem, dkr_g[h01]),
                               (qsem, dqr_g[h01])])
                    bank_guard[bank] = (S["v"], vr)
                    dkr_g[h01] = dma("sync", ksem,
                                     KD[64:128, h01, 512 * tb:512 * (tb + 1)],
                                     TMPC[64 * h01:64 * h01 + 64, 0:512],
                                     waits=[(S["v"], vr)])
            d_proj = [(S["d0"], S["d0"].n), (S["d1"], S["d1"].n),
                      (S["d2"], S["d2"].n), (S["dq1"], S["dq1"].n),
                      (S["dk1"], S["dk1"].n)]
            v65o = vop(lambda e: e.memset(
                bass.AP(V65, 64, [[4160, 128], [1040, 2], [65, NT], [1, 1]]),
                1.0), waits=[(S["v"], S["v"].n)])

            prev_ctxb = [None]
            for tb in range(NTB):
                for h01 in range(2):
                    njt = 4 * tb + 4
                    npair = njt // 2
                    sfree = {}
                    tpvp = {}
                    tS = [0] * npair

                    def emit_S(jp, h01=h01, tb=tb):
                        sl = jp % 2
                        j0, j1 = 2 * jp, 2 * jp + 1
                        w0 = []
                        if jp < 2:
                            w0 += bw(2 * sl) + bw(2 * sl + 1)
                        else:
                            w0 += [(S["s"], sfree[sl])]
                        if jp == 0:
                            w0 += [(S["v"], v65o)] + d_proj
                        top(lambda e, j=j0, h01=h01, tb=tb, sl=sl:
                            e.matmul(
                                SP[:, 1024 * sl:1024 * sl + 512],
                                KD[:, h01, 128 * j:128 * (j + 1)],
                                Qh(h01, 512 * tb, 512),
                                start=True, stop=True),
                            waits=w0)
                        tS[jp] = top(lambda e, j=j1, h01=h01, tb=tb, sl=sl:
                                     e.matmul(
                                         SP[:, 1024 * sl + 512:
                                            1024 * sl + 1024],
                                         KD[:, h01, 128 * j:128 * (j + 1)],
                                         Qh(h01, 512 * tb, 512),
                                         start=True, stop=True))

                    emit_S(0)
                    if npair > 1:
                        emit_S(1)
                    if prev_ctxb[0] is not None:
                        prev_ctxb[0]()
                    for jp in range(npair):
                        sl = jp % 2
                        j0, j1 = 2 * jp, 2 * jp + 1
                        wE = [(S["t"], tS[jp]), (S["s"], S["s"].n)]
                        if jp >= 2:
                            wE.append((S["t"], tpvp[sl]))
                        se = sop(lambda e, sl=sl: e.activation(
                            PT[:, 2 * sl:2 * sl + 2, :],
                            SP[:, 1024 * sl:1024 * sl + 1024], AF.Exp,
                            scale=0.125), waits=wE)
                        sfree[sl] = se
                        bank_guard[2 * sl] = (S["s"], se)
                        bank_guard[2 * sl + 1] = (S["s"], se)
                        if jp + 2 < npair:
                            emit_S(jp + 2)
                        if j0 >= 4 * tb:
                            moff = 4 + j0 - 4 * tb
                            vm = vop(lambda e, sl=sl, moff=moff: e.tensor_mul(
                                PT[:, 2 * sl:2 * sl + 2, :],
                                PT[:, 2 * sl:2 * sl + 2, :],
                                MASKS[:, moff:moff + 2, :]),
                                waits=[(S["s"], se), (S["v"], S["v"].n)])
                            wPV = [(S["v"], vm)]
                        else:
                            wPV = [(S["s"], se)]
                        top(lambda e, j=j0, h01=h01, sl=sl, jp=jp:
                            e.matmul(
                                PVa[0:65, :],
                                V65[:, 1040 * h01 + 65 * j:
                                    1040 * h01 + 65 * j + 65],
                                PT[:, 2 * sl, :], start=(jp == 0),
                                stop=False),
                            waits=wPV + (bw(4) if jp == 0 else []))
                        tpvp[sl] = top(lambda e, j=j1, h01=h01, sl=sl,
                                       jp=jp, npair=npair: e.matmul(
                                           PVa[0:65, :],
                                           V65[:, 1040 * h01 + 65 * j:
                                               1040 * h01 + 65 * j + 65],
                                           PT[:, 2 * sl + 1, :], start=False,
                                           stop=(jp == npair - 1)))
                    t_pv = S["t"].n

                    def mk_ctxb(h01=h01, t_pv=t_pv):
                        def go():
                            ctx_norm(4, AR2[64 * h01:64 * h01 + 64, 0:512],
                                     t_pv)
                        return go
                    prev_ctxb[0] = mk_ctxb()
                if prev_ctxb[0] is not None:
                    prev_ctxb[0]()
                    prev_ctxb[0] = None
                vboth = S["v"].n
                top(lambda e: e.matmul(
                    PB[5][0:64, :], WSLOT2[0:64, 3712:3776],
                    AR2[0:64, 0:512], start=True, stop=True),
                    waits=bw(5) + [(S["v"], vboth)])
                to2 = top(lambda e: e.matmul(
                    PB[5][64:128, :], WSLOT2[64:128, 3712:3776],
                    AR2[64:128, 0:512], start=True, stop=True))
                vr1 = vop(lambda e, hp=hp, tb=tb: e.tensor_add(
                    Xs(hp, 1 + 512 * tb, 512), Xs(hp, 1 + 512 * tb, 512),
                    PB[5][:, :]),
                    waits=[(S["t"], to2), (S["v"], S["v"].n)])
                bank_guard[5] = (S["v"], vr1)

    def phase_out():
        barrier()
        v_fin = S["v"].n
        dstore = {}
        ssems = [S["sp"], S["spb"], S["dq1"], S["dk1"]]
        slots = [BC1[:, 0:128], BC1[:, 128:256], BC2[:, 0:128],
                 BC2[:, 128:256]]
        for c in range(NCH):
            for tt in range(NT):
                i = c * NT + tt
                k = i % 4
                tw = [(S["v"], v_fin)] + (bw(i % 4) if i < 4 else [])
                if i >= 4:
                    tw.append((ssems[k], dstore[i - 4]))
                tn = top(lambda e, c=c, tt=tt, pb=PB[i % 4]: e.transpose(
                    pb[:, 0:128], Xs(c, 1 + 128 * tt, 128), IDENT[:, :]),
                    waits=tw)
                vv = vop(lambda e, pb=PB[i % 4], k=k: e.tensor_copy(
                    slots[k], pb[:, 0:128]),
                    waits=[(S["t"], tn), (S["v"], S["v"].n)])
                dstore[i] = dma("scalar", ssems[k],
                                out[128 * tt:128 * (tt + 1),
                                    128 * c:128 * (c + 1)],
                                slots[k],
                                waits=[(S["v"], vv)])
        p.do("scalar", None, waits=[(s, s.n) for s in ssems])

    if 1 in phases:
        phase_attn_a()
    if 2 in phases:
        phase_ffn(0)
    if 3 in phases:
        phase_attn_b()
    if 4 in phases:
        phase_ffn(1)
    phase_out()

    block = es.enter_context(nc.Block())
    with es:
        p.realize(block)
    return nc


def _prep_inputs(inputs):
    f32 = np.float32

    def c(a):
        return np.ascontiguousarray(np.asarray(a, dtype=f32))

    shared = {
        "aWqT": c(np.asarray(inputs["L0_attn_a_Wq"]).T),
        "aWkT": c(np.asarray(inputs["L0_attn_a_Wk"]).T),
        "aWvT": c(np.asarray(inputs["L0_attn_a_Wv"]).T),
        "aWoT": c(np.asarray(inputs["L0_attn_a_Wo"]).T),
        "abq_c": c(np.asarray(inputs["L0_attn_a_bq"]).reshape(8, 128).T),
        "abo_c": c(np.asarray(inputs["L0_attn_a_bo"]).reshape(8, 128).T),
        "bDQT": c(np.asarray(inputs["L0_attn_b_W_DQ"]).T),
        "bUQT": c(np.asarray(inputs["L0_attn_b_W_UQ"]).T),
        "bDKVT": c(np.asarray(inputs["L0_attn_b_W_DKV"]).T),
        "bUKVT": c(np.asarray(inputs["L0_attn_b_W_UKV"]).T),
        "bQRT": c(np.asarray(inputs["L0_attn_b_W_QR"]).T),
        "bKRT": c(np.asarray(inputs["L0_attn_b_W_KR"]).T),
        "bWOT": c(np.asarray(inputs["L0_attn_b_W_O"]).T),
        "fwqkv": c(inputs["L0_ffn_wqkv"]),
        "fqs": c(inputs["L0_ffn_q_scale"]),
        "fks": c(inputs["L0_ffn_k_scale"]),
        "fwo": c(inputs["L0_ffn_wo"]),
        "flng_c": c(np.asarray(inputs["L0_ffn_ln_g"]).reshape(2, 8, 128)
                    .transpose(0, 2, 1)),
        "flnb_c": c(np.asarray(inputs["L0_ffn_ln_b"]).reshape(2, 8, 128)
                    .transpose(0, 2, 1)),
        "fw1": c(inputs["L0_ffn_w1"]),
        "fw2": c(inputs["L0_ffn_w2"]),
        "fb2_c": c(np.asarray(inputs["L0_ffn_b2"]).reshape(2, 8, 128)
                   .transpose(0, 2, 1)),
        "ident_in": np.eye(128, dtype=f32),
        "rotc": _host_rotary_tables(),
        "masks_in": _host_masks(),
    }
    bv = np.zeros((128, 1), f32)
    bv[0:64, 0] = np.asarray(inputs["L0_attn_a_bv"], dtype=f32)
    shared["abv_c"] = bv
    qn = np.zeros((128, 1), f32)
    qn[0:64, 0] = np.asarray(inputs["L0_attn_b_q_norm_w"], dtype=f32)
    qn[64:128, 0] = qn[0:64, 0]
    shared["bqnw_c"] = qn
    kn = np.zeros((128, 1), f32)
    kn[0:64, 0] = np.asarray(inputs["L0_attn_b_k_norm_w"], dtype=f32)
    kn[64:128, 0] = kn[0:64, 0]
    shared["bknw_c"] = kn
    b1 = np.asarray(inputs["L0_ffn_b1"], dtype=f32)
    b1c = np.zeros((2, 128, 44), f32)
    for pr in range(NPAIR):
        off, wid = A_OFFS[pr], A_WIDS[pr]
        b1c[:, 0:wid, pr] = b1[:, off:off + wid]
        b1c[:, 0:wid, 22 + pr] = b1[:, 2730 + off:2730 + off + wid]
    shared["fb1_c"] = b1c
    return shared


def kernel(**inputs):
    shared = _prep_inputs(inputs)
    x = np.ascontiguousarray(np.asarray(inputs["L0_attn_a_hidden_states"],
                                        dtype=np.float32))
    nc = build()
    in_maps = [dict(shared, x=x[i]) for i in range(8)]
    res = run_bass_kernel_spmd(nc, in_maps, core_ids=list(range(8)))
    return np.stack([r["out"] for r in res.results], axis=0)
